# version 02
# baseline (speedup 1.0000x reference)
import contextlib
import numpy as np
import ml_dtypes
import concourse.bass as bass
import concourse.mybir as mybir
from concourse.bass_utils import run_bass_kernel_spmd

F32 = mybir.dt.float32
BF16 = mybir.dt.bfloat16
AX = mybir.AxisListType.X
OP = mybir.AluOpType
ACT = mybir.ActivationFunctionType

BATCH, SEQ, D = 2, 4096, 2048
TL = 1024
NG = D // 128
HID = 8192
NHC, HDIM = 4, 128
NE, FMOE = 8, 1024
GRPS = [[0, 1, 2, 3], [4, 5, 6, 7]]
ALL8 = [[0, 1, 2, 3, 4, 5, 6, 7]]
ENGS = ("sync", "tensor", "vector", "scalar", "gpsimd")

OX = 0
G = [16400, 32784, 49168, 65552]
ARENA_COLS = 81936


def ts(i, n=128):
    return slice(i * n, (i + 1) * n)


class Emitter:
    def __init__(self, nc):
        self.nc = nc
        self.q = {e: [] for e in ENGS}
        self.semv = {}
        self.ctx = contextlib.ExitStack()

    def sem(self, name):
        s = self.ctx.enter_context(self.nc.semaphore(name))
        self.semv[s] = 0
        return s

    def sbuf(self, name, shape, dtype):
        return self.ctx.enter_context(self.nc.sbuf_tensor(name, list(shape), dtype))

    def op(self, eng, fn, inc=None, incn=1):
        if inc is not None:
            self.semv[inc] += incn

            def g(e, fn=fn, inc=inc, incn=incn):
                fn(e).then_inc(inc, incn)
            self.q[eng].append(g)
        else:
            self.q[eng].append(fn)

    def dma(self, eng, out, in_, inc, **kw):
        self.op(eng, lambda e, o=out, i=in_, kw=kw: e.dma_start(out=o, in_=i, **kw),
                inc=inc, incn=16)

    def tiny(self, eng, fn, inc=None, incn=1):
        self.op(eng, fn, inc=inc, incn=incn)
        self.q[eng].append(lambda e: e.drain())

    def wait(self, eng, sem, val=None):
        v = self.semv[sem] if val is None else val
        self.q[eng].append(lambda e, s=sem, v=v: e.wait_ge(s, v))

    def val(self, sem):
        return self.semv[sem]

    def flush(self):
        blk = self.nc.Block()
        with blk:
            for name in ENGS:
                if self.q[name]:
                    fs = self.q[name]

                    def run(eng, fs=fs):
                        for f in fs:
                            f(eng)
                    getattr(blk, name)(run)
        self.q = {e: [] for e in ENGS}


def build_program(upto=99, debug=False):
    nc = bass.Bass("TRN2", target_bir_lowering=False)

    def din(name, shape, dtype=F32):
        return nc.dram_tensor(name, list(shape), dtype, kind="ExternalInput")

    def dint(name, shape, dtype):
        return nc.dram_tensor(name, list(shape), dtype, kind="Internal")

    x_in = din("x_f", [D, TL + 1])
    wk_in = din("l0_wk", [NG, D, 128])
    wv_in = din("l0_wv", [NG, D, 128])
    wr_in = din("l0_wr", [NG, D, 128])
    wo_in = din("l0_wo", [NG, D, 128])
    decay_in = din("l0_decay", [D, 1])
    first_in = din("l0_first", [D, 1])
    mixk_in = din("l0_mixk", [D, 1])
    mixv_in = din("l0_mixv", [D, 1])
    mixr_in = din("l0_mixr", [D, 1])
    cmult_in = din("carry_mult", [1, 4])
    wfc_in = din("wfc_slice", [D, 1024])
    wproj_in = din("wproj_slice", [1024, D])
    wq_in = din("l1_wq", [D, 512])
    wk1_in = din("l1_wk", [D, 512])
    wv1_in = din("l1_wv", [D, 512])
    wo1_in = din("l1_wo", [512, D])
    router_in = din("router_w", [D, NE])
    gate_in = din("moe_gate", [NE, 8, D, 128])
    up_in = din("moe_up", [NE, 8, D, 128])
    down_in = din("moe_down", [NE, FMOE, D])
    identb_in = din("ident_bf", [128, 128], BF16)
    maskst_in = din("mask_st", [128, 128], BF16)
    ones_in = din("ones_col", [128, 1])
    onesrow_in = din("ones_row", [1, 128])
    onessq_in = din("ones_sq", [128, 128], BF16)
    ascale_in = din("attn_scale", [128, 1])

    out_d = nc.dram_tensor("out", [TL, D], F32, kind="ExternalOutput")
    dbg = {}
    if debug:
        for name, shape, dt in [
            ("dbg_x1", [TL, D], BF16), ("dbg_x2", [TL, D], BF16),
            ("dbg_x3", [TL, D], BF16), ("dbg_srw", [D, TL], BF16),
            ("dbg_ek", [D, TL], BF16), ("dbg_r1", [D, TL], BF16),
            ("dbg_xhat", [TL, D], BF16), ("dbg_h", [TL, HID], BF16),
            ("dbg_attn", [TL, D], BF16), ("dbg_moe", [TL, D], BF16),
            ("dbg_r1tok", [TL, D], BF16),
        ]:
            dbg[name] = nc.dram_tensor(name, shape, dt, kind="ExternalOutput")

    tot_in_d = dint("wkv_tot_in", [D, 2], F32)
    tot_ag_d = dint("wkv_tot_ag", [4, D, 2], F32)
    gam_in_d = dint("gam_in", [1, 2], F32)
    gam_out_d = dint("gam_out", [1, 2], F32)
    qfc_in_d = dint("qfc_in", [D, 1024], BF16)
    qfc_d = dint("qfc_ag", [8, D, 1024], BF16)
    qproj_in_d = dint("qproj_in", [1024, D], BF16)
    qproj_d = dint("qproj_ag", [8, 1024, D], BF16)
    x2_in_d = dint("x2_in", [D, TL], BF16)
    x2_ag_d = dint("x2_ag", [4, D, TL], BF16)
    rs_in_d = dint("rs_in", [4, TL, D], BF16)
    rs_out_d = dint("rs_out", [TL, D], BF16)
    hhat_d = dint("hhat", [TL, HID], BF16)
    hf_d = dint("hf", [HID, TL], BF16)
    ffn_d = dint("ffn", [D, TL], BF16)
    downb_d = dint("down_bf", [NE, FMOE, D], BF16)
    x1_spill_d = dint("x1_spill", [TL, D], BF16)
    x2_spill_d = dint("x2_spill", [TL, D], BF16)
    x3_spill_d = dint("x3_spill", [TL, D], BF16)
    bc_d = dint("bc_scratch", [1, 16], F32)
    w8_d = dint("w8", [8, 1024], BF16)

    em = Emitter(nc)
    sLD = em.sem("sLD")
    sXD = em.sem("sXD")
    sTE = em.sem("sTE")
    sDV = em.sem("sDV")
    sSC = em.sem("sSC")
    sCC = em.sem("sCC")
    sPF = em.sem("sPF")
    sWA = em.sem("sWA")
    sWB = em.sem("sWB")
    sS0 = em.sem("sS0")
    sS1 = em.sem("sS1")
    sDC = em.sem("sDC")

    AR = em.sbuf("AR", [128, ARENA_COLS], BF16)
    ARFW = 3520
    ARF = em.sbuf("ARF", [128, ARFW], F32)
    SM = em.sbuf("SM", [128, 12288], BF16)
    w8row = em.sbuf("w8row", [1, 1024], BF16)

    identb = SM[:, 0:128]
    maskst = SM[:, 128:256]
    onessq = SM[:, 272:400]
    onescol_b = SM[:, 256:257]
    S0, S1, S2, SSP = 512, 2560, 4608, 6656

    def smv(a, n):
        return SM[:, a:a + n]

    def fv(a, n):
        return ARF[:, a:a + n]
    onescol = fv(3074, 1)
    vec_dec = fv(3076, NG)
    vec_w = fv(3092, NG)
    vec_lam = fv(3108, NG)
    vec_eu = fv(3124, NG)
    vec_mixk = fv(3140, NG)
    vec_mixv = fv(3156, NG)
    vec_mixr = fv(3172, NG)
    cmult = fv(3188, 4)
    scalars = fv(3192, 8)
    negthr = fv(3200, 2)
    stat = fv(3204, 16)
    tot_sb = fv(3220, 2 * NG)
    init_sb = fv(3252, 2 * NG)
    tots_g = fv(3284, 8)
    coef_g = fv(3292, 4)
    attC = fv(3296, 8)
    ascale = fv(3310, 1)
    attK = fv(3312, 8)
    onesrow = ARF[0:1, 3392:3520]
    lamfull = ARF[:, 0:TL]
    sa_full = ARF[:, TL:2 * TL + 1]
    sb_full = ARF[:, 2 * TL + 1:3 * TL + 2]
    wt0 = ARF[:, 0:1024]
    wt1 = ARF[:, 1024:2048]
    outstg = ARF[:, 0:2048]

    def fm(base, g, w=TL):
        return AR[:, base + g * w: base + g * w + w]

    def tmt(base, t):
        return AR[:, base + t * D: base + t * D + D]

    def xg(g, a, b):
        return AR[:, OX + g * (TL + 1) + a: OX + g * (TL + 1) + b]

    pctx = contextlib.ExitStack()
    ps1 = pctx.enter_context(nc.psum_tensor("p0_ps1", [1, 512], F32))
    psB16 = pctx.enter_context(nc.psum_tensor("p0_psB16", [128, 16], F32))

    for buf, src in [(identb, identb_in), (maskst, maskst_in), (onessq, onessq_in)]:
        em.dma("sync", buf, src[:, :], sLD)
    em.dma("sync", ascale, ascale_in[:, :], sLD)
    em.dma("sync", onescol, ones_in[:, :], sLD)
    em.dma("sync", onesrow, onesrow_in[:, :], sLD)
    for g in range(NG):
        em.dma("sync", vec_dec[:, g:g + 1], decay_in[ts(g), :], sLD)
        em.dma("sync", vec_eu[:, g:g + 1], first_in[ts(g), :], sLD)
        em.dma("sync", vec_mixk[:, g:g + 1], mixk_in[ts(g), :], sLD)
        em.dma("sync", vec_mixv[:, g:g + 1], mixv_in[ts(g), :], sLD)
        em.dma("sync", vec_mixr[:, g:g + 1], mixr_in[ts(g), :], sLD)
    em.dma("sync", cmult[:, :], bass.AP(cmult_in, 0, [[0, 128], [1, 4]]), sLD)
    for g in range(NG):
        em.dma("gpsimd", xg(g, 0, TL + 1), x_in[ts(g), :], sLD)
    em.wait("scalar", sLD)
    em.op("scalar", lambda e: e.activation(vec_w[:, :], vec_dec[:, :], ACT.Exp), inc=sSC)
    em.wait("vector", sSC)
    em.op("vector", lambda e: e.tensor_scalar_mul(vec_w[:, :], vec_w[:, :], -1.0), inc=sDV)
    em.wait("scalar", sDV)
    em.op("scalar", lambda e: e.activation(vec_lam[:, :], vec_w[:, :], ACT.Exp))
    em.op("scalar", lambda e: e.activation(vec_eu[:, :], vec_eu[:, :], ACT.Exp), inc=sSC)
    em.wait("vector", sSC)
    em.op("vector", lambda e: e.tensor_copy(onescol_b, onescol), inc=sDV)

    def wtile_src(i):
        if i < NG:
            return wfc_in[ts(i), :]
        j, half = divmod(i - NG, 2)
        return bass.AP(wproj_in, j * 128 * D + half * 1024, [[D, 128], [1, 1024]])

    wsem = [sWA, sWB]
    ld_marks = [None] * 33
    em.dma("sync", wt0[:, :], wtile_src(0), wsem[0])
    ld_marks[0] = em.val(wsem[0])
    for i in range(32):
        cur = wt0 if i % 2 == 0 else wt1
        if i + 1 < 32:
            em.dma("sync", (wt1 if i % 2 == 0 else wt0)[:, :], wtile_src(i + 1),
                   wsem[(i + 1) % 2])
            ld_marks[i + 1] = em.val(wsem[(i + 1) % 2])
        em.wait("vector", wsem[i % 2], ld_marks[i])
        em.tiny("vector", lambda e, cur=cur: e.tensor_reduce(
            stat[:, 1:2], cur[:, :], AX, OP.add, apply_absolute_value=True))
        dst = stat[:, 2:3] if i < NG else stat[:, 3:4]
        if i == 0 or i == NG:
            em.tiny("vector", lambda e, dst=dst: e.tensor_copy(dst, stat[:, 1:2]),
                    inc=sDV)
        else:
            em.tiny("vector", lambda e, dst=dst: e.tensor_add(dst, dst, stat[:, 1:2]),
                    inc=sDV)
        if i + 1 < 32:
            em.wait("sync", sDV)
    em.tiny("vector", lambda e: e.tensor_copy(stat[:, 4:6], stat[:, 2:4]), inc=sDV)
    em.wait("tensor", sDV)
    em.op("tensor", lambda e: e.matmul(ps1[0:1, 0:2], onescol, stat[:, 4:6],
                                       start=True, stop=True), inc=sTE)
    em.wait("vector", sTE)
    em.tiny("vector", lambda e: e.tensor_copy(stat[0:1, 6:8], ps1[0:1, 0:2]), inc=sDV)
    em.wait("gpsimd", sDV)
    em.dma("gpsimd", gam_in_d[:, :], stat[0:1, 6:8], sXD)
    em.wait("gpsimd", sXD)
    em.op("gpsimd", lambda e: e.collective_compute(
        "AllReduce", OP.add, replica_groups=ALL8,
        ins=[gam_in_d.ap().opt()], outs=[gam_out_d.ap().opt()]), inc=sCC)
    em.wait("sync", sCC)
    em.dma("sync", stat[0:1, 6:8], gam_out_d[:, :], sLD)
    em.wait("vector", sLD)
    NW = float(HID * D)
    em.tiny("vector", lambda e: e.tensor_scalar_mul(stat[0:1, 8:10], stat[0:1, 6:8],
                                                    1.0 / NW))
    em.tiny("vector", lambda e: e.tensor_scalar_add(stat[0:1, 10:12], stat[0:1, 8:10],
                                                    1e-6))
    em.tiny("vector", lambda e: e.tensor_scalar_mul(stat[0:1, 10:12], stat[0:1, 10:12],
                                                    0.5))
    em.tiny("vector", lambda e: e.tensor_scalar_mul(stat[0:1, 12:14], stat[0:1, 8:10],
                                                    0.5), inc=sDV)
    em.wait("tensor", sDV)
    em.op("tensor", lambda e: e.matmul(psB16[:, 0:4], onesrow, stat[0:1, 10:14],
                                       start=True, stop=True), inc=sTE)
    em.wait("vector", sTE)
    em.tiny("vector", lambda e: e.tensor_copy(scalars[:, 0:4], psB16[:, 0:4]), inc=sDV)
    em.tiny("vector", lambda e: e.tensor_scalar_mul(negthr[:, 0:2], scalars[:, 0:2],
                                                    -1.0), inc=sDV)
    em.wait("scalar", sDV)
    em.wait("sync", sDV)

    qa = AR[:, G[0]:G[0] + 1024]
    qb = AR[:, G[0] + 1024:G[0] + 2048]

    def qtile_dst(i):
        if i < NG:
            return qfc_in_d[ts(i), :]
        j, half = divmod(i - NG, 2)
        return bass.AP(qproj_in_d, j * 128 * D + half * 1024, [[D, 128], [1, 1024]])

    em.dma("sync", wt0[:, :], wtile_src(0), wsem[0])
    ld_marks[0] = em.val(wsem[0])
    for i in range(32):
        cur = wt0 if i % 2 == 0 else wt1
        if i + 1 < 32:
            em.dma("sync", (wt1 if i % 2 == 0 else wt0)[:, :], wtile_src(i + 1),
                   wsem[(i + 1) % 2])
            ld_marks[i + 1] = em.val(wsem[(i + 1) % 2])
        em.wait("scalar", wsem[i % 2], ld_marks[i])
        thr_pos = scalars[:, 0:1] if i < NG else scalars[:, 1:2]
        thr_neg = negthr[:, 0:1] if i < NG else negthr[:, 1:2]
        em.op("scalar", lambda e, cur=cur, b=thr_neg: e.activation(
            qa, cur[:, :], ACT.Sign, bias=b))
        em.op("scalar", lambda e, cur=cur, b=thr_pos: e.activation(
            qb, cur[:, :], ACT.Sign, bias=b), inc=sSC)
        em.wait("vector", sSC)
        em.op("vector", lambda e: e.tensor_add(qa, qa, qb), inc=sDV)
        em.wait("gpsimd", sDV)
        em.dma("gpsimd", qtile_dst(i), qa, sXD)
        if i + 1 < 32:
            em.wait("sync", sSC)
            em.wait("scalar", sXD)
    em.wait("gpsimd", sXD)
    em.op("gpsimd", lambda e: e.collective_compute(
        "AllGather", OP.bypass, replica_groups=ALL8,
        ins=[qfc_in_d.ap().opt()], outs=[qfc_d.ap().opt()]), inc=sCC)
    em.op("gpsimd", lambda e: e.collective_compute(
        "AllGather", OP.bypass, replica_groups=ALL8,
        ins=[qproj_in_d.ap().opt()], outs=[qproj_d.ap().opt()]), inc=sCC)
    em.flush()
    pctx.close()

    if upto < 1:
        em.ctx.close()
        return nc, dbg

    pctx = contextlib.ExitStack()
    psA = pctx.enter_context(nc.psum_tensor("p1_psA", [128, 512], F32))
    psB = pctx.enter_context(nc.psum_tensor("p1_psB", [128, 512], F32))
    MIX, EK, YV, SRW = G[0], G[1], G[2], G[3]
    numb = AR[:, MIX:MIX + TL]
    wbuf = [smv(S0, 2048), smv(S1, 2048)]
    wslot_sem = [sWA, sWB]
    wmarks = {}

    def prefetch_w(src, g, slot):
        em.wait("gpsimd", sTE)
        em.dma("gpsimd", bass.AP(SM, 512 + slot * 2048, [[12288, 128], [128, NG], [1, 128]]),
               bass.AP(src, g * D * 128, [[128, 128], [128 * 128, NG], [1, 128]]),
               wslot_sem[slot])
        wmarks[g] = (slot, em.val(wslot_sem[slot]))

    def emit_mix(mixvec):
        for g in range(NG):
            em.op("vector", lambda e, g=g: e.tensor_sub(
                fm(MIX, g), xg(g, 1, TL + 1), xg(g, 0, TL)))
            em.op("vector", lambda e, g=g, mv=mixvec: e.scalar_tensor_tensor(
                fm(MIX, g), fm(MIX, g), mv[:, g:g + 1], xg(g, 0, TL),
                OP.mult, OP.add), inc=sDV)

    def emit_kvr(kind, src_in):
        for g in range(NG):
            if g + 1 < NG:
                prefetch_w(src_in, g + 1, (g + 1) % 2)
            sl, mv = wmarks[g]
            em.wait("tensor", wslot_sem[sl], mv)
            wb = wbuf[g % 2]
            for th in range(2):
                ps = psA if th == 0 else psB
                for c in range(NG):
                    em.op("tensor", lambda e, ps=ps, wb=wb, c=c, th=th: e.matmul(
                        ps[:, :], wb[:, ts(c)], fm(MIX, c)[:, ts(th, 512)],
                        start=(c == 0), stop=(c == NG - 1)),
                        inc=(sTE if c == NG - 1 else None))
                tev = em.val(sTE)
                if kind == "k":
                    em.wait("scalar", sTE, tev)
                    em.op("scalar", lambda e, ps=ps, g=g, th=th: e.activation(
                        fm(EK, g)[:, ts(th, 512)], ps[:, :], ACT.Exp), inc=sSC)
                    em.wait("tensor", sSC)
                elif kind == "v":
                    em.wait("vector", sTE, tev)
                    em.op("vector", lambda e, ps=ps, g=g, th=th: e.tensor_mul(
                        fm(YV, g)[:, ts(th, 512)], fm(EK, g)[:, ts(th, 512)], ps[:, :]),
                        inc=sDV)
                    em.wait("tensor", sDV)
                else:
                    em.wait("scalar", sTE, tev)
                    em.op("scalar", lambda e, ps=ps, g=g, th=th: e.activation(
                        fm(SRW, g)[:, ts(th, 512)], ps[:, :], ACT.Sigmoid), inc=sSC)
                    em.wait("tensor", sSC)

    prefetch_w(wk_in, 0, 0)
    emit_mix(vec_mixk)
    em.wait("tensor", sDV)
    emit_kvr("k", wk_in)
    prefetch_w(wv_in, 0, 0)
    em.wait("vector", sTE)
    emit_mix(vec_mixv)
    em.wait("tensor", sDV)
    em.wait("vector", sSC)
    emit_kvr("v", wv_in)
    prefetch_w(wr_in, 0, 0)
    em.wait("vector", sTE)
    emit_mix(vec_mixr)
    em.wait("tensor", sDV)
    emit_kvr("r", wr_in)
    em.flush()

    def lam_bcast(g):
        return bass.AP(ARF, 3108 + g, [[ARFW, 128], [0, TL]])

    for g in range(NG):
        em.op("vector", lambda e, g=g: e.tensor_copy(lamfull, lam_bcast(g)))
        em.tiny("vector", lambda e, g=g: e.tensor_tensor_scan(
            sa_full[:, 1:TL + 1], lamfull, fm(YV, g), 0.0, OP.mult, OP.add))
        em.tiny("vector", lambda e, g=g: e.tensor_copy(tot_sb[:, 2 * g:2 * g + 1],
                                                       sa_full[:, TL:TL + 1]))
        em.tiny("vector", lambda e, g=g: e.tensor_tensor_scan(
            sa_full[:, 1:TL + 1], lamfull, fm(EK, g), 0.0, OP.mult, OP.add))
        em.tiny("vector", lambda e, g=g: e.tensor_copy(tot_sb[:, 2 * g + 1:2 * g + 2],
                                                       sa_full[:, TL:TL + 1]), inc=sDV)
    em.wait("gpsimd", sDV)
    em.op("gpsimd", lambda e: e.dma_start(
        out=bass.AP(tot_in_d, 0, [[2, 128], [256, NG], [1, 2]]),
        in_=bass.AP(ARF, 3220, [[ARFW, 128], [2, NG], [1, 2]])), inc=sXD, incn=16)
    em.wait("gpsimd", sXD)
    em.op("gpsimd", lambda e: e.collective_compute(
        "AllGather", OP.bypass, replica_groups=GRPS,
        ins=[tot_in_d.ap().opt()], outs=[tot_ag_d.ap().opt()]), inc=sCC)
    em.wait("sync", sCC)
    for g in range(NG):
        em.dma("sync", tots_g[:, :],
               bass.AP(tot_ag_d, g * 256, [[2, 128], [D * 2, 4], [1, 2]]), sLD)
        em.wait("scalar", sLD)
        for j in range(4):
            em.op("scalar", lambda e, g=g, j=j: e.activation(
                coef_g[:, j:j + 1], vec_w[:, g:g + 1], ACT.Exp,
                scale=cmult[:, j:j + 1]), inc=(sSC if j == 3 else None))
        em.wait("vector", sSC)
        em.tiny("vector", lambda e, g=g: e.tensor_mul(
            init_sb[:, 2 * g:2 * g + 2], tots_g[:, 0:2],
            bass.AP(ARF, 3292, [[ARFW, 128], [0, 2]])))
        for j in range(1, 4):
            em.tiny("vector", lambda e, g=g, j=j: e.scalar_tensor_tensor(
                init_sb[:, 2 * g:2 * g + 2], tots_g[:, 2 * j:2 * j + 2],
                coef_g[:, j:j + 1], init_sb[:, 2 * g:2 * g + 2], OP.mult, OP.add),
                inc=(sDV if j == 3 else None))
        em.wait("sync", sDV)
    em.flush()

    for g in range(NG):
        em.op("vector", lambda e, g=g: e.tensor_copy(lamfull, lam_bcast(g)))
        em.tiny("vector", lambda e, g=g: e.tensor_copy(sa_full[:, 0:1],
                                                       init_sb[:, 2 * g:2 * g + 1]))
        em.tiny("vector", lambda e, g=g: e.tensor_copy(sb_full[:, 0:1],
                                                       init_sb[:, 2 * g + 1:2 * g + 2]))
        em.op("vector", lambda e, g=g: e.tensor_tensor_scan(
            sa_full[:, 1:TL + 1], lamfull, fm(YV, g),
            init_sb[:, 2 * g:2 * g + 1], OP.mult, OP.add))
        em.tiny("vector", lambda e, g=g: e.tensor_tensor_scan(
            sb_full[:, 1:TL + 1], lamfull, fm(EK, g),
            init_sb[:, 2 * g + 1:2 * g + 2], OP.mult, OP.add))
        em.op("vector", lambda e, g=g: e.scalar_tensor_tensor(
            numb, fm(YV, g), vec_eu[:, g:g + 1], sa_full[:, 0:TL], OP.mult, OP.add))
        em.op("vector", lambda e, g=g: e.scalar_tensor_tensor(
            sa_full[:, 0:TL], fm(EK, g), vec_eu[:, g:g + 1], sb_full[:, 0:TL],
            OP.mult, OP.add))
        em.op("vector", lambda e: e.reciprocal(sa_full[:, 0:TL], sa_full[:, 0:TL]))
        em.op("vector", lambda e, g=g: e.tensor_mul(fm(YV, g), numb, sa_full[:, 0:TL]))
        em.op("vector", lambda e, g=g: e.tensor_mul(fm(SRW, g), fm(SRW, g), fm(YV, g)),
              inc=sDV)
    em.flush()

    if debug:
        dbg["dbg_init"] = nc.dram_tensor("dbg_init", [128, 2 * NG], F32,
                                         kind="ExternalOutput")
        dbg["dbg_tots"] = nc.dram_tensor("dbg_tots", [4, D, 2], F32,
                                         kind="ExternalOutput")
        for g in range(NG):
            em.dma("gpsimd", dbg["dbg_ek"][ts(g), :], fm(EK, g), sXD)
            em.dma("gpsimd", dbg["dbg_srw"][ts(g), :], fm(SRW, g), sXD)
        em.dma("gpsimd", dbg["dbg_init"][:, :], init_sb[:, :], sXD)
        em.dma("gpsimd", dbg["dbg_tots"][:, :, :], tot_ag_d[:, :, :], sXD)
        em.flush()

    prefetch_w(wo_in, 0, 0)
    for c in range(NG):
        if c + 1 < NG:
            prefetch_w(wo_in, c + 1, (c + 1) % 2)
        sl, mv = wmarks[c]
        em.wait("tensor", wslot_sem[sl], mv)
        wb = wbuf[c % 2]
        for th in range(2):
            ps = psA if th == 0 else psB
            for dd in range(NG):
                em.op("tensor", lambda e, ps=ps, wb=wb, dd=dd, th=th: e.matmul(
                    ps[:, :], wb[:, ts(dd)], fm(SRW, dd)[:, ts(th, 512)],
                    start=(dd == 0), stop=(dd == NG - 1)),
                    inc=(sTE if dd == NG - 1 else None))
            em.wait("vector", sTE)
            em.op("vector", lambda e, ps=ps, c=c, th=th: e.tensor_add(
                xg(c, 1 + th * 512, 1 + th * 512 + 512), ps[:, :],
                xg(c, 1 + th * 512, 1 + th * 512 + 512)), inc=sDV)
            em.wait("tensor", sDV)
    em.flush()
    if debug:
        for g in range(NG):
            em.dma("gpsimd", dbg["dbg_r1"][ts(g), :], xg(g, 1, TL + 1), sXD)
        em.flush()
    pctx.close()

    pctx = contextlib.ExitStack()
    psT = [pctx.enter_context(nc.psum_tensor(f"tr_psT{i}", [128, 128], BF16))
           for i in range(4)]
    R1TOK, X1TOK, XNT = G[0], G[1], G[2]

    def emit_transpose(src_tile_fn, dst_tile_fn, n_i, n_j):
        cnt = 0
        for i in range(n_i):
            for j in range(n_j):
                pst = psT[cnt % 4]
                em.op("tensor", lambda e, i=i, j=j, pst=pst: e.transpose(
                    pst[:, :], src_tile_fn(i, j), identb), inc=sTE)
                eng = "vector" if cnt % 2 == 0 else "scalar"
                em.wait(eng, sTE)
                if eng == "vector":
                    em.op(eng, lambda e, i=i, j=j, pst=pst: e.tensor_copy(
                        dst_tile_fn(i, j), pst[:, :]), inc=sDV)
                else:
                    em.op(eng, lambda e, i=i, j=j, pst=pst: e.copy(
                        dst_tile_fn(i, j), pst[:, :]), inc=sSC)
                cnt += 1
                if cnt % 4 == 0:
                    em.wait("tensor", sDV)
                    em.wait("tensor", sSC)
        em.wait("tensor", sDV)
        em.wait("tensor", sSC)

    emit_transpose(lambda i, j: xg(i, 1 + j * 128, 1 + (j + 1) * 128),
                   lambda i, j: tmt(R1TOK, j)[:, ts(i)], NG, 8)
    em.flush()
    if debug:
        for tt in range(8):
            em.dma("gpsimd", dbg["dbg_r1tok"][ts(tt), :], tmt(R1TOK, tt), sXD)
        em.flush()

    sqscr = smv(S1, 2048)

    def emit_ln(src_ap, dst_ap, eps, ln_dim, dst2_ap=None, eps2=None, sq_w=2048,
                sq_ap=None):
        sqd = sqscr if sq_ap is None else sq_ap
        em.wait("scalar", sDV)
        em.tiny("vector", lambda e: e.tensor_reduce(stat[:, 0:1], src_ap, AX, OP.add),
                inc=sDV)
        nchunk = ln_dim // sq_w
        for ci in range(nchunk):
            em.op("scalar", lambda e, ci=ci: e.activation(
                sqd[:, 0:sq_w], src_ap[:, ci * sq_w:(ci + 1) * sq_w],
                ACT.Square, accum_out=stat[:, 8 + ci:9 + ci]), inc=sSC)
        em.wait("vector", sSC)
        for ci in range(1, nchunk):
            em.tiny("vector", lambda e, ci=ci: e.tensor_add(
                stat[:, 8:9], stat[:, 8:9], stat[:, 8 + ci:9 + ci]))
        em.tiny("vector", lambda e: e.tensor_scalar_mul(stat[:, 2:3], stat[:, 0:1],
                                                        1.0 / ln_dim))
        em.tiny("vector", lambda e: e.tensor_scalar_mul(stat[:, 1:2], stat[:, 8:9],
                                                        1.0 / ln_dim))
        em.tiny("vector", lambda e: e.tensor_mul(stat[:, 3:4], stat[:, 2:3],
                                                 stat[:, 2:3]))
        em.tiny("vector", lambda e: e.tensor_sub(stat[:, 3:4], stat[:, 1:2],
                                                 stat[:, 3:4]), inc=sDV)
        em.tiny("vector", lambda e, eps=eps: e.tensor_scalar_add(stat[:, 9:10],
                                                                 stat[:, 3:4], eps),
                inc=sDV)
        em.wait("scalar", sDV)
        em.op("scalar", lambda e: e.activation(stat[:, 4:5], stat[:, 9:10], ACT.Ln),
              inc=sSC)
        em.wait("vector", sSC)
        em.tiny("vector", lambda e: e.tensor_scalar_mul(stat[:, 4:5], stat[:, 4:5],
                                                        -0.5), inc=sDV)
        em.wait("scalar", sDV)
        em.op("scalar", lambda e: e.activation(stat[:, 4:5], stat[:, 4:5], ACT.Exp),
              inc=sSC)
        em.wait("vector", sSC)
        em.op("vector", lambda e: e.tensor_single_scalar(dst_ap, src_ap, stat[:, 2:3],
                                                         OP.subtract))
        em.op("vector", lambda e: e.tensor_single_scalar(dst_ap, dst_ap, stat[:, 4:5],
                                                         OP.mult), inc=sDV)
        if dst2_ap is not None:
            em.tiny("vector", lambda e, eps=eps: e.tensor_scalar_add(stat[:, 6:7],
                                                                     stat[:, 3:4], eps))
            em.tiny("vector", lambda e: e.reciprocal(stat[:, 6:7], stat[:, 6:7]))
            em.tiny("vector", lambda e: e.tensor_mul(stat[:, 6:7], stat[:, 3:4],
                                                     stat[:, 6:7]))
            em.tiny("vector", lambda e, eps2=eps2: e.tensor_scalar_add(
                stat[:, 6:7], stat[:, 6:7], eps2), inc=sDV)
            em.wait("scalar", sDV)
            em.op("scalar", lambda e: e.activation(stat[:, 6:7], stat[:, 6:7], ACT.Ln),
                  inc=sSC)
            em.wait("vector", sSC)
            em.tiny("vector", lambda e: e.tensor_scalar_mul(stat[:, 6:7], stat[:, 6:7],
                                                            -0.5), inc=sDV)
            em.wait("scalar", sDV)
            em.op("scalar", lambda e: e.activation(stat[:, 6:7], stat[:, 6:7], ACT.Exp),
                  inc=sSC)
            em.wait("vector", sSC)
            em.tiny("vector", lambda e: e.tensor_mul(stat[:, 6:7], stat[:, 4:5],
                                                     stat[:, 6:7]))
            em.op("vector", lambda e: e.tensor_single_scalar(dst2_ap, src_ap,
                                                             stat[:, 2:3], OP.subtract))
            em.op("vector", lambda e: e.tensor_single_scalar(dst2_ap, dst2_ap,
                                                             stat[:, 6:7], OP.mult),
                  inc=sDV)

    for tt in range(8):
        emit_ln(tmt(R1TOK, tt), tmt(X1TOK, tt), 1e-5, D,
                dst2_ap=tmt(XNT, tt), eps2=1e-6)
        if debug and tt == 3:
            if "dbg_stat" not in dbg:
                dbg["dbg_stat"] = nc.dram_tensor("dbg_stat", [128, 16], F32,
                                                 kind="ExternalOutput")
            em.wait("gpsimd", sDV)
            em.wait("gpsimd", sSC)
            em.dma("gpsimd", dbg["dbg_stat"][:, :], stat[:, :], sXD)
            em.wait("vector", sXD)
            em.wait("scalar", sXD)
    em.flush()
    if debug:
        for tt in range(8):
            em.dma("gpsimd", dbg["dbg_x1"][ts(tt), :], tmt(X1TOK, tt), sXD)
        em.flush()
    pctx.close()

    if upto < 2:
        em.wait("gpsimd", sXD)
        em.wait("gpsimd", sCC)
        em.flush()
        em.ctx.close()
        return nc, dbg

    pctx = contextlib.ExitStack()
    psT2 = [pctx.enter_context(nc.psum_tensor(f"p2_psT{i}", [128, 128], BF16))
            for i in range(4)]
    XHF = G[3]

    for tt in range(8):
        srcx = tmt(XNT, tt)
        em.tiny("vector", lambda e, s=srcx: e.tensor_reduce(
            stat[:, 0:1], s, AX, OP.max, apply_absolute_value=True))
        em.tiny("vector", lambda e: e.tensor_scalar_mul(
            stat[:, 1:2], stat[:, 0:1], (128.0 - 1e-6) / 128.0))
        em.tiny("vector", lambda e: e.tensor_scalar_mul(stat[:, 2:3], stat[:, 1:2],
                                                        -1.0))
        em.op("vector", lambda e, s=srcx: e.tensor_single_scalar(
            s, s, stat[:, 1:2], OP.min))
        em.op("vector", lambda e, s=srcx: e.tensor_single_scalar(
            s, s, stat[:, 2:3], OP.max), inc=sDV)
    em.flush()
    if debug:
        for tt in range(8):
            em.dma("gpsimd", dbg["dbg_xhat"][ts(tt), :], tmt(XNT, tt), sXD)
        em.flush()

    cnt2 = [0]
    for jt in range(8):
        for ic in range(NG):
            pst = psT2[cnt2[0] % 4]
            em.op("tensor", lambda e, jt=jt, ic=ic, pst=pst: e.transpose(
                pst[:, :], tmt(XNT, jt)[:, ts(ic)], identb), inc=sTE)
            eng = "vector" if cnt2[0] % 2 == 0 else "scalar"
            em.wait(eng, sTE)
            if eng == "vector":
                em.op(eng, lambda e, jt=jt, ic=ic, pst=pst: e.tensor_copy(
                    fm(XHF, ic)[:, ts(jt)], pst[:, :]), inc=sDV)
            else:
                em.op(eng, lambda e, jt=jt, ic=ic, pst=pst: e.copy(
                    fm(XHF, ic)[:, ts(jt)], pst[:, :]), inc=sSC)
            cnt2[0] += 1
            if cnt2[0] % 4 == 0:
                em.wait("tensor", sDV)
                em.wait("tensor", sSC)
    em.wait("tensor", sDV)
    em.wait("tensor", sSC)
    em.flush()
    pctx.close()

    pctx = contextlib.ExitStack()
    psF = [pctx.enter_context(nc.psum_tensor(f"fc_ps{i}", [128, 512], F32))
           for i in range(2)]
    em.wait("gpsimd", sCC)
    wtile = [smv(S0, 128), smv(S0 + 128, 128)]
    hrow = [smv(S1, 1024), smv(S1 + 1024, 1024)]
    wm = {}

    def load_qfc_tile(hc, c, slot):
        sh, off = divmod(hc * 128, 1024)
        em.dma("gpsimd", wtile[slot],
               bass.AP(qfc_d, sh * D * 1024 + c * 128 * 1024 + off,
                       [[1024, 128], [1, 128]]), wslot_sem[slot])
        wm[(hc, c)] = (slot, em.val(wslot_sem[slot]))

    nseq = [0]
    load_qfc_tile(0, 0, 0)
    for hc in range(64):
        for c in range(NG):
            idx = nseq[0]
            slot = idx % 2
            if idx + 1 < 64 * NG:
                hn, cn = divmod(idx + 1, NG)
                em.wait("gpsimd", sTE)
                load_qfc_tile(hn, cn, 1 - slot)
            sl, mv = wm[(hc, c)]
            em.wait("tensor", wslot_sem[sl], mv)
            for th in range(2):
                em.op("tensor", lambda e, ps=psF[th], c=c, th=th, sl=sl: e.matmul(
                    ps[:, :], wtile[sl], fm(XHF, c)[:, th * 512:(th + 1) * 512],
                    start=(c == 0), stop=(c == NG - 1)),
                    inc=(sTE if th == 1 else None))
            nseq[0] += 1
        tev = em.val(sTE)
        hslot = hc % 2
        ssem = sS0 if hslot == 0 else sS1
        em.wait("scalar", sTE, tev)
        em.wait("scalar", ssem)
        for th in range(2):
            em.op("scalar", lambda e, ps=psF[th], th=th, hs=hrow[hslot]: e.activation(
                hs[:, th * 512:(th + 1) * 512], ps[:, :], ACT.Gelu,
                scale=scalars[:, 2:3]), inc=sSC)
        em.wait("tensor", sSC)
        em.wait("gpsimd", sSC)
        em.dma("gpsimd", hf_d[ts(hc), :], hrow[hslot], ssem)

    em.flush()
    pctx.close()

    if debug:
        dbg["dbg_hfr"] = nc.dram_tensor("dbg_hfr", [HID, TL], BF16,
                                        kind="ExternalOutput")
        dbg["dbg_qfc"] = nc.dram_tensor("dbg_qfc", [D, 1024], BF16,
                                        kind="ExternalOutput")
        dbg["dbg_scal"] = nc.dram_tensor("dbg_scal", [128, 8], F32,
                                         kind="ExternalOutput")
        em.wait("gpsimd", sS0)
        em.wait("gpsimd", sS1)
        em.dma("gpsimd", dbg["dbg_hfr"][:, :], hf_d[:, :], sXD)
        em.dma("gpsimd", dbg["dbg_qfc"][:, :], qfc_in_d[:, :], sXD)
        em.dma("gpsimd", dbg["dbg_scal"][:, :], scalars[:, :], sXD)
        em.flush()

    em.wait("sync", sS0)
    em.wait("sync", sS1)
    htok = [AR[:, G[0]:G[0] + HID], AR[:, G[0] + HID:G[0] + 2 * HID]]
    hm = []
    for tt in range(8):
        em.wait("sync", sS0 if tt % 2 == 0 else sS1)
        em.dma("sync", htok[tt % 2], bass.AP(hf_d, tt * 128, [[TL, HID], [1, 128]]),
               wslot_sem[tt % 2], transpose=True)
        hm.append(em.val(wslot_sem[tt % 2]))
        em.wait("vector", wslot_sem[tt % 2], hm[tt])
        em.wait("scalar", wslot_sem[tt % 2], hm[tt])
        emit_ln(htok[tt % 2], htok[tt % 2], 1e-6, HID)
        em.wait("vector", sSC)
        em.tiny("vector", lambda e, s=htok[tt % 2]: e.tensor_reduce(
            stat[:, 0:1], s, AX, OP.max, apply_absolute_value=True))
        em.tiny("vector", lambda e: e.tensor_scalar_mul(
            stat[:, 1:2], stat[:, 0:1], (128.0 - 1e-6) / 128.0))
        em.tiny("vector", lambda e: e.tensor_scalar_mul(stat[:, 2:3], stat[:, 1:2],
                                                        -1.0))
        em.op("vector", lambda e, s=htok[tt % 2]: e.tensor_single_scalar(
            s, s, stat[:, 1:2], OP.min))
        em.op("vector", lambda e, s=htok[tt % 2]: e.tensor_single_scalar(
            s, s, stat[:, 2:3], OP.max), inc=sDV)
        em.wait("gpsimd", sDV)
        em.dma("gpsimd", hhat_d[ts(tt), :], htok[tt % 2], sS0 if tt % 2 == 0 else sS1)
        if debug:
            em.dma("gpsimd", dbg["dbg_h"][ts(tt), :], htok[tt % 2], sXD)
        if tt + 1 < 8:
            em.wait("sync", sDV)
    em.flush()

    pctx = contextlib.ExitStack()
    psP = [pctx.enter_context(nc.psum_tensor(f"pj_ps{i}", [128, 512], F32))
           for i in range(2)]
    em.wait("sync", sS0)
    em.wait("sync", sS1)

    def hfv(hc):
        base = [G[0], G[2], G[3], OX][hc // 16]
        return AR[:, base + (hc % 16) * 1024: base + (hc % 16) * 1024 + 1024]

    hfm = []
    for hc in range(64):
        em.dma("sync", hfv(hc), bass.AP(hhat_d, hc * 128, [[HID, TL], [1, 128]]),
               sPF, transpose=True)
        hfm.append(em.val(sPF))
    frow = [smv(S1, 1024), smv(S1 + 1024, 1024)]
    wm2 = {}

    def load_qproj_tile(dc, hc, slot):
        sh, off = divmod(hc * 128, 1024)
        em.dma("gpsimd", wtile[slot],
               bass.AP(qproj_d, sh * 1024 * D + off * D + dc * 128,
                       [[D, 128], [1, 128]]), wslot_sem[slot])
        wm2[(dc, hc)] = (slot, em.val(wslot_sem[slot]))

    nseq = [0]
    load_qproj_tile(0, 0, 0)
    for dc in range(NG):
        for hc in range(64):
            idx = nseq[0]
            slot = idx % 2
            if idx + 1 < NG * 64:
                dn, hn = divmod(idx + 1, 64)
                em.wait("gpsimd", sTE)
                load_qproj_tile(dn, hn, 1 - slot)
            sl, mv = wm2[(dc, hc)]
            em.wait("tensor", wslot_sem[sl], mv)
            if dc == 0 and hc == 0:
                em.wait("tensor", sPF, hfm[63])
            for th in range(2):
                em.op("tensor", lambda e, ps=psP[th], th=th, sl=sl, hc=hc: e.matmul(
                    ps[:, :], wtile[sl], hfv(hc)[:, th * 512:(th + 1) * 512],
                    start=(hc == 0), stop=(hc == 63)),
                    inc=(sTE if th == 1 else None))
            nseq[0] += 1
        tev = em.val(sTE)
        fslot = dc % 2
        fsem = sS0 if fslot == 0 else sS1
        em.wait("vector", sTE, tev)
        em.wait("vector", fsem)
        for th in range(2):
            em.op("vector", lambda e, ps=psP[th], th=th, fs=frow[fslot]: e.tensor_copy(
                fs[:, th * 512:(th + 1) * 512], ps[:, :]), inc=sDV)
        em.wait("tensor", sDV)
        em.wait("gpsimd", sDV)
        em.dma("gpsimd", ffn_d[ts(dc), :], frow[fslot], fsem)
    em.flush()
    pctx.close()

    if debug:
        dbg["dbg_ffn"] = nc.dram_tensor("dbg_ffn", [D, TL], BF16,
                                        kind="ExternalOutput")
        em.wait("gpsimd", sS0)
        em.wait("gpsimd", sS1)
        em.dma("gpsimd", dbg["dbg_ffn"][:, :], ffn_d[:, :], sXD)
        em.flush()

    X2T = G[0]
    em.wait("sync", sS0)
    em.wait("sync", sS1)
    f2m = []
    ftok = [AR[:, G[2]:G[2] + 2048], AR[:, G[2] + 2048:G[2] + 4096]]
    for tt in range(8):
        em.dma("sync", ftok[tt % 2], bass.AP(ffn_d, tt * 128, [[TL, D], [1, 128]]),
               wslot_sem[tt % 2], transpose=True)
        f2m.append(em.val(wslot_sem[tt % 2]))
        em.wait("vector", wslot_sem[tt % 2], f2m[tt])
        em.op("vector", lambda e, tt=tt: e.scalar_tensor_tensor(
            tmt(X2T, tt), ftok[tt % 2], scalars[:, 3:4], tmt(X1TOK, tt),
            OP.mult, OP.add), inc=sDV)
        emit_ln(tmt(X2T, tt), tmt(X2T, tt), 1e-5, D)
        if tt + 1 < 8:
            em.wait("sync", sDV)
    em.flush()
    if debug:
        for tt in range(8):
            em.dma("gpsimd", dbg["dbg_x2"][ts(tt), :], tmt(X2T, tt), sXD)
        em.flush()

    if upto < 3:
        em.wait("gpsimd", sXD)
        em.wait("gpsimd", sCC)
        em.flush()
        em.ctx.close()
        return nc, dbg

    pctx = contextlib.ExitStack()
    psT3 = [pctx.enter_context(nc.psum_tensor(f"p3_psT{i}", [128, 128], BF16))
            for i in range(4)]
    X2F = G[1]
    cnt3 = [0]
    for jt in range(8):
        for ic in range(NG):
            pst = psT3[cnt3[0] % 4]
            em.op("tensor", lambda e, jt=jt, ic=ic, pst=pst: e.transpose(
                pst[:, :], tmt(X2T, jt)[:, ts(ic)], identb), inc=sTE)
            eng = "vector" if cnt3[0] % 2 == 0 else "scalar"
            em.wait(eng, sTE)
            if eng == "vector":
                em.op(eng, lambda e, jt=jt, ic=ic, pst=pst: e.tensor_copy(
                    fm(X2F, ic)[:, ts(jt)], pst[:, :]), inc=sDV)
            else:
                em.op(eng, lambda e, jt=jt, ic=ic, pst=pst: e.copy(
                    fm(X2F, ic)[:, ts(jt)], pst[:, :]), inc=sSC)
            cnt3[0] += 1
            if cnt3[0] % 4 == 0:
                em.wait("tensor", sDV)
                em.wait("tensor", sSC)
    em.wait("tensor", sDV)
    em.wait("tensor", sSC)
    em.wait("gpsimd", sDV)
    em.wait("gpsimd", sSC)
    for g in range(NG):
        em.dma("gpsimd", x2_in_d[ts(g), :], fm(X2F, g), sXD)
    em.wait("gpsimd", sXD)
    em.op("gpsimd", lambda e: e.collective_compute(
        "AllGather", OP.bypass, replica_groups=GRPS,
        ins=[x2_in_d.ap().opt()], outs=[x2_ag_d.ap().opt()]), inc=sCC)
    em.flush()
    pctx.close()

    QF, KF, VA, SLAB = G[2], G[2] + 4096, G[2] + 8192, G[2] + 12320
    NKB = SEQ // 128
    whead = [smv(S0, 2048), smv(S1, 2048)]

    def p3_loadw(src_in, h, slot):
        em.wait("gpsimd", sTE)
        em.dma("gpsimd", bass.AP(SM, 512 + slot * 2048,
                                 [[12288, 128], [128, NG], [1, 128]]),
               bass.AP(src_in, h * 128, [[512, 128], [512 * 128, NG], [1, 128]]),
               wslot_sem[slot])
        return (slot, em.val(wslot_sem[slot]))

    pctx = contextlib.ExitStack()
    psQ = [pctx.enter_context(nc.psum_tensor(f"qk_ps{i}", [128, 512], F32))
           for i in range(2)]
    psVt = [psQ[0][:, 0:128], psQ[1][:, 0:128]]
    psO = [pctx.enter_context(nc.psum_tensor(f"o_ps{i}", [128, 129], F32))
           for i in range(4)]
    psN = psQ[0][:, 0:512]
    psT3b = [pctx.enter_context(nc.psum_tensor("p3_ot", [128, 128], BF16))]
    xbm = {}

    def load_x2blk(sp, slot):
        j, half = divmod(sp, 2)
        em.wait("sync", sTE)
        em.dma("sync", bass.AP(AR, G[1] + slot * 8192,
                               [[ARENA_COLS, 128], [512, NG], [1, 512]]),
               bass.AP(x2_ag_d, j * D * TL + half * 512,
                       [[TL, 128], [128 * TL, NG], [1, 512]]), wslot_sem[slot])
        xbm[sp] = (slot, em.val(wslot_sem[slot]))

    def x2blk_tile(slot, c):
        return AR[:, G[1] + slot * 8192 + c * 512: G[1] + slot * 8192 + (c + 1) * 512]

    em.wait("sync", sCC)
    for h in range(NHC):
        wq_m = p3_loadw(wq_in, h, 0)
        wk_m = p3_loadw(wk1_in, h, 1)
        em.op("vector", lambda e: e.tensor_copy(
            bass.AP(AR, VA + 128, [[ARENA_COLS, 128], [129, 32]]),
            bass.AP(SM, 256, [[12288, 128], [0, 32]])), inc=sDV)
        load_x2blk(0, 0)
        for sp in range(8):
            if sp + 1 < 8:
                load_x2blk(sp + 1, (sp + 1) % 2)
            xsl, xmv = xbm[sp]
            em.wait("tensor", wslot_sem[xsl], xmv)
            em.wait("tensor", wslot_sem[wq_m[0]], wq_m[1])
            em.wait("tensor", wslot_sem[wk_m[0]], wk_m[1])
            for c in range(NG):
                em.op("tensor", lambda e, c=c, xsl=xsl: e.matmul(
                    psQ[0][:, :], whead[0][:, ts(c)], x2blk_tile(xsl, c),
                    start=(c == 0), stop=(c == NG - 1)), inc=sTE)
            tevq = em.val(sTE)
            for c in range(NG):
                em.op("tensor", lambda e, c=c, xsl=xsl: e.matmul(
                    psQ[1][:, :], whead[1][:, ts(c)], x2blk_tile(xsl, c),
                    start=(c == 0), stop=(c == NG - 1)), inc=sTE)
            tevk = em.val(sTE)
            em.wait("vector", sTE, tevq)
            em.op("vector", lambda e, sp=sp: e.tensor_copy(
                AR[:, QF + sp * 512: QF + (sp + 1) * 512], psQ[0][:, :]), inc=sDV)
            em.wait("scalar", sTE, tevk)
            em.op("scalar", lambda e, sp=sp: e.copy(
                AR[:, KF + sp * 512: KF + (sp + 1) * 512], psQ[1][:, :]), inc=sSC)
            em.wait("tensor", sDV)
            em.wait("tensor", sSC)
        wv_m = p3_loadw(wv1_in, h, 0)
        for sp in range(8):
            xsl, xmv = xbm[sp]
            load_x2blk(sp, sp % 2)
            xsl, xmv = xbm[sp]
            em.wait("tensor", wslot_sem[xsl], xmv)
            em.wait("tensor", wslot_sem[wv_m[0]], wv_m[1])
            for t4 in range(4):
                kb = sp * 4 + t4
                ps = psVt[t4 % 2]
                for c in range(NG):
                    em.op("tensor", lambda e, c=c, xsl=xsl, t4=t4, ps=ps: e.matmul(
                        ps[:, :],
                        x2blk_tile(xsl, c)[:, t4 * 128:(t4 + 1) * 128],
                        whead[0][:, ts(c)],
                        start=(c == 0), stop=(c == NG - 1)), inc=sTE)
                tev = em.val(sTE)
                eng = "vector" if t4 % 2 == 0 else "scalar"
                em.wait(eng, sTE, tev)
                if eng == "vector":
                    em.op(eng, lambda e, kb=kb, ps=ps: e.tensor_copy(
                        AR[:, VA + kb * 129: VA + kb * 129 + 128], ps[:, :]), inc=sDV)
                else:
                    em.op(eng, lambda e, kb=kb, ps=ps: e.copy(
                        AR[:, VA + kb * 129: VA + kb * 129 + 128], ps[:, :]), inc=sSC)
                em.wait("tensor", sDV)
                em.wait("tensor", sSC)
        em.flush()

        qsq = AR[:, SLAB:SLAB + 512]
        for which, base in [(0, QF), (1, KF)]:
            dstv = attC if which == 0 else attK
            for chunk in range(8):
                em.op("scalar", lambda e, base=base, chunk=chunk: e.activation(
                    qsq, AR[:, base + chunk * 512: base + (chunk + 1) * 512],
                    ACT.Square), inc=sSC)
                em.wait("tensor", sSC)
                em.op("tensor", lambda e: e.matmul(
                    psN, onessq, qsq, start=True, stop=True), inc=sTE)
                em.wait("vector", sTE)
                em.tiny("vector", lambda e, dstv=dstv, chunk=chunk: e.tensor_reduce(
                    dstv[:, chunk:chunk + 1], psN, AX, OP.max))
                em.op("vector", lambda e: e.tensor_copy(qsq[:, 0:1], qsq[:, 0:1]),
                      inc=sDV)
                em.wait("scalar", sDV)
                em.wait("tensor", sDV)
        for sfx in range(1, 8):
            em.tiny("vector", lambda e, sfx=sfx: e.tensor_max(
                attK[:, sfx:sfx + 1], attK[:, sfx:sfx + 1], attK[:, sfx - 1:sfx]),
                inc=(sDV if sfx == 7 else None))

        for s in range(8):
            em.wait("tensor", sDV)
            em.wait("tensor", sSC)
            em.tiny("vector", lambda e, s=s: e.tensor_mul(
                stat[:, 2:3], attC[:, s:s + 1], attK[:, s:s + 1]), inc=sDV)
            em.wait("scalar", sDV)
            em.op("scalar", lambda e: e.activation(stat[:, 3:4], stat[:, 2:3],
                                                   ACT.Sqrt), inc=sSC)
            em.wait("vector", sSC)
            em.tiny("vector", lambda e: e.tensor_mul(stat[:, 4:5], stat[:, 3:4],
                                                     ascale))
            em.tiny("vector", lambda e: e.tensor_scalar_mul(stat[:, 4:5],
                                                            stat[:, 4:5], -1.0),
                    inc=sDV)
            em.wait("scalar", sDV)
            nkb = 4 * s + 4
            for kb in range(nkb):
                ps = psQ[kb % 2]
                em.op("tensor", lambda e, kb=kb, s=s, ps=ps: e.matmul(
                    ps[:, :], AR[:, KF + kb * 128: KF + (kb + 1) * 128],
                    AR[:, QF + s * 512: QF + (s + 1) * 512],
                    start=True, stop=True), inc=sTE)
                tev = em.val(sTE)
                slab = AR[:, SLAB + (kb % 2) * 512: SLAB + (kb % 2) * 512 + 512]
                em.wait("scalar", sTE, tev)
                em.op("scalar", lambda e, ps=ps, slab=slab: e.activation(
                    slab, ps[:, :], ACT.Exp, bias=stat[:, 4:5], scale=ascale),
                    inc=sSC)
                jd = kb - 4 * s
                if 0 <= jd < 4:
                    em.wait("vector", sSC)
                    em.op("vector", lambda e, slab=slab, jd=jd: e.tensor_mul(
                        slab[:, jd * 128:(jd + 1) * 128],
                        slab[:, jd * 128:(jd + 1) * 128], maskst), inc=sDV)
                    em.wait("tensor", sDV)
                em.wait("tensor", sSC)
                if debug and h == NHC - 1 and s == 0 and kb == 0:
                    if "dbg_slab" not in dbg:
                        dbg["dbg_slab"] = nc.dram_tensor("dbg_slab", [128, 512], BF16,
                                                         kind="ExternalOutput")
                    em.wait("gpsimd", sDV)
                    em.wait("gpsimd", sSC)
                    em.dma("gpsimd", dbg["dbg_slab"][:, :], slab, sXD)
                    em.wait("tensor", sXD)
                for j in range(4):
                    if kb <= 4 * s + j:
                        em.op("tensor", lambda e, j=j, kb=kb, slab=slab: e.matmul(
                            psO[j][:, :], slab[:, j * 128:(j + 1) * 128],
                            AR[:, VA + kb * 129: VA + kb * 129 + 129],
                            start=(kb == 0), stop=(kb == 4 * s + j)), inc=sTE)
            tev = em.val(sTE)
            for j in range(4):
                qt = 4 * s + j
                em.wait("vector", sTE, tev)
                em.tiny("vector", lambda e, j=j: e.tensor_scalar_mul(
                    stat[:, 6:7], psO[j][:, 128:129], float(2.0 ** 44)), inc=sDV)
                em.wait("scalar", sDV)
                em.op("scalar", lambda e: e.activation(
                    stat[:, 6:7], stat[:, 6:7], ACT.Ln), inc=sSC)
                em.wait("vector", sSC)
                em.tiny("vector", lambda e: e.tensor_scalar_mul(
                    stat[:, 6:7], stat[:, 6:7], -1.0), inc=sDV)
                em.wait("scalar", sDV)
                em.op("scalar", lambda e: e.activation(stat[:, 7:8], stat[:, 6:7],
                                                       ACT.Exp), inc=sSC)
                em.wait("vector", sSC)
                em.tiny("vector", lambda e: e.tensor_scalar_mul(
                    stat[:, 7:8], stat[:, 7:8], float(2.0 ** 44)), inc=sDV)
                em.op("vector", lambda e, j=j: e.tensor_single_scalar(
                    qsq[:, 0:128], psO[j][:, 0:128], stat[:, 7:8], OP.mult),
                    inc=sDV)
                em.wait("tensor", sDV)
                em.op("tensor", lambda e: e.transpose(psT3b[0][:, :], qsq[:, 0:128], identb),
                      inc=sTE)
                em.wait("vector", sTE)
                em.op("vector", lambda e, h=h, qt=qt: e.tensor_copy(
                    AR[:, G[3] + h * 4096 + qt * 128: G[3] + h * 4096 + (qt + 1) * 128],
                    psT3b[0][:, :][:, :]), inc=sDV)
                em.wait("tensor", sDV)
                em.wait("scalar", sDV)
        em.flush()
        if debug and h == NHC - 1:
            dbg["dbg_qf"] = nc.dram_tensor("dbg_qf", [128, SEQ], BF16,
                                           kind="ExternalOutput")
            dbg["dbg_kf"] = nc.dram_tensor("dbg_kf", [128, SEQ], BF16,
                                           kind="ExternalOutput")
            dbg["dbg_va"] = nc.dram_tensor("dbg_va", [128, 4128], BF16,
                                           kind="ExternalOutput")
            dbg["dbg_of"] = nc.dram_tensor("dbg_of", [128, SEQ], BF16,
                                           kind="ExternalOutput")
            dbg["dbg_cst"] = nc.dram_tensor("dbg_cst", [128, 16], F32,
                                            kind="ExternalOutput")
            em.dma("gpsimd", dbg["dbg_qf"][:, :], AR[:, QF:QF + SEQ], sXD)
            em.dma("gpsimd", dbg["dbg_kf"][:, :], AR[:, KF:KF + SEQ], sXD)
            em.dma("gpsimd", dbg["dbg_va"][:, :], AR[:, VA:VA + 4128], sXD)
            em.dma("gpsimd", dbg["dbg_of"][:, :],
                   AR[:, G[3] + h * 4096: G[3] + (h + 1) * 4096], sXD)
            em.dma("gpsimd", dbg["dbg_cst"][:, :], stat[:, :], sXD)
            em.wait("gpsimd", sXD)
            em.flush()
    pctx.close()

    pctx = contextlib.ExitStack()
    psC4 = [pctx.enter_context(nc.psum_tensor(f"op_ps{i}", [128, 512], F32))
            for i in range(4)]
    em.wait("gpsimd", sTE)
    for hh in range(4):
        for cs in range(4):
            em.dma("gpsimd", AR[:, G[1] + (hh * 4 + cs) * 512:
                                G[1] + (hh * 4 + cs) * 512 + 512],
                   bass.AP(wo1_in, hh * 128 * D + cs * 512, [[D, 128], [1, 512]]),
                   sPF)
    WO_PF = em.val(sPF)
    ostg = [smv(S0, 2048), smv(S1, 2048)]
    em.wait("tensor", sPF, WO_PF)
    for tt32 in range(32):
        for hh in range(4):
            for cs in range(4):
                em.op("tensor", lambda e, hh=hh, cs=cs, tt32=tt32: e.matmul(
                    psC4[cs][:, :],
                    AR[:, G[3] + hh * 4096 + tt32 * 128:
                       G[3] + hh * 4096 + (tt32 + 1) * 128],
                    AR[:, G[1] + (hh * 4 + cs) * 512: G[1] + (hh * 4 + cs) * 512 + 512],
                    start=(hh == 0), stop=(hh == 3)),
                    inc=(sTE if hh == 3 else None))
        tev = em.val(sTE)
        oslot = tt32 % 2
        osem = sS0 if oslot == 0 else sS1
        em.wait("vector", sTE, tev)
        em.wait("vector", osem)
        for cs in range(4):
            em.op("vector", lambda e, cs=cs, oslot=oslot: e.tensor_copy(
                ostg[oslot][:, cs * 512:(cs + 1) * 512], psC4[cs][:, :]), inc=sDV)
        em.wait("tensor", sDV)
        em.wait("gpsimd", sDV)
        em.dma("gpsimd", rs_in_d[tt32 // 8, ts(tt32 % 8), :], ostg[oslot], osem)
    em.wait("gpsimd", sS0)
    em.wait("gpsimd", sS1)
    em.op("gpsimd", lambda e: e.collective_compute(
        "ReduceScatter", OP.add, replica_groups=GRPS,
        ins=[rs_in_d.ap().opt()], outs=[rs_out_d.ap().opt()]), inc=sCC)
    em.flush()
    pctx.close()

    em.wait("sync", sCC)
    rstg = [AR[:, G[1]:G[1] + 2048], AR[:, G[1] + 2048:G[1] + 4096]]
    for tt in range(8):
        em.dma("sync", rstg[tt % 2], rs_out_d[ts(tt), :], wslot_sem[tt % 2])
        rm = em.val(wslot_sem[tt % 2])
        em.wait("vector", wslot_sem[tt % 2], rm)
        em.op("vector", lambda e, tt=tt: e.tensor_add(
            tmt(X2T, tt), tmt(X2T, tt), rstg[tt % 2]), inc=sDV)
        if debug:
            em.wait("gpsimd", sDV)
            em.dma("gpsimd", dbg["dbg_attn"][ts(tt), :], rstg[tt % 2], sXD)
        emit_ln(tmt(X2T, tt), tmt(X2T, tt), 1e-5, D)
        if tt + 1 < 8:
            em.wait("sync", sDV)
            em.wait("sync", sXD)
    em.flush()
    X3T = X2T
    if debug:
        for tt in range(8):
            em.dma("gpsimd", dbg["dbg_x3"][ts(tt), :], tmt(X3T, tt), sXD)
        em.flush()

    if upto < 4:
        em.wait("gpsimd", sXD)
        em.wait("gpsimd", sCC)
        em.flush()
        em.ctx.close()
        return nc, dbg

    pctx = contextlib.ExitStack()
    psT4 = [pctx.enter_context(nc.psum_tensor(f"p4_psT{i}", [128, 128], BF16))
            for i in range(2)]
    for tt in range(8):
        em.dma("gpsimd", x3_spill_d[ts(tt), :], tmt(X3T, tt), sXD)
    X3F = OX
    cnt4 = [0]
    for jt in range(8):
        for ic in range(NG):
            pst = psT4[cnt4[0] % 2]
            em.op("tensor", lambda e, jt=jt, ic=ic, pst=pst: e.transpose(
                pst[:, :], tmt(X3T, jt)[:, ts(ic)], identb), inc=sTE)
            eng = "vector" if cnt4[0] % 2 == 0 else "scalar"
            em.wait(eng, sTE)
            if eng == "vector":
                em.op(eng, lambda e, jt=jt, ic=ic, pst=pst: e.tensor_copy(
                    AR[:, X3F + ic * TL + jt * 128: X3F + ic * TL + (jt + 1) * 128],
                    pst[:, :]), inc=sDV)
            else:
                em.op(eng, lambda e, jt=jt, ic=ic, pst=pst: e.copy(
                    AR[:, X3F + ic * TL + jt * 128: X3F + ic * TL + (jt + 1) * 128],
                    pst[:, :]), inc=sSC)
            cnt4[0] += 1
            if cnt4[0] % 2 == 0:
                em.wait("tensor", sDV)
                em.wait("tensor", sSC)
    em.wait("tensor", sDV)
    em.wait("tensor", sSC)
    em.flush()
    pctx.close()

    def x3f(c):
        return AR[:, X3F + c * TL: X3F + (c + 1) * TL]

    pctx = contextlib.ExitStack()
    psR = pctx.enter_context(nc.psum_tensor("p4_psR", [128, 8], F32))
    psW = pctx.enter_context(nc.psum_tensor("p4_psW", [8, 128], BF16))
    psB = pctx.enter_context(nc.psum_tensor("p4_psB", [128, 512], F32))
    routb = smv(SSP, 128)
    for c in range(NG):
        em.dma("gpsimd", routb[:, c * 8:(c + 1) * 8], router_in[ts(c), :], sPF)
    RT_PF = em.val(sPF)
    W8 = SM[0:8, S2:S2 + 1024]
    wtt = smv(S0, 8)
    em.wait("tensor", sPF, RT_PF)
    for tt in range(8):
        for c in range(NG):
            em.op("tensor", lambda e, tt=tt, c=c: e.matmul(
                psR[:, :], x3f(c)[:, ts(tt)], routb[:, c * 8:(c + 1) * 8],
                start=(c == 0), stop=(c == NG - 1)), inc=(sTE if c == NG - 1 else None))
        em.wait("vector", sTE)
        em.tiny("vector", lambda e: e.tensor_reduce(stat[:, 0:1], psR[:, :], AX,
                                                    OP.max))
        em.tiny("vector", lambda e: e.tensor_scalar_mul(stat[:, 0:1], stat[:, 0:1],
                                                        -1.0), inc=sDV)
        em.wait("scalar", sDV)
        em.op("scalar", lambda e: e.activation(sqscr[:, 0:8], psR[:, :], ACT.Exp,
                                               bias=stat[:, 0:1]), inc=sSC)
        em.wait("vector", sSC)
        em.tiny("vector", lambda e: e.tensor_single_scalar(
            sqscr[:, 8:16], sqscr[:, 0:8], 1.0, OP.is_ge))
        em.tiny("vector", lambda e: e.scalar_tensor_tensor(
            sqscr[:, 8:16], sqscr[:, 8:16], -1e30, sqscr[:, 0:8], OP.mult, OP.add))
        em.tiny("vector", lambda e: e.tensor_reduce(stat[:, 1:2], sqscr[:, 8:16],
                                                    AX, OP.max))
        em.tiny("vector", lambda e: e.tensor_scalar_add(stat[:, 2:3], stat[:, 1:2],
                                                        1.0))
        em.tiny("vector", lambda e: e.reciprocal(stat[:, 2:3], stat[:, 2:3]))
        em.tiny("vector", lambda e: e.tensor_single_scalar(
            wtt, sqscr[:, 0:8], stat[:, 1:2], OP.is_ge))
        em.tiny("vector", lambda e: e.tensor_mul(wtt, wtt, sqscr[:, 0:8]))
        em.tiny("vector", lambda e: e.tensor_single_scalar(
            wtt, wtt, stat[:, 2:3], OP.mult), inc=sDV)
        em.wait("tensor", sDV)
        em.op("tensor", lambda e: e.transpose(psW[:, :], wtt, identb), inc=sTE)
        em.wait("vector", sTE)
        em.op("vector", lambda e, tt=tt: e.tensor_copy(W8[:, ts(tt)], psW[:, :]),
              inc=sDV)
        em.wait("tensor", sDV)
    em.wait("gpsimd", sDV)
    em.dma("gpsimd", w8_d[:, :], W8[:, :], sXD)
    em.flush()

    psG = [pctx.enter_context(nc.psum_tensor(f"p4_g{i}", [128, 512], F32))
           for i in range(2)]
    psU = [pctx.enter_context(nc.psum_tensor(f"p4_u{i}", [128, 512], F32))
           for i in range(2)]
    gbufs = [smv(512, 2048), smv(2560, 2048)]
    ubufs = [smv(4608, 2048), smv(6656, 2048)]
    wbc = smv(8704, 1024)
    silus = [smv(9728, 512), smv(10240, 512)]
    dstg = smv(10752, 1024)

    def act_ap(e, fc, th):
        base = G[e // 2] + (e % 2) * 8192
        return AR[:, base + fc * 1024 + th * 512: base + fc * 1024 + (th + 1) * 512]

    gu_marks = {}

    def emit_gu_load(e_, fc_, slot):
        em.wait("gpsimd", sTE)
        em.dma("gpsimd",
               bass.AP(SM, 512 + slot * 2048, [[12288, 128], [128, NG], [1, 128]]),
               bass.AP(gate_in, (e_ * 8 + fc_) * D * 128,
                       [[128, 128], [128 * 128, NG], [1, 128]]), sWA)
        gm = em.val(sWA)
        em.dma("gpsimd",
               bass.AP(SM, 4608 + slot * 2048, [[12288, 128], [128, NG], [1, 128]]),
               bass.AP(up_in, (e_ * 8 + fc_) * D * 128,
                       [[128, 128], [128 * 128, NG], [1, 128]]), sWB)
        gu_marks[(e_, fc_)] = (gm, em.val(sWB))

    dcast_seq = [0]

    def emit_downcast_step():
        i = dcast_seq[0]
        if i >= NE * 8 * 2:
            return
        e_, r = divmod(i, 16)
        fc_, half = divmod(r, 2)
        em.wait("gpsimd", sS0)
        em.dma("gpsimd", dstg,
               bass.AP(down_in, e_ * FMOE * D + fc_ * 128 * D + half * 1024,
                       [[D, 128], [1, 1024]]), sDC)
        em.wait("gpsimd", sDC)
        em.dma("gpsimd",
               bass.AP(downb_d, e_ * FMOE * D + fc_ * 128 * D + half * 1024,
                       [[D, 128], [1, 1024]]), dstg, sS0)
        dcast_seq[0] += 1

    for e in range(NE):
        em.wait("tensor", sDV)
        em.wait("sync", sXD)
        em.wait("sync", sTE)
        em.dma("sync", w8row[0:1, :], w8_d[e:e + 1, :], sPF)
        W8M = em.val(sPF)
        em.wait("tensor", sPF, W8M)
        for th2 in range(2):
            em.op("tensor", lambda ee, th2=th2: ee.matmul(
                psB[:, :], onessq[0:1, 0:128], w8row[0:1, th2 * 512:(th2 + 1) * 512],
                start=True, stop=True), inc=sTE)
            em.wait("vector", sTE)
            em.op("vector", lambda ee, th2=th2: ee.tensor_copy(
                wbc[:, th2 * 512:(th2 + 1) * 512], psB[:, :]), inc=sDV)
            em.wait("tensor", sDV)
        for fc in range(8):
            gslot = (e * 8 + fc) % 2
            if e == 0 and fc == 0:
                emit_gu_load(0, 0, 0)
            if e * 8 + fc + 1 < 64:
                en, fn = divmod(e * 8 + fc + 1, 8)
                emit_gu_load(en, fn, 1 - gslot)
            gm, um = gu_marks[(e, fc)]
            em.wait("tensor", sWA, gm)
            em.wait("tensor", sWB, um)
            emit_downcast_step()
            emit_downcast_step()
            gbuf, ubuf = gbufs[gslot], ubufs[gslot]
            for c in range(NG):
                for th in range(2):
                    em.op("tensor", lambda ee, c=c, th=th, gbuf=gbuf: ee.matmul(
                        psG[th][:, :], gbuf[:, ts(c)], x3f(c)[:, ts(th, 512)],
                        start=(c == 0), stop=(c == NG - 1)),
                        inc=(sTE if th == 1 else None))
            for c in range(NG):
                for th in range(2):
                    em.op("tensor", lambda ee, c=c, th=th, ubuf=ubuf: ee.matmul(
                        psU[th][:, :], ubuf[:, ts(c)], x3f(c)[:, ts(th, 512)],
                        start=(c == 0), stop=(c == NG - 1)),
                        inc=(sTE if th == 1 else None))
            tev = em.val(sTE)
            for th in range(2):
                em.wait("scalar", sTE, tev)
                em.op("scalar", lambda ee, th=th: ee.activation(
                    silus[th], psG[th][:, :], ACT.Silu), inc=sSC)
                em.wait("vector", sSC)
                em.op("vector", lambda ee, th=th: ee.tensor_mul(
                    silus[th], silus[th], psU[th][:, :]))
                em.op("vector", lambda ee, e=e, fc=fc, th=th: ee.tensor_mul(
                    act_ap(e, fc, th), silus[th],
                    wbc[:, th * 512:(th + 1) * 512]), inc=sDV)
            em.wait("tensor", sDV)
            em.wait("scalar", sDV)
    em.flush()
    pctx.close()

    if debug:
        dbg["dbg_downb"] = nc.dram_tensor("dbg_downb", [NE, FMOE, D], BF16,
                                          kind="ExternalOutput")
        dbg["dbg_act"] = nc.dram_tensor("dbg_act", [128, 8192], BF16,
                                        kind="ExternalOutput")
        dbg["dbg_wbc"] = nc.dram_tensor("dbg_wbc", [128, 1024], BF16,
                                        kind="ExternalOutput")
        em.wait("gpsimd", sS0)
        em.dma("gpsimd", dbg["dbg_downb"][:, :, :], downb_d[:, :, :], sXD)
        em.dma("gpsimd", dbg["dbg_act"][:, :], AR[:, G[0]:G[0] + 8192], sXD)
        em.dma("gpsimd", dbg["dbg_wbc"][:, :], wbc, sXD)
        em.wait("gpsimd", sXD)
        em.flush()

    pctx = contextlib.ExitStack()
    psD = [pctx.enter_context(nc.psum_tensor(f"p4_d{i}", [128, 512], F32))
           for i in range(8)]
    dbuf = [smv(512, 4096), smv(4608, 4096)]
    x3r = smv(8704, 2048)
    while dcast_seq[0] < NE * 8 * 2:
        emit_downcast_step()
    em.wait("gpsimd", sS0)
    em.wait("sync", sS0)
    R4 = OX
    dn_marks = {}

    def load_down_e(ttg, dsg, e_, ds2, slot):
        em.wait("sync", sTE)
        em.dma("sync", bass.AP(SM, 512 + slot * 4096,
                               [[12288, 128], [512, 8], [1, 512]]),
               bass.AP(downb_d, e_ * FMOE * D + dsg * 1024 + ds2 * 512,
                       [[D, 128], [128 * D, 8], [1, 512]]), wslot_sem[slot])
        dn_marks[(ttg, dsg, e_, ds2)] = (slot, em.val(wslot_sem[slot]))

    def dn_key(i):
        ttg, r = divmod(i, 2 * NE * 2)
        dsg, r2 = divmod(r, NE * 2)
        e_, ds2 = divmod(r2, 2)
        return (ttg, dsg, e_, ds2)

    NDN = 2 * 2 * NE * 2
    load_down_e(*dn_key(0), 0)
    dseq = [0]
    for ttg in range(2):
        for dsg in range(2):
            for e in range(NE):
                for ds2 in range(2):
                    i = dseq[0]
                    if i + 1 < NDN:
                        load_down_e(*dn_key(i + 1), (i + 1) % 2)
                    sl, mv = dn_marks[dn_key(i)]
                    em.wait("tensor", wslot_sem[sl], mv)
                    for fc in range(8):
                        first = (e == 0 and fc == 0)
                        last = (e == NE - 1 and fc == 7)
                        for t4 in range(4):
                            tt = ttg * 4 + t4
                            em.op("tensor", lambda ee, e=e, fc=fc, tt=tt, t4=t4,
                                  ds2=ds2, sl=sl, first=first, last=last: ee.matmul(
                                      psD[t4 * 2 + ds2][:, :],
                                      AR[:, G[e // 2] + (e % 2) * 8192 + fc * 1024
                                         + tt * 128: G[e // 2] + (e % 2) * 8192
                                         + fc * 1024 + (tt + 1) * 128],
                                      dbuf[sl][:, fc * 512:(fc + 1) * 512],
                                      start=first, stop=last),
                                  inc=(sTE if t4 == 3 else None))
                    dseq[0] += 1
            tev = em.val(sTE)
            em.wait("vector", sTE, tev)
            for t4 in range(4):
                tt = ttg * 4 + t4
                em.wait("sync", sDV)
                em.dma("sync", x3r, x3_spill_d[ts(tt), :], sPF)
                xm = em.val(sPF)
                em.wait("vector", sPF, xm)
                for ds2 in range(2):
                    em.op("vector", lambda ee, t4=t4, ds2=ds2, tt=tt, dsg=dsg:
                          ee.tensor_add(
                              AR[:, R4 + tt * 2048 + dsg * 1024 + ds2 * 512:
                                 R4 + tt * 2048 + dsg * 1024 + (ds2 + 1) * 512],
                              psD[t4 * 2 + ds2][:, :],
                              x3r[:, dsg * 1024 + ds2 * 512:
                                  dsg * 1024 + (ds2 + 1) * 512]), inc=sDV)
            em.wait("tensor", sDV)
        for t4 in range(4):
            tt = ttg * 4 + t4
            r4t = AR[:, R4 + tt * 2048: R4 + (tt + 1) * 2048]
            em.wait("vector", sS1)
            em.wait("scalar", sS1)
            emit_ln(r4t, outstg, 1e-5, D, sq_ap=smv(10752, 1024), sq_w=1024)
            em.wait("gpsimd", sDV)
            em.dma("gpsimd", out_d[ts(tt), :], outstg, sS1)
            if debug:
                em.dma("gpsimd", dbg["dbg_moe"][ts(tt), :], r4t, sXD)
    em.flush()
    pctx.close()

    em.wait("gpsimd", sS1)
    em.wait("gpsimd", sS0)
    em.wait("gpsimd", sXD)
    em.wait("gpsimd", sCC)
    em.flush()
    em.ctx.close()
    return nc, dbg


def _stage(inputs):
    f = np.ascontiguousarray
    x = np.asarray(inputs["L0_attn_x"], dtype=np.float32)
    wkT = np.asarray(inputs["L0_attn_Wk"], np.float32).T
    wvT = np.asarray(inputs["L0_attn_Wv"], np.float32).T
    wrT = np.asarray(inputs["L0_attn_Wr"], np.float32).T
    woT = np.asarray(inputs["L0_attn_Wo"], np.float32).T
    blk = lambda w: f(w.reshape(D, NG, 128).transpose(1, 0, 2))
    wkB, wvB, wrB, woB = blk(wkT), blk(wvT), blk(wrT), blk(woT)
    col = lambda v: f(np.asarray(v, np.float32).reshape(D, 1))
    wfcT = f(np.asarray(inputs["L0_ffn_w_fc"], np.float32).T)
    wprojT = f(np.asarray(inputs["L0_ffn_w_proj"], np.float32).T)
    wqT = f(np.asarray(inputs["L1_attn_wq"], np.float32).T)
    wk1T = f(np.asarray(inputs["L1_attn_wk"], np.float32).T)
    wv1T = f(np.asarray(inputs["L1_attn_wv"], np.float32).T)
    wo1T = f(np.asarray(inputs["L1_attn_wo"], np.float32).T)
    router = f(np.asarray(inputs["L1_ffn_router_w"], np.float32))
    gate = np.asarray(inputs["L1_ffn_gate_proj"], np.float32)
    up = np.asarray(inputs["L1_ffn_up_proj"], np.float32)
    down = f(np.asarray(inputs["L1_ffn_down_proj"], np.float32))
    gateB = f(gate.reshape(NE, D, 8, 128).transpose(0, 2, 1, 3))
    upB = f(up.reshape(NE, D, 8, 128).transpose(0, 2, 1, 3))
    identb = np.eye(128, dtype=ml_dtypes.bfloat16)
    maskst = np.triu(np.ones((128, 128), np.float32)).astype(ml_dtypes.bfloat16)
    ones_col = np.ones((128, 1), np.float32)
    ones_row = np.ones((1, 128), np.float32)
    ones_sq = np.ones((128, 128), dtype=ml_dtypes.bfloat16)
    attn_scale = np.full((128, 1), 1.0 / np.sqrt(128.0), np.float32)

    in_maps = []
    for c in range(8):
        b, q = c // 4, c % 4
        t0 = q * TL
        xloc = np.zeros((D, TL + 1), np.float32)
        xs = x[b, max(t0 - 1, 0):t0 + TL, :]
        xloc[:, (1 if t0 == 0 else 0):] = xs.T
        cm = np.full((1, 4), 1e6, np.float32)
        for j in range(q):
            cm[0, j] = 1024.0 * (q - 1 - j)
        hsl = slice(c * 1024, (c + 1) * 1024)
        heads = slice((c % 4) * 512, (c % 4) * 512 + 512)
        im = {
            "x_f": f(xloc),
            "l0_wk": wkB, "l0_wv": wvB, "l0_wr": wrB, "l0_wo": woB,
            "l0_decay": col(inputs["L0_attn_time_decay"]),
            "l0_first": col(inputs["L0_attn_time_first"]),
            "l0_mixk": col(inputs["L0_attn_time_mix_k"]),
            "l0_mixv": col(inputs["L0_attn_time_mix_v"]),
            "l0_mixr": col(inputs["L0_attn_time_mix_r"]),
            "carry_mult": cm,
            "wfc_slice": f(wfcT[:, hsl]),
            "wproj_slice": f(wprojT[hsl, :]),
            "l1_wq": f(wqT[:, heads]), "l1_wk": f(wk1T[:, heads]),
            "l1_wv": f(wv1T[:, heads]), "l1_wo": f(wo1T[heads, :]),
            "router_w": router,
            "moe_gate": gateB, "moe_up": upB, "moe_down": down,
            "ident_bf": identb, "mask_st": maskst, "ones_col": ones_col,
            "ones_row": ones_row, "ones_sq": ones_sq, "attn_scale": attn_scale,
        }
        in_maps.append(im)
    return in_maps


def kernel(**inputs):
    nc, _ = build_program(upto=99, debug=False)
    in_maps = _stage(inputs)
    res = run_bass_kernel_spmd(nc, in_maps, core_ids=list(range(8)))
    shards = [res.results[c]["out"] for c in range(8)]
    out = np.stack(shards).reshape(BATCH, SEQ, D).astype(np.float32)
    return out


# version 03
# speedup vs baseline: 1.0120x; 1.0120x over previous
import contextlib
import numpy as np
import ml_dtypes
import concourse.bass as bass
import concourse.mybir as mybir
from concourse.bass_utils import run_bass_kernel_spmd

F32 = mybir.dt.float32
BF16 = mybir.dt.bfloat16
AX = mybir.AxisListType.X
OP = mybir.AluOpType
ACT = mybir.ActivationFunctionType

BATCH, SEQ, D = 2, 4096, 2048
TL = 1024
NG = D // 128
HID = 8192
NHC, HDIM = 4, 128
NE, FMOE = 8, 1024
GRPS = [[0, 1, 2, 3], [4, 5, 6, 7]]
ALL8 = [[0, 1, 2, 3, 4, 5, 6, 7]]
ENGS = ("sync", "tensor", "vector", "scalar", "gpsimd")

OX = 0
G = [16400, 32784, 49168, 65552]
ARENA_COLS = 81936


def ts(i, n=128):
    return slice(i * n, (i + 1) * n)


class Emitter:
    def __init__(self, nc):
        self.nc = nc
        self.q = {e: [] for e in ENGS}
        self.semv = {}
        self.ctx = contextlib.ExitStack()

    def sem(self, name):
        s = self.ctx.enter_context(self.nc.semaphore(name))
        self.semv[s] = 0
        return s

    def sbuf(self, name, shape, dtype):
        return self.ctx.enter_context(self.nc.sbuf_tensor(name, list(shape), dtype))

    def op(self, eng, fn, inc=None, incn=1):
        if inc is not None:
            self.semv[inc] += incn

            def g(e, fn=fn, inc=inc, incn=incn):
                fn(e).then_inc(inc, incn)
            self.q[eng].append(g)
        else:
            self.q[eng].append(fn)

    def dma(self, eng, out, in_, inc, **kw):
        self.op(eng, lambda e, o=out, i=in_, kw=kw: e.dma_start(out=o, in_=i, **kw),
                inc=inc, incn=16)

    def tiny(self, eng, fn, inc=None, incn=1):
        self.op(eng, fn, inc=inc, incn=incn)
        self.q[eng].append(lambda e: e.drain())

    def wait(self, eng, sem, val=None):
        v = self.semv[sem] if val is None else val
        self.q[eng].append(lambda e, s=sem, v=v: e.wait_ge(s, v))

    def val(self, sem):
        return self.semv[sem]

    def flush(self):
        blk = self.nc.Block()
        with blk:
            for name in ENGS:
                if self.q[name]:
                    fs = self.q[name]

                    def run(eng, fs=fs):
                        for f in fs:
                            f(eng)
                    getattr(blk, name)(run)
        self.q = {e: [] for e in ENGS}


def build_program(upto=99, debug=False):
    nc = bass.Bass("TRN2", target_bir_lowering=False)

    def din(name, shape, dtype=F32):
        return nc.dram_tensor(name, list(shape), dtype, kind="ExternalInput")

    def dint(name, shape, dtype):
        return nc.dram_tensor(name, list(shape), dtype, kind="Internal")

    x_in = din("x_f", [D, TL + 1])
    wk_in = din("l0_wk", [NG, D, 128])
    wv_in = din("l0_wv", [NG, D, 128])
    wr_in = din("l0_wr", [NG, D, 128])
    wo_in = din("l0_wo", [NG, D, 128])
    decay_in = din("l0_decay", [D, 1])
    first_in = din("l0_first", [D, 1])
    mixk_in = din("l0_mixk", [D, 1])
    mixv_in = din("l0_mixv", [D, 1])
    mixr_in = din("l0_mixr", [D, 1])
    cmult_in = din("carry_mult", [1, 4])
    wfc_in = din("wfc_slice", [D, 1024])
    wproj_in = din("wproj_slice", [1024, D])
    wq_in = din("l1_wq", [D, 512])
    wk1_in = din("l1_wk", [D, 512])
    wv1_in = din("l1_wv", [D, 512])
    wo1_in = din("l1_wo", [512, D])
    router_in = din("router_w", [D, NE])
    gate_in = din("moe_gate", [NE, 8, D, 128])
    up_in = din("moe_up", [NE, 8, D, 128])
    down_in = din("moe_down", [NE, FMOE, D])
    identb_in = din("ident_bf", [128, 128], BF16)
    maskst_in = din("mask_st", [128, 128], BF16)
    ones_in = din("ones_col", [128, 1])
    onesrow_in = din("ones_row", [1, 128])
    onessq_in = din("ones_sq", [128, 128], BF16)
    ascale_in = din("attn_scale", [128, 1])

    out_d = nc.dram_tensor("out", [TL, D], F32, kind="ExternalOutput")
    dbg = {}
    if debug:
        for name, shape, dt in [
            ("dbg_x1", [TL, D], BF16), ("dbg_x2", [TL, D], BF16),
            ("dbg_x3", [TL, D], BF16), ("dbg_srw", [D, TL], BF16),
            ("dbg_ek", [D, TL], BF16), ("dbg_r1", [D, TL], BF16),
            ("dbg_xhat", [TL, D], BF16), ("dbg_h", [TL, HID], BF16),
            ("dbg_attn", [TL, D], BF16), ("dbg_moe", [TL, D], BF16),
            ("dbg_r1tok", [TL, D], BF16),
        ]:
            dbg[name] = nc.dram_tensor(name, shape, dt, kind="ExternalOutput")

    tot_in_d = dint("wkv_tot_in", [D, 2], F32)
    tot_ag_d = dint("wkv_tot_ag", [4, D, 2], F32)
    gam_in_d = dint("gam_in", [1, 2], F32)
    gam_out_d = dint("gam_out", [1, 2], F32)
    qfc_in_d = dint("qfc_in", [D, 1024], BF16)
    qfc_d = dint("qfc_ag", [8, D, 1024], BF16)
    qproj_in_d = dint("qproj_in", [1024, D], BF16)
    qproj_d = dint("qproj_ag", [8, 1024, D], BF16)
    x2_in_d = dint("x2_in", [D, TL], BF16)
    x2_ag_d = dint("x2_ag", [4, D, TL], BF16)
    rs_in_d = dint("rs_in", [4, TL, D], BF16)
    rs_out_d = dint("rs_out", [TL, D], BF16)
    hhat_d = dint("hhat", [TL, HID], BF16)
    hf_d = dint("hf", [HID, TL], BF16)
    ffn_d = dint("ffn", [D, TL], BF16)
    downb_d = dint("down_bf", [NE, FMOE, D], BF16)
    x1_spill_d = dint("x1_spill", [TL, D], BF16)
    x2_spill_d = dint("x2_spill", [TL, D], BF16)
    x3_spill_d = dint("x3_spill", [TL, D], BF16)
    bc_d = dint("bc_scratch", [1, 16], F32)
    w8_d = dint("w8", [8, 1024], BF16)

    em = Emitter(nc)
    sLD = em.sem("sLD")
    sXD = em.sem("sXD")
    sTE = em.sem("sTE")
    sDV = em.sem("sDV")
    sSC = em.sem("sSC")
    sCC = em.sem("sCC")
    sPF = em.sem("sPF")
    sWA = em.sem("sWA")
    sWB = em.sem("sWB")
    sS0 = em.sem("sS0")
    sS1 = em.sem("sS1")
    sDC = em.sem("sDC")

    AR = em.sbuf("AR", [128, ARENA_COLS], BF16)
    ARFW = 3520
    ARF = em.sbuf("ARF", [128, ARFW], F32)
    SM = em.sbuf("SM", [128, 12288], BF16)
    w8row = em.sbuf("w8row", [1, 1024], BF16)
    w8rowb = em.sbuf("w8rowb", [1, 1024], BF16)

    identb = SM[:, 0:128]
    maskst = SM[:, 128:256]
    onessq = SM[:, 272:400]
    onescol_b = SM[:, 256:257]
    S0, S1, S2, SSP = 512, 2560, 4608, 6656

    def smv(a, n):
        return SM[:, a:a + n]

    def fv(a, n):
        return ARF[:, a:a + n]
    onescol = fv(3074, 1)
    vec_dec = fv(3076, NG)
    vec_w = fv(3092, NG)
    vec_lam = fv(3108, NG)
    vec_eu = fv(3124, NG)
    vec_mixk = fv(3140, NG)
    vec_mixv = fv(3156, NG)
    vec_mixr = fv(3172, NG)
    cmult = fv(3188, 4)
    scalars = fv(3192, 8)
    negthr = fv(3200, 2)
    stat = fv(3204, 16)
    tot_sb = fv(3220, 2 * NG)
    init_sb = fv(3252, 2 * NG)
    tots_g = fv(3284, 8)
    coef_g = fv(3292, 4)
    attC = fv(3296, 8)
    ascale = fv(3310, 1)
    attK = fv(3312, 8)
    onesrow = ARF[0:1, 3392:3520]
    lamfull = ARF[:, 0:TL]
    sa_full = ARF[:, TL:2 * TL + 1]
    sb_full = ARF[:, 2 * TL + 1:3 * TL + 2]
    wt0 = ARF[:, 0:1024]
    wt1 = ARF[:, 1024:2048]
    outstg = ARF[:, 0:2048]

    def fm(base, g, w=TL):
        return AR[:, base + g * w: base + g * w + w]

    def tmt(base, t):
        return AR[:, base + t * D: base + t * D + D]

    def xg(g, a, b):
        return AR[:, OX + g * (TL + 1) + a: OX + g * (TL + 1) + b]

    pctx = contextlib.ExitStack()
    ps1 = pctx.enter_context(nc.psum_tensor("p0_ps1", [1, 512], F32))
    psB16 = pctx.enter_context(nc.psum_tensor("p0_psB16", [128, 16], F32))

    for buf, src in [(identb, identb_in), (maskst, maskst_in), (onessq, onessq_in)]:
        em.dma("sync", buf, src[:, :], sLD)
    em.dma("sync", ascale, ascale_in[:, :], sLD)
    em.dma("sync", onescol, ones_in[:, :], sLD)
    em.dma("sync", onesrow, onesrow_in[:, :], sLD)
    for g in range(NG):
        em.dma("sync", vec_dec[:, g:g + 1], decay_in[ts(g), :], sLD)
        em.dma("sync", vec_eu[:, g:g + 1], first_in[ts(g), :], sLD)
        em.dma("sync", vec_mixk[:, g:g + 1], mixk_in[ts(g), :], sLD)
        em.dma("sync", vec_mixv[:, g:g + 1], mixv_in[ts(g), :], sLD)
        em.dma("sync", vec_mixr[:, g:g + 1], mixr_in[ts(g), :], sLD)
    em.dma("sync", cmult[:, :], bass.AP(cmult_in, 0, [[0, 128], [1, 4]]), sLD)
    for g in range(NG):
        em.dma("gpsimd", xg(g, 0, TL + 1), x_in[ts(g), :], sLD)
    em.wait("scalar", sLD)
    em.op("scalar", lambda e: e.activation(vec_w[:, :], vec_dec[:, :], ACT.Exp), inc=sSC)
    em.wait("vector", sSC)
    em.op("vector", lambda e: e.tensor_scalar_mul(vec_w[:, :], vec_w[:, :], -1.0), inc=sDV)
    em.wait("scalar", sDV)
    em.op("scalar", lambda e: e.activation(vec_lam[:, :], vec_w[:, :], ACT.Exp))
    em.op("scalar", lambda e: e.activation(vec_eu[:, :], vec_eu[:, :], ACT.Exp), inc=sSC)
    em.wait("vector", sSC)
    em.op("vector", lambda e: e.tensor_copy(onescol_b, onescol), inc=sDV)

    def wtile_src(i):
        if i < NG:
            return wfc_in[ts(i), :]
        j, half = divmod(i - NG, 2)
        return bass.AP(wproj_in, j * 128 * D + half * 1024, [[D, 128], [1, 1024]])

    wsem = [sWA, sWB]
    ld_marks = [None] * 33
    em.dma("sync", wt0[:, :], wtile_src(0), wsem[0])
    ld_marks[0] = em.val(wsem[0])
    for i in range(32):
        cur = wt0 if i % 2 == 0 else wt1
        if i + 1 < 32:
            em.dma("sync", (wt1 if i % 2 == 0 else wt0)[:, :], wtile_src(i + 1),
                   wsem[(i + 1) % 2])
            ld_marks[i + 1] = em.val(wsem[(i + 1) % 2])
        em.wait("vector", wsem[i % 2], ld_marks[i])
        em.tiny("vector", lambda e, cur=cur: e.tensor_reduce(
            stat[:, 1:2], cur[:, :], AX, OP.add, apply_absolute_value=True))
        dst = stat[:, 2:3] if i < NG else stat[:, 3:4]
        if i == 0 or i == NG:
            em.tiny("vector", lambda e, dst=dst: e.tensor_copy(dst, stat[:, 1:2]),
                    inc=sDV)
        else:
            em.tiny("vector", lambda e, dst=dst: e.tensor_add(dst, dst, stat[:, 1:2]),
                    inc=sDV)
        if i + 1 < 32:
            em.wait("sync", sDV)
    em.tiny("vector", lambda e: e.tensor_copy(stat[:, 4:6], stat[:, 2:4]), inc=sDV)
    em.wait("tensor", sDV)
    em.op("tensor", lambda e: e.matmul(ps1[0:1, 0:2], onescol, stat[:, 4:6],
                                       start=True, stop=True), inc=sTE)
    em.wait("vector", sTE)
    em.tiny("vector", lambda e: e.tensor_copy(stat[0:1, 6:8], ps1[0:1, 0:2]), inc=sDV)
    em.wait("gpsimd", sDV)
    em.dma("gpsimd", gam_in_d[:, :], stat[0:1, 6:8], sXD)
    em.wait("gpsimd", sXD)
    em.op("gpsimd", lambda e: e.collective_compute(
        "AllReduce", OP.add, replica_groups=ALL8,
        ins=[gam_in_d.ap().opt()], outs=[gam_out_d.ap().opt()]), inc=sCC)
    em.wait("sync", sCC)
    em.dma("sync", stat[0:1, 6:8], gam_out_d[:, :], sLD)
    em.wait("vector", sLD)
    NW = float(HID * D)
    em.tiny("vector", lambda e: e.tensor_scalar_mul(stat[0:1, 8:10], stat[0:1, 6:8],
                                                    1.0 / NW))
    em.tiny("vector", lambda e: e.tensor_scalar_add(stat[0:1, 10:12], stat[0:1, 8:10],
                                                    1e-6))
    em.tiny("vector", lambda e: e.tensor_scalar_mul(stat[0:1, 10:12], stat[0:1, 10:12],
                                                    0.5))
    em.tiny("vector", lambda e: e.tensor_scalar_mul(stat[0:1, 12:14], stat[0:1, 8:10],
                                                    0.5), inc=sDV)
    em.wait("tensor", sDV)
    em.op("tensor", lambda e: e.matmul(psB16[:, 0:4], onesrow, stat[0:1, 10:14],
                                       start=True, stop=True), inc=sTE)
    em.wait("vector", sTE)
    em.tiny("vector", lambda e: e.tensor_copy(scalars[:, 0:4], psB16[:, 0:4]), inc=sDV)
    em.tiny("vector", lambda e: e.tensor_scalar_mul(negthr[:, 0:2], scalars[:, 0:2],
                                                    -1.0), inc=sDV)
    em.wait("scalar", sDV)
    em.wait("sync", sDV)

    qa = AR[:, G[0]:G[0] + 1024]
    qb = AR[:, G[0] + 1024:G[0] + 2048]

    def qtile_dst(i):
        if i < NG:
            return qfc_in_d[ts(i), :]
        j, half = divmod(i - NG, 2)
        return bass.AP(qproj_in_d, j * 128 * D + half * 1024, [[D, 128], [1, 1024]])

    em.dma("sync", wt0[:, :], wtile_src(0), wsem[0])
    ld_marks[0] = em.val(wsem[0])
    for i in range(32):
        cur = wt0 if i % 2 == 0 else wt1
        if i + 1 < 32:
            em.dma("sync", (wt1 if i % 2 == 0 else wt0)[:, :], wtile_src(i + 1),
                   wsem[(i + 1) % 2])
            ld_marks[i + 1] = em.val(wsem[(i + 1) % 2])
        em.wait("scalar", wsem[i % 2], ld_marks[i])
        thr_pos = scalars[:, 0:1] if i < NG else scalars[:, 1:2]
        thr_neg = negthr[:, 0:1] if i < NG else negthr[:, 1:2]
        em.op("scalar", lambda e, cur=cur, b=thr_neg: e.activation(
            qa, cur[:, :], ACT.Sign, bias=b))
        em.op("scalar", lambda e, cur=cur, b=thr_pos: e.activation(
            qb, cur[:, :], ACT.Sign, bias=b), inc=sSC)
        em.wait("vector", sSC)
        em.op("vector", lambda e: e.tensor_add(qa, qa, qb), inc=sDV)
        em.wait("gpsimd", sDV)
        em.dma("gpsimd", qtile_dst(i), qa, sXD)
        if i + 1 < 32:
            em.wait("sync", sSC)
            em.wait("scalar", sXD)
    em.wait("gpsimd", sXD)
    em.op("gpsimd", lambda e: e.collective_compute(
        "AllGather", OP.bypass, replica_groups=ALL8,
        ins=[qfc_in_d.ap().opt()], outs=[qfc_d.ap().opt()]), inc=sCC)
    em.op("gpsimd", lambda e: e.collective_compute(
        "AllGather", OP.bypass, replica_groups=ALL8,
        ins=[qproj_in_d.ap().opt()], outs=[qproj_d.ap().opt()]), inc=sCC)
    em.flush()
    pctx.close()

    if upto < 1:
        em.ctx.close()
        return nc, dbg

    pctx = contextlib.ExitStack()
    ps4 = [pctx.enter_context(nc.psum_tensor(f"p1_ps{i}", [128, 512], F32))
           for i in range(4)]
    bankmark = {}

    def bank_wait(b, _unused=None):
        if b in bankmark:
            em.wait("tensor", bankmark[b][0], bankmark[b][1])

    MIX, EK, YV, SRW = G[0], G[1], G[2], G[3]
    numb = AR[:, MIX:MIX + TL]
    wbuf = [smv(S0, 2048), smv(S1, 2048)]
    wslot_sem = [sWA, sWB]
    wmarks = {}

    def prefetch_w(src, g, slot):
        em.wait("gpsimd", sTE)
        em.dma("gpsimd", bass.AP(SM, 512 + slot * 2048, [[12288, 128], [128, NG], [1, 128]]),
               bass.AP(src, g * D * 128, [[128, 128], [128 * 128, NG], [1, 128]]),
               wslot_sem[slot])
        wmarks[g] = (slot, em.val(wslot_sem[slot]))

    def emit_mix(mixvec):
        for g in range(NG):
            em.op("vector", lambda e, g=g: e.tensor_sub(
                fm(MIX, g), xg(g, 1, TL + 1), xg(g, 0, TL)))
            em.op("vector", lambda e, g=g, mv=mixvec: e.scalar_tensor_tensor(
                fm(MIX, g), fm(MIX, g), mv[:, g:g + 1], xg(g, 0, TL),
                OP.mult, OP.add), inc=sDV)

    def emit_kvr(kind, src_in):
        for g in range(NG):
            if g + 1 < NG:
                prefetch_w(src_in, g + 1, (g + 1) % 2)
            sl, mv = wmarks[g]
            em.wait("tensor", wslot_sem[sl], mv)
            wb = wbuf[g % 2]
            for th in range(2):
                b = (g % 2) * 2 + th
                ps = ps4[b]
                bank_wait(b, None)
                for c in range(NG):
                    em.op("tensor", lambda e, ps=ps, wb=wb, c=c, th=th: e.matmul(
                        ps[:, :], wb[:, ts(c)], fm(MIX, c)[:, ts(th, 512)],
                        start=(c == 0), stop=(c == NG - 1)),
                        inc=(sTE if c == NG - 1 else None))
                tev = em.val(sTE)
                if kind == "k":
                    em.wait("scalar", sTE, tev)
                    em.op("scalar", lambda e, ps=ps, g=g, th=th: e.activation(
                        fm(EK, g)[:, ts(th, 512)], ps[:, :], ACT.Exp), inc=sSC)
                    bankmark[b] = (sSC, em.val(sSC))
                elif kind == "v":
                    em.wait("vector", sTE, tev)
                    em.op("vector", lambda e, ps=ps, g=g, th=th: e.tensor_mul(
                        fm(YV, g)[:, ts(th, 512)], fm(EK, g)[:, ts(th, 512)], ps[:, :]),
                        inc=sDV)
                    bankmark[b] = (sDV, em.val(sDV))
                else:
                    em.wait("scalar", sTE, tev)
                    em.op("scalar", lambda e, ps=ps, g=g, th=th: e.activation(
                        fm(SRW, g)[:, ts(th, 512)], ps[:, :], ACT.Sigmoid), inc=sSC)
                    bankmark[b] = (sSC, em.val(sSC))

    prefetch_w(wk_in, 0, 0)
    emit_mix(vec_mixk)
    em.wait("tensor", sDV)
    emit_kvr("k", wk_in)
    prefetch_w(wv_in, 0, 0)
    em.wait("vector", sTE)
    emit_mix(vec_mixv)
    em.wait("tensor", sDV)
    em.wait("vector", sSC)
    emit_kvr("v", wv_in)
    prefetch_w(wr_in, 0, 0)
    em.wait("vector", sTE)
    emit_mix(vec_mixr)
    em.wait("tensor", sDV)
    emit_kvr("r", wr_in)
    em.flush()

    def lam_bcast(g):
        return bass.AP(ARF, 3108 + g, [[ARFW, 128], [0, TL]])

    for g in range(NG):
        em.op("vector", lambda e, g=g: e.tensor_copy(lamfull, lam_bcast(g)))
        em.tiny("vector", lambda e, g=g: e.tensor_tensor_scan(
            sa_full[:, 1:TL + 1], lamfull, fm(YV, g), 0.0, OP.mult, OP.add))
        em.tiny("vector", lambda e, g=g: e.tensor_copy(tot_sb[:, 2 * g:2 * g + 1],
                                                       sa_full[:, TL:TL + 1]))
        em.tiny("vector", lambda e, g=g: e.tensor_tensor_scan(
            sa_full[:, 1:TL + 1], lamfull, fm(EK, g), 0.0, OP.mult, OP.add))
        em.tiny("vector", lambda e, g=g: e.tensor_copy(tot_sb[:, 2 * g + 1:2 * g + 2],
                                                       sa_full[:, TL:TL + 1]), inc=sDV)
    em.wait("gpsimd", sDV)
    em.op("gpsimd", lambda e: e.dma_start(
        out=bass.AP(tot_in_d, 0, [[2, 128], [256, NG], [1, 2]]),
        in_=bass.AP(ARF, 3220, [[ARFW, 128], [2, NG], [1, 2]])), inc=sXD, incn=16)
    em.wait("gpsimd", sXD)
    em.op("gpsimd", lambda e: e.collective_compute(
        "AllGather", OP.bypass, replica_groups=GRPS,
        ins=[tot_in_d.ap().opt()], outs=[tot_ag_d.ap().opt()]), inc=sCC)
    em.wait("sync", sCC)
    for g in range(NG):
        em.dma("sync", tots_g[:, :],
               bass.AP(tot_ag_d, g * 256, [[2, 128], [D * 2, 4], [1, 2]]), sLD)
        em.wait("scalar", sLD)
        for j in range(4):
            em.op("scalar", lambda e, g=g, j=j: e.activation(
                coef_g[:, j:j + 1], vec_w[:, g:g + 1], ACT.Exp,
                scale=cmult[:, j:j + 1]), inc=(sSC if j == 3 else None))
        em.wait("vector", sSC)
        em.tiny("vector", lambda e, g=g: e.tensor_mul(
            init_sb[:, 2 * g:2 * g + 2], tots_g[:, 0:2],
            bass.AP(ARF, 3292, [[ARFW, 128], [0, 2]])))
        for j in range(1, 4):
            em.tiny("vector", lambda e, g=g, j=j: e.scalar_tensor_tensor(
                init_sb[:, 2 * g:2 * g + 2], tots_g[:, 2 * j:2 * j + 2],
                coef_g[:, j:j + 1], init_sb[:, 2 * g:2 * g + 2], OP.mult, OP.add),
                inc=(sDV if j == 3 else None))
        em.wait("sync", sDV)
    em.flush()

    for g in range(NG):
        em.op("vector", lambda e, g=g: e.tensor_copy(lamfull, lam_bcast(g)))
        em.tiny("vector", lambda e, g=g: e.tensor_copy(sa_full[:, 0:1],
                                                       init_sb[:, 2 * g:2 * g + 1]))
        em.tiny("vector", lambda e, g=g: e.tensor_copy(sb_full[:, 0:1],
                                                       init_sb[:, 2 * g + 1:2 * g + 2]))
        em.op("vector", lambda e, g=g: e.tensor_tensor_scan(
            sa_full[:, 1:TL + 1], lamfull, fm(YV, g),
            init_sb[:, 2 * g:2 * g + 1], OP.mult, OP.add))
        em.tiny("vector", lambda e, g=g: e.tensor_tensor_scan(
            sb_full[:, 1:TL + 1], lamfull, fm(EK, g),
            init_sb[:, 2 * g + 1:2 * g + 2], OP.mult, OP.add))
        em.op("vector", lambda e, g=g: e.scalar_tensor_tensor(
            numb, fm(YV, g), vec_eu[:, g:g + 1], sa_full[:, 0:TL], OP.mult, OP.add))
        em.op("vector", lambda e, g=g: e.scalar_tensor_tensor(
            sa_full[:, 0:TL], fm(EK, g), vec_eu[:, g:g + 1], sb_full[:, 0:TL],
            OP.mult, OP.add))
        em.op("vector", lambda e: e.reciprocal(sa_full[:, 0:TL], sa_full[:, 0:TL]))
        em.op("vector", lambda e, g=g: e.tensor_mul(fm(YV, g), numb, sa_full[:, 0:TL]))
        em.op("vector", lambda e, g=g: e.tensor_mul(fm(SRW, g), fm(SRW, g), fm(YV, g)),
              inc=sDV)
    em.flush()

    if debug:
        dbg["dbg_init"] = nc.dram_tensor("dbg_init", [128, 2 * NG], F32,
                                         kind="ExternalOutput")
        dbg["dbg_tots"] = nc.dram_tensor("dbg_tots", [4, D, 2], F32,
                                         kind="ExternalOutput")
        for g in range(NG):
            em.dma("gpsimd", dbg["dbg_ek"][ts(g), :], fm(EK, g), sXD)
            em.dma("gpsimd", dbg["dbg_srw"][ts(g), :], fm(SRW, g), sXD)
        em.dma("gpsimd", dbg["dbg_init"][:, :], init_sb[:, :], sXD)
        em.dma("gpsimd", dbg["dbg_tots"][:, :, :], tot_ag_d[:, :, :], sXD)
        em.flush()

    prefetch_w(wo_in, 0, 0)
    for c in range(NG):
        if c + 1 < NG:
            prefetch_w(wo_in, c + 1, (c + 1) % 2)
        sl, mv = wmarks[c]
        em.wait("tensor", wslot_sem[sl], mv)
        wb = wbuf[c % 2]
        for th in range(2):
            b = (c % 2) * 2 + th
            ps = ps4[b]
            bank_wait(b, None)
            for dd in range(NG):
                em.op("tensor", lambda e, ps=ps, wb=wb, dd=dd, th=th: e.matmul(
                    ps[:, :], wb[:, ts(dd)], fm(SRW, dd)[:, ts(th, 512)],
                    start=(dd == 0), stop=(dd == NG - 1)),
                    inc=(sTE if dd == NG - 1 else None))
            em.wait("vector", sTE)
            em.op("vector", lambda e, ps=ps, c=c, th=th: e.tensor_add(
                xg(c, 1 + th * 512, 1 + th * 512 + 512), ps[:, :],
                xg(c, 1 + th * 512, 1 + th * 512 + 512)), inc=sDV)
            bankmark[b] = (sDV, em.val(sDV))
    em.flush()
    if debug:
        for g in range(NG):
            em.dma("gpsimd", dbg["dbg_r1"][ts(g), :], xg(g, 1, TL + 1), sXD)
        em.flush()
    pctx.close()

    pctx = contextlib.ExitStack()
    psT = [pctx.enter_context(nc.psum_tensor(f"tr_psT{i}", [128, 128], BF16))
           for i in range(4)]
    R1TOK, X1TOK, XNT = G[0], G[1], G[2]

    def emit_transpose(src_tile_fn, dst_tile_fn, n_i, n_j):
        cnt = 0
        for i in range(n_i):
            for j in range(n_j):
                pst = psT[cnt % 4]
                em.op("tensor", lambda e, i=i, j=j, pst=pst: e.transpose(
                    pst[:, :], src_tile_fn(i, j), identb), inc=sTE)
                eng = "vector" if cnt % 2 == 0 else "scalar"
                em.wait(eng, sTE)
                if eng == "vector":
                    em.op(eng, lambda e, i=i, j=j, pst=pst: e.tensor_copy(
                        dst_tile_fn(i, j), pst[:, :]), inc=sDV)
                else:
                    em.op(eng, lambda e, i=i, j=j, pst=pst: e.copy(
                        dst_tile_fn(i, j), pst[:, :]), inc=sSC)
                cnt += 1
                if cnt % 4 == 0:
                    em.wait("tensor", sDV)
                    em.wait("tensor", sSC)
        em.wait("tensor", sDV)
        em.wait("tensor", sSC)

    emit_transpose(lambda i, j: xg(i, 1 + j * 128, 1 + (j + 1) * 128),
                   lambda i, j: tmt(R1TOK, j)[:, ts(i)], NG, 8)
    em.flush()
    if debug:
        for tt in range(8):
            em.dma("gpsimd", dbg["dbg_r1tok"][ts(tt), :], tmt(R1TOK, tt), sXD)
        em.flush()

    sqscr = smv(S1, 2048)

    def emit_ln(src_ap, dst_ap, eps, ln_dim, dst2_ap=None, eps2=None, sq_w=2048,
                sq_ap=None):
        sqd = sqscr if sq_ap is None else sq_ap
        em.wait("scalar", sDV)
        em.tiny("vector", lambda e: e.tensor_reduce(stat[:, 0:1], src_ap, AX, OP.add),
                inc=sDV)
        nchunk = ln_dim // sq_w
        for ci in range(nchunk):
            em.op("scalar", lambda e, ci=ci: e.activation(
                sqd[:, 0:sq_w], src_ap[:, ci * sq_w:(ci + 1) * sq_w],
                ACT.Square, accum_out=stat[:, 8 + ci:9 + ci]), inc=sSC)
        em.wait("vector", sSC)
        for ci in range(1, nchunk):
            em.tiny("vector", lambda e, ci=ci: e.tensor_add(
                stat[:, 8:9], stat[:, 8:9], stat[:, 8 + ci:9 + ci]))
        em.tiny("vector", lambda e: e.tensor_scalar_mul(stat[:, 2:3], stat[:, 0:1],
                                                        1.0 / ln_dim))
        em.tiny("vector", lambda e: e.tensor_scalar_mul(stat[:, 1:2], stat[:, 8:9],
                                                        1.0 / ln_dim))
        em.tiny("vector", lambda e: e.tensor_mul(stat[:, 3:4], stat[:, 2:3],
                                                 stat[:, 2:3]))
        em.tiny("vector", lambda e: e.tensor_sub(stat[:, 3:4], stat[:, 1:2],
                                                 stat[:, 3:4]), inc=sDV)
        em.tiny("vector", lambda e, eps=eps: e.tensor_scalar_add(stat[:, 9:10],
                                                                 stat[:, 3:4], eps),
                inc=sDV)
        em.wait("scalar", sDV)
        em.op("scalar", lambda e: e.activation(stat[:, 4:5], stat[:, 9:10], ACT.Ln),
              inc=sSC)
        em.wait("vector", sSC)
        em.tiny("vector", lambda e: e.tensor_scalar_mul(stat[:, 4:5], stat[:, 4:5],
                                                        -0.5), inc=sDV)
        em.wait("scalar", sDV)
        em.op("scalar", lambda e: e.activation(stat[:, 4:5], stat[:, 4:5], ACT.Exp),
              inc=sSC)
        em.wait("vector", sSC)
        em.op("vector", lambda e: e.tensor_single_scalar(dst_ap, src_ap, stat[:, 2:3],
                                                         OP.subtract))
        em.op("vector", lambda e: e.tensor_single_scalar(dst_ap, dst_ap, stat[:, 4:5],
                                                         OP.mult), inc=sDV)
        if dst2_ap is not None:
            em.tiny("vector", lambda e, eps=eps: e.tensor_scalar_add(stat[:, 6:7],
                                                                     stat[:, 3:4], eps))
            em.tiny("vector", lambda e: e.reciprocal(stat[:, 6:7], stat[:, 6:7]))
            em.tiny("vector", lambda e: e.tensor_mul(stat[:, 6:7], stat[:, 3:4],
                                                     stat[:, 6:7]))
            em.tiny("vector", lambda e, eps2=eps2: e.tensor_scalar_add(
                stat[:, 6:7], stat[:, 6:7], eps2), inc=sDV)
            em.wait("scalar", sDV)
            em.op("scalar", lambda e: e.activation(stat[:, 6:7], stat[:, 6:7], ACT.Ln),
                  inc=sSC)
            em.wait("vector", sSC)
            em.tiny("vector", lambda e: e.tensor_scalar_mul(stat[:, 6:7], stat[:, 6:7],
                                                            -0.5), inc=sDV)
            em.wait("scalar", sDV)
            em.op("scalar", lambda e: e.activation(stat[:, 6:7], stat[:, 6:7], ACT.Exp),
                  inc=sSC)
            em.wait("vector", sSC)
            em.tiny("vector", lambda e: e.tensor_mul(stat[:, 6:7], stat[:, 4:5],
                                                     stat[:, 6:7]))
            em.op("vector", lambda e: e.tensor_single_scalar(dst2_ap, src_ap,
                                                             stat[:, 2:3], OP.subtract))
            em.op("vector", lambda e: e.tensor_single_scalar(dst2_ap, dst2_ap,
                                                             stat[:, 6:7], OP.mult),
                  inc=sDV)

    for tt in range(8):
        emit_ln(tmt(R1TOK, tt), tmt(X1TOK, tt), 1e-5, D,
                dst2_ap=tmt(XNT, tt), eps2=1e-6)
        if debug and tt == 3:
            if "dbg_stat" not in dbg:
                dbg["dbg_stat"] = nc.dram_tensor("dbg_stat", [128, 16], F32,
                                                 kind="ExternalOutput")
            em.wait("gpsimd", sDV)
            em.wait("gpsimd", sSC)
            em.dma("gpsimd", dbg["dbg_stat"][:, :], stat[:, :], sXD)
            em.wait("vector", sXD)
            em.wait("scalar", sXD)
    em.flush()
    if debug:
        for tt in range(8):
            em.dma("gpsimd", dbg["dbg_x1"][ts(tt), :], tmt(X1TOK, tt), sXD)
        em.flush()
    pctx.close()

    if upto < 2:
        em.wait("gpsimd", sXD)
        em.wait("gpsimd", sCC)
        em.flush()
        em.ctx.close()
        return nc, dbg

    pctx = contextlib.ExitStack()
    psT2 = [pctx.enter_context(nc.psum_tensor(f"p2_psT{i}", [128, 128], BF16))
            for i in range(4)]
    XHF = G[3]

    for tt in range(8):
        srcx = tmt(XNT, tt)
        em.tiny("vector", lambda e, s=srcx: e.tensor_reduce(
            stat[:, 0:1], s, AX, OP.max, apply_absolute_value=True))
        em.tiny("vector", lambda e: e.tensor_scalar_mul(
            stat[:, 1:2], stat[:, 0:1], (128.0 - 1e-6) / 128.0))
        em.tiny("vector", lambda e: e.tensor_scalar_mul(stat[:, 2:3], stat[:, 1:2],
                                                        -1.0))
        em.op("vector", lambda e, s=srcx: e.tensor_single_scalar(
            s, s, stat[:, 1:2], OP.min))
        em.op("vector", lambda e, s=srcx: e.tensor_single_scalar(
            s, s, stat[:, 2:3], OP.max), inc=sDV)
    em.flush()
    if debug:
        for tt in range(8):
            em.dma("gpsimd", dbg["dbg_xhat"][ts(tt), :], tmt(XNT, tt), sXD)
        em.flush()

    cnt2 = [0]
    for jt in range(8):
        for ic in range(NG):
            pst = psT2[cnt2[0] % 4]
            em.op("tensor", lambda e, jt=jt, ic=ic, pst=pst: e.transpose(
                pst[:, :], tmt(XNT, jt)[:, ts(ic)], identb), inc=sTE)
            eng = "vector" if cnt2[0] % 2 == 0 else "scalar"
            em.wait(eng, sTE)
            if eng == "vector":
                em.op(eng, lambda e, jt=jt, ic=ic, pst=pst: e.tensor_copy(
                    fm(XHF, ic)[:, ts(jt)], pst[:, :]), inc=sDV)
            else:
                em.op(eng, lambda e, jt=jt, ic=ic, pst=pst: e.copy(
                    fm(XHF, ic)[:, ts(jt)], pst[:, :]), inc=sSC)
            cnt2[0] += 1
            if cnt2[0] % 4 == 0:
                em.wait("tensor", sDV)
                em.wait("tensor", sSC)
    em.wait("tensor", sDV)
    em.wait("tensor", sSC)
    em.flush()
    pctx.close()

    pctx = contextlib.ExitStack()
    psF = [pctx.enter_context(nc.psum_tensor(f"fc_ps{i}", [128, 512], F32))
           for i in range(4)]
    fcmark = {}
    em.wait("gpsimd", sCC)
    wtile = [smv(S0, 128), smv(S0 + 128, 128)]
    hrow = [smv(S1, 1024), smv(S1 + 1024, 1024)]
    wm = {}

    def load_qfc_tile(hc, c, slot):
        sh, off = divmod(hc * 128, 1024)
        em.dma("gpsimd", wtile[slot],
               bass.AP(qfc_d, sh * D * 1024 + c * 128 * 1024 + off,
                       [[1024, 128], [1, 128]]), wslot_sem[slot])
        wm[(hc, c)] = (slot, em.val(wslot_sem[slot]))

    nseq = [0]
    load_qfc_tile(0, 0, 0)
    for hc in range(64):
        for c in range(NG):
            idx = nseq[0]
            slot = idx % 2
            if idx + 1 < 64 * NG:
                hn, cn = divmod(idx + 1, NG)
                em.wait("gpsimd", sTE)
                load_qfc_tile(hn, cn, 1 - slot)
            sl, mv = wm[(hc, c)]
            em.wait("tensor", wslot_sem[sl], mv)
            for th in range(2):
                b = (hc % 2) * 2 + th
                if c == 0 and b in fcmark:
                    em.wait("tensor", fcmark[b][0], fcmark[b][1])
                em.op("tensor", lambda e, ps=psF[(hc % 2) * 2 + th], c=c, th=th,
                      sl=sl: e.matmul(
                    ps[:, :], wtile[sl], fm(XHF, c)[:, th * 512:(th + 1) * 512],
                    start=(c == 0), stop=(c == NG - 1)),
                    inc=(sTE if th == 1 else None))
            nseq[0] += 1
        tev = em.val(sTE)
        hslot = hc % 2
        ssem = sS0 if hslot == 0 else sS1
        em.wait("scalar", sTE, tev)
        em.wait("scalar", ssem)
        for th in range(2):
            b = (hc % 2) * 2 + th
            em.op("scalar", lambda e, ps=psF[(hc % 2) * 2 + th], th=th,
                  hs=hrow[hslot]: e.activation(
                hs[:, th * 512:(th + 1) * 512], ps[:, :], ACT.Gelu,
                scale=scalars[:, 2:3]), inc=sSC)
            fcmark[b] = (sSC, em.val(sSC))
        em.wait("gpsimd", sSC)
        em.dma("gpsimd", hf_d[ts(hc), :], hrow[hslot], ssem)

    em.flush()
    pctx.close()

    if debug:
        dbg["dbg_hfr"] = nc.dram_tensor("dbg_hfr", [HID, TL], BF16,
                                        kind="ExternalOutput")
        dbg["dbg_qfc"] = nc.dram_tensor("dbg_qfc", [D, 1024], BF16,
                                        kind="ExternalOutput")
        dbg["dbg_scal"] = nc.dram_tensor("dbg_scal", [128, 8], F32,
                                         kind="ExternalOutput")
        em.wait("gpsimd", sS0)
        em.wait("gpsimd", sS1)
        em.dma("gpsimd", dbg["dbg_hfr"][:, :], hf_d[:, :], sXD)
        em.dma("gpsimd", dbg["dbg_qfc"][:, :], qfc_in_d[:, :], sXD)
        em.dma("gpsimd", dbg["dbg_scal"][:, :], scalars[:, :], sXD)
        em.flush()

    em.wait("sync", sS0)
    em.wait("sync", sS1)
    htok = [AR[:, G[0]:G[0] + HID], AR[:, G[0] + HID:G[0] + 2 * HID]]
    hm = []
    for tt in range(8):
        em.wait("sync", sS0 if tt % 2 == 0 else sS1)
        em.dma("sync", htok[tt % 2], bass.AP(hf_d, tt * 128, [[TL, HID], [1, 128]]),
               wslot_sem[tt % 2], transpose=True)
        hm.append(em.val(wslot_sem[tt % 2]))
        em.wait("vector", wslot_sem[tt % 2], hm[tt])
        em.wait("scalar", wslot_sem[tt % 2], hm[tt])
        emit_ln(htok[tt % 2], htok[tt % 2], 1e-6, HID)
        em.wait("vector", sSC)
        em.tiny("vector", lambda e, s=htok[tt % 2]: e.tensor_reduce(
            stat[:, 0:1], s, AX, OP.max, apply_absolute_value=True))
        em.tiny("vector", lambda e: e.tensor_scalar_mul(
            stat[:, 1:2], stat[:, 0:1], (128.0 - 1e-6) / 128.0))
        em.tiny("vector", lambda e: e.tensor_scalar_mul(stat[:, 2:3], stat[:, 1:2],
                                                        -1.0))
        em.op("vector", lambda e, s=htok[tt % 2]: e.tensor_single_scalar(
            s, s, stat[:, 1:2], OP.min))
        em.op("vector", lambda e, s=htok[tt % 2]: e.tensor_single_scalar(
            s, s, stat[:, 2:3], OP.max), inc=sDV)
        em.wait("gpsimd", sDV)
        em.dma("gpsimd", hhat_d[ts(tt), :], htok[tt % 2], sS0 if tt % 2 == 0 else sS1)
        if debug:
            em.dma("gpsimd", dbg["dbg_h"][ts(tt), :], htok[tt % 2], sXD)
        if tt + 1 < 8:
            em.wait("sync", sDV)
    em.flush()

    pctx = contextlib.ExitStack()
    psP = [pctx.enter_context(nc.psum_tensor(f"pj_ps{i}", [128, 512], F32))
           for i in range(4)]
    pjmark = {}
    em.wait("sync", sS0)
    em.wait("sync", sS1)

    def hfv(hc):
        base = [G[0], G[2], G[3], OX][hc // 16]
        return AR[:, base + (hc % 16) * 1024: base + (hc % 16) * 1024 + 1024]

    hfm = []
    for hc in range(64):
        em.dma("sync", hfv(hc), bass.AP(hhat_d, hc * 128, [[HID, TL], [1, 128]]),
               sPF, transpose=True)
        hfm.append(em.val(sPF))
    frow = [smv(S1, 1024), smv(S1 + 1024, 1024)]
    wm2 = {}

    def load_qproj_tile(dc, hc, slot):
        sh, off = divmod(hc * 128, 1024)
        em.dma("gpsimd", wtile[slot],
               bass.AP(qproj_d, sh * 1024 * D + off * D + dc * 128,
                       [[D, 128], [1, 128]]), wslot_sem[slot])
        wm2[(dc, hc)] = (slot, em.val(wslot_sem[slot]))

    nseq = [0]
    load_qproj_tile(0, 0, 0)
    for dc in range(NG):
        for hc in range(64):
            idx = nseq[0]
            slot = idx % 2
            if idx + 1 < NG * 64:
                dn, hn = divmod(idx + 1, 64)
                em.wait("gpsimd", sTE)
                load_qproj_tile(dn, hn, 1 - slot)
            sl, mv = wm2[(dc, hc)]
            em.wait("tensor", wslot_sem[sl], mv)
            if dc == 0 and hc == 0:
                em.wait("tensor", sPF, hfm[63])
            for th in range(2):
                b = (dc % 2) * 2 + th
                if hc == 0 and b in pjmark:
                    em.wait("tensor", pjmark[b][0], pjmark[b][1])
                em.op("tensor", lambda e, ps=psP[(dc % 2) * 2 + th], th=th, sl=sl,
                      hc=hc: e.matmul(
                    ps[:, :], wtile[sl], hfv(hc)[:, th * 512:(th + 1) * 512],
                    start=(hc == 0), stop=(hc == 63)),
                    inc=(sTE if th == 1 else None))
            nseq[0] += 1
        tev = em.val(sTE)
        fslot = dc % 2
        fsem = sS0 if fslot == 0 else sS1
        em.wait("vector", sTE, tev)
        em.wait("vector", fsem)
        for th in range(2):
            b = (dc % 2) * 2 + th
            em.op("vector", lambda e, ps=psP[(dc % 2) * 2 + th], th=th,
                  fs=frow[fslot]: e.tensor_copy(
                fs[:, th * 512:(th + 1) * 512], ps[:, :]), inc=sDV)
            pjmark[b] = (sDV, em.val(sDV))
        em.wait("gpsimd", sDV)
        em.dma("gpsimd", ffn_d[ts(dc), :], frow[fslot], fsem)
    em.flush()
    pctx.close()

    if debug:
        dbg["dbg_ffn"] = nc.dram_tensor("dbg_ffn", [D, TL], BF16,
                                        kind="ExternalOutput")
        em.wait("gpsimd", sS0)
        em.wait("gpsimd", sS1)
        em.dma("gpsimd", dbg["dbg_ffn"][:, :], ffn_d[:, :], sXD)
        em.flush()

    X2T = G[0]
    em.wait("sync", sS0)
    em.wait("sync", sS1)
    f2m = []
    ftok = [AR[:, G[2]:G[2] + 2048], AR[:, G[2] + 2048:G[2] + 4096]]
    for tt in range(8):
        em.dma("sync", ftok[tt % 2], bass.AP(ffn_d, tt * 128, [[TL, D], [1, 128]]),
               wslot_sem[tt % 2], transpose=True)
        f2m.append(em.val(wslot_sem[tt % 2]))
        em.wait("vector", wslot_sem[tt % 2], f2m[tt])
        em.op("vector", lambda e, tt=tt: e.scalar_tensor_tensor(
            tmt(X2T, tt), ftok[tt % 2], scalars[:, 3:4], tmt(X1TOK, tt),
            OP.mult, OP.add), inc=sDV)
        emit_ln(tmt(X2T, tt), tmt(X2T, tt), 1e-5, D)
        if tt + 1 < 8:
            em.wait("sync", sDV)
    em.flush()
    if debug:
        for tt in range(8):
            em.dma("gpsimd", dbg["dbg_x2"][ts(tt), :], tmt(X2T, tt), sXD)
        em.flush()

    if upto < 3:
        em.wait("gpsimd", sXD)
        em.wait("gpsimd", sCC)
        em.flush()
        em.ctx.close()
        return nc, dbg

    pctx = contextlib.ExitStack()
    psT3 = [pctx.enter_context(nc.psum_tensor(f"p3_psT{i}", [128, 128], BF16))
            for i in range(4)]
    X2F = G[1]
    cnt3 = [0]
    for jt in range(8):
        for ic in range(NG):
            pst = psT3[cnt3[0] % 4]
            em.op("tensor", lambda e, jt=jt, ic=ic, pst=pst: e.transpose(
                pst[:, :], tmt(X2T, jt)[:, ts(ic)], identb), inc=sTE)
            eng = "vector" if cnt3[0] % 2 == 0 else "scalar"
            em.wait(eng, sTE)
            if eng == "vector":
                em.op(eng, lambda e, jt=jt, ic=ic, pst=pst: e.tensor_copy(
                    fm(X2F, ic)[:, ts(jt)], pst[:, :]), inc=sDV)
            else:
                em.op(eng, lambda e, jt=jt, ic=ic, pst=pst: e.copy(
                    fm(X2F, ic)[:, ts(jt)], pst[:, :]), inc=sSC)
            cnt3[0] += 1
            if cnt3[0] % 4 == 0:
                em.wait("tensor", sDV)
                em.wait("tensor", sSC)
    em.wait("tensor", sDV)
    em.wait("tensor", sSC)
    em.wait("gpsimd", sDV)
    em.wait("gpsimd", sSC)
    for g in range(NG):
        em.dma("gpsimd", x2_in_d[ts(g), :], fm(X2F, g), sXD)
    em.wait("gpsimd", sXD)
    em.op("gpsimd", lambda e: e.collective_compute(
        "AllGather", OP.bypass, replica_groups=GRPS,
        ins=[x2_in_d.ap().opt()], outs=[x2_ag_d.ap().opt()]), inc=sCC)
    em.flush()
    pctx.close()

    QF, KF, VA, SLAB = G[2], G[2] + 4096, G[2] + 8192, G[2] + 12320
    NKB = SEQ // 128
    whead = [smv(S0, 2048), smv(S1, 2048)]

    def p3_loadw(src_in, h, slot):
        em.wait("gpsimd", sTE)
        em.dma("gpsimd", bass.AP(SM, 512 + slot * 2048,
                                 [[12288, 128], [128, NG], [1, 128]]),
               bass.AP(src_in, h * 128, [[512, 128], [512 * 128, NG], [1, 128]]),
               wslot_sem[slot])
        return (slot, em.val(wslot_sem[slot]))

    pctx = contextlib.ExitStack()
    psQ = [pctx.enter_context(nc.psum_tensor(f"qk_ps{i}", [128, 512], F32))
           for i in range(2)]
    psVt = [psQ[0][:, 0:128], psQ[1][:, 0:128]]
    psO = [pctx.enter_context(nc.psum_tensor(f"o_ps{i}", [128, 129], F32))
           for i in range(4)]
    psN = psQ[0][:, 0:512]
    psT3b = [pctx.enter_context(nc.psum_tensor("p3_ot", [128, 128], BF16))]
    xbm = {}

    def load_x2blk(sp, slot):
        j, half = divmod(sp, 2)
        em.wait("sync", sTE)
        em.dma("sync", bass.AP(AR, G[1] + slot * 8192,
                               [[ARENA_COLS, 128], [512, NG], [1, 512]]),
               bass.AP(x2_ag_d, j * D * TL + half * 512,
                       [[TL, 128], [128 * TL, NG], [1, 512]]), wslot_sem[slot])
        xbm[sp] = (slot, em.val(wslot_sem[slot]))

    def x2blk_tile(slot, c):
        return AR[:, G[1] + slot * 8192 + c * 512: G[1] + slot * 8192 + (c + 1) * 512]

    em.wait("sync", sCC)
    for h in range(NHC):
        wq_m = p3_loadw(wq_in, h, 0)
        wk_m = p3_loadw(wk1_in, h, 1)
        em.op("vector", lambda e: e.tensor_copy(
            bass.AP(AR, VA + 128, [[ARENA_COLS, 128], [129, 32]]),
            bass.AP(SM, 256, [[12288, 128], [0, 32]])), inc=sDV)
        load_x2blk(0, 0)
        for sp in range(8):
            if sp + 1 < 8:
                load_x2blk(sp + 1, (sp + 1) % 2)
            xsl, xmv = xbm[sp]
            em.wait("tensor", wslot_sem[xsl], xmv)
            em.wait("tensor", wslot_sem[wq_m[0]], wq_m[1])
            em.wait("tensor", wslot_sem[wk_m[0]], wk_m[1])
            for c in range(NG):
                em.op("tensor", lambda e, c=c, xsl=xsl: e.matmul(
                    psQ[0][:, :], whead[0][:, ts(c)], x2blk_tile(xsl, c),
                    start=(c == 0), stop=(c == NG - 1)), inc=sTE)
            tevq = em.val(sTE)
            for c in range(NG):
                em.op("tensor", lambda e, c=c, xsl=xsl: e.matmul(
                    psQ[1][:, :], whead[1][:, ts(c)], x2blk_tile(xsl, c),
                    start=(c == 0), stop=(c == NG - 1)), inc=sTE)
            tevk = em.val(sTE)
            em.wait("vector", sTE, tevq)
            em.op("vector", lambda e, sp=sp: e.tensor_copy(
                AR[:, QF + sp * 512: QF + (sp + 1) * 512], psQ[0][:, :]), inc=sDV)
            em.wait("scalar", sTE, tevk)
            em.op("scalar", lambda e, sp=sp: e.copy(
                AR[:, KF + sp * 512: KF + (sp + 1) * 512], psQ[1][:, :]), inc=sSC)
            em.wait("tensor", sDV)
            em.wait("tensor", sSC)
        wv_m = p3_loadw(wv1_in, h, 0)
        for sp in range(8):
            xsl, xmv = xbm[sp]
            load_x2blk(sp, sp % 2)
            xsl, xmv = xbm[sp]
            em.wait("tensor", wslot_sem[xsl], xmv)
            em.wait("tensor", wslot_sem[wv_m[0]], wv_m[1])
            for t4 in range(4):
                kb = sp * 4 + t4
                ps = psVt[t4 % 2]
                for c in range(NG):
                    em.op("tensor", lambda e, c=c, xsl=xsl, t4=t4, ps=ps: e.matmul(
                        ps[:, :],
                        x2blk_tile(xsl, c)[:, t4 * 128:(t4 + 1) * 128],
                        whead[0][:, ts(c)],
                        start=(c == 0), stop=(c == NG - 1)), inc=sTE)
                tev = em.val(sTE)
                eng = "vector" if t4 % 2 == 0 else "scalar"
                em.wait(eng, sTE, tev)
                if eng == "vector":
                    em.op(eng, lambda e, kb=kb, ps=ps: e.tensor_copy(
                        AR[:, VA + kb * 129: VA + kb * 129 + 128], ps[:, :]), inc=sDV)
                else:
                    em.op(eng, lambda e, kb=kb, ps=ps: e.copy(
                        AR[:, VA + kb * 129: VA + kb * 129 + 128], ps[:, :]), inc=sSC)
                em.wait("tensor", sDV)
                em.wait("tensor", sSC)
        em.flush()

        qsq = AR[:, SLAB:SLAB + 512]
        for which, base in [(0, QF), (1, KF)]:
            dstv = attC if which == 0 else attK
            for chunk in range(8):
                em.op("scalar", lambda e, base=base, chunk=chunk: e.activation(
                    qsq, AR[:, base + chunk * 512: base + (chunk + 1) * 512],
                    ACT.Square), inc=sSC)
                em.wait("tensor", sSC)
                em.op("tensor", lambda e: e.matmul(
                    psN, onessq, qsq, start=True, stop=True), inc=sTE)
                em.wait("vector", sTE)
                em.tiny("vector", lambda e, dstv=dstv, chunk=chunk: e.tensor_reduce(
                    dstv[:, chunk:chunk + 1], psN, AX, OP.max))
                em.op("vector", lambda e: e.tensor_copy(qsq[:, 0:1], qsq[:, 0:1]),
                      inc=sDV)
                em.wait("scalar", sDV)
                em.wait("tensor", sDV)
        for sfx in range(1, 8):
            em.tiny("vector", lambda e, sfx=sfx: e.tensor_max(
                attK[:, sfx:sfx + 1], attK[:, sfx:sfx + 1], attK[:, sfx - 1:sfx]),
                inc=(sDV if sfx == 7 else None))

        for s in range(8):
            em.wait("tensor", sDV)
            em.wait("tensor", sSC)
            em.tiny("vector", lambda e, s=s: e.tensor_mul(
                stat[:, 2:3], attC[:, s:s + 1], attK[:, s:s + 1]), inc=sDV)
            em.wait("scalar", sDV)
            em.op("scalar", lambda e: e.activation(stat[:, 3:4], stat[:, 2:3],
                                                   ACT.Sqrt), inc=sSC)
            em.wait("vector", sSC)
            em.tiny("vector", lambda e: e.tensor_mul(stat[:, 4:5], stat[:, 3:4],
                                                     ascale))
            em.tiny("vector", lambda e: e.tensor_scalar_mul(stat[:, 4:5],
                                                            stat[:, 4:5], -1.0),
                    inc=sDV)
            em.wait("scalar", sDV)
            nkb = 4 * s + 4
            for kb in range(nkb):
                ps = psQ[kb % 2]
                em.op("tensor", lambda e, kb=kb, s=s, ps=ps: e.matmul(
                    ps[:, :], AR[:, KF + kb * 128: KF + (kb + 1) * 128],
                    AR[:, QF + s * 512: QF + (s + 1) * 512],
                    start=True, stop=True), inc=sTE)
                tev = em.val(sTE)
                slab = AR[:, SLAB + (kb % 2) * 512: SLAB + (kb % 2) * 512 + 512]
                em.wait("scalar", sTE, tev)
                em.op("scalar", lambda e, ps=ps, slab=slab: e.activation(
                    slab, ps[:, :], ACT.Exp, bias=stat[:, 4:5], scale=ascale),
                    inc=sSC)
                jd = kb - 4 * s
                if 0 <= jd < 4:
                    em.wait("vector", sSC)
                    em.op("vector", lambda e, slab=slab, jd=jd: e.tensor_mul(
                        slab[:, jd * 128:(jd + 1) * 128],
                        slab[:, jd * 128:(jd + 1) * 128], maskst), inc=sDV)
                    em.wait("tensor", sDV)
                em.wait("tensor", sSC)
                if debug and h == NHC - 1 and s == 0 and kb == 0:
                    if "dbg_slab" not in dbg:
                        dbg["dbg_slab"] = nc.dram_tensor("dbg_slab", [128, 512], BF16,
                                                         kind="ExternalOutput")
                    em.wait("gpsimd", sDV)
                    em.wait("gpsimd", sSC)
                    em.dma("gpsimd", dbg["dbg_slab"][:, :], slab, sXD)
                    em.wait("tensor", sXD)
                for j in range(4):
                    if kb <= 4 * s + j:
                        em.op("tensor", lambda e, j=j, kb=kb, slab=slab: e.matmul(
                            psO[j][:, :], slab[:, j * 128:(j + 1) * 128],
                            AR[:, VA + kb * 129: VA + kb * 129 + 129],
                            start=(kb == 0), stop=(kb == 4 * s + j)), inc=sTE)
            tev = em.val(sTE)
            for j in range(4):
                qt = 4 * s + j
                em.wait("vector", sTE, tev)
                em.tiny("vector", lambda e, j=j: e.tensor_scalar_mul(
                    stat[:, 6:7], psO[j][:, 128:129], float(2.0 ** 44)), inc=sDV)
                em.wait("scalar", sDV)
                em.op("scalar", lambda e: e.activation(
                    stat[:, 6:7], stat[:, 6:7], ACT.Ln), inc=sSC)
                em.wait("vector", sSC)
                em.tiny("vector", lambda e: e.tensor_scalar_mul(
                    stat[:, 6:7], stat[:, 6:7], -1.0), inc=sDV)
                em.wait("scalar", sDV)
                em.op("scalar", lambda e: e.activation(stat[:, 7:8], stat[:, 6:7],
                                                       ACT.Exp), inc=sSC)
                em.wait("vector", sSC)
                em.tiny("vector", lambda e: e.tensor_scalar_mul(
                    stat[:, 7:8], stat[:, 7:8], float(2.0 ** 44)), inc=sDV)
                em.op("vector", lambda e, j=j: e.tensor_single_scalar(
                    qsq[:, 0:128], psO[j][:, 0:128], stat[:, 7:8], OP.mult),
                    inc=sDV)
                em.wait("tensor", sDV)
                em.op("tensor", lambda e: e.transpose(psT3b[0][:, :], qsq[:, 0:128], identb),
                      inc=sTE)
                em.wait("vector", sTE)
                em.op("vector", lambda e, h=h, qt=qt: e.tensor_copy(
                    AR[:, G[3] + h * 4096 + qt * 128: G[3] + h * 4096 + (qt + 1) * 128],
                    psT3b[0][:, :][:, :]), inc=sDV)
                em.wait("tensor", sDV)
                em.wait("scalar", sDV)
        em.flush()
        if debug and h == NHC - 1:
            dbg["dbg_qf"] = nc.dram_tensor("dbg_qf", [128, SEQ], BF16,
                                           kind="ExternalOutput")
            dbg["dbg_kf"] = nc.dram_tensor("dbg_kf", [128, SEQ], BF16,
                                           kind="ExternalOutput")
            dbg["dbg_va"] = nc.dram_tensor("dbg_va", [128, 4128], BF16,
                                           kind="ExternalOutput")
            dbg["dbg_of"] = nc.dram_tensor("dbg_of", [128, SEQ], BF16,
                                           kind="ExternalOutput")
            dbg["dbg_cst"] = nc.dram_tensor("dbg_cst", [128, 16], F32,
                                            kind="ExternalOutput")
            em.dma("gpsimd", dbg["dbg_qf"][:, :], AR[:, QF:QF + SEQ], sXD)
            em.dma("gpsimd", dbg["dbg_kf"][:, :], AR[:, KF:KF + SEQ], sXD)
            em.dma("gpsimd", dbg["dbg_va"][:, :], AR[:, VA:VA + 4128], sXD)
            em.dma("gpsimd", dbg["dbg_of"][:, :],
                   AR[:, G[3] + h * 4096: G[3] + (h + 1) * 4096], sXD)
            em.dma("gpsimd", dbg["dbg_cst"][:, :], stat[:, :], sXD)
            em.wait("gpsimd", sXD)
            em.flush()
    pctx.close()

    pctx = contextlib.ExitStack()
    psC4 = [pctx.enter_context(nc.psum_tensor(f"op_ps{i}", [128, 512], F32))
            for i in range(4)]
    em.wait("gpsimd", sTE)
    for hh in range(4):
        for cs in range(4):
            em.dma("gpsimd", AR[:, G[1] + (hh * 4 + cs) * 512:
                                G[1] + (hh * 4 + cs) * 512 + 512],
                   bass.AP(wo1_in, hh * 128 * D + cs * 512, [[D, 128], [1, 512]]),
                   sPF)
    WO_PF = em.val(sPF)
    ostg = [smv(S0, 2048), smv(S1, 2048)]
    em.wait("tensor", sPF, WO_PF)
    for tt32 in range(32):
        for hh in range(4):
            for cs in range(4):
                em.op("tensor", lambda e, hh=hh, cs=cs, tt32=tt32: e.matmul(
                    psC4[cs][:, :],
                    AR[:, G[3] + hh * 4096 + tt32 * 128:
                       G[3] + hh * 4096 + (tt32 + 1) * 128],
                    AR[:, G[1] + (hh * 4 + cs) * 512: G[1] + (hh * 4 + cs) * 512 + 512],
                    start=(hh == 0), stop=(hh == 3)),
                    inc=(sTE if hh == 3 else None))
        tev = em.val(sTE)
        oslot = tt32 % 2
        osem = sS0 if oslot == 0 else sS1
        em.wait("vector", sTE, tev)
        em.wait("vector", osem)
        for cs in range(4):
            em.op("vector", lambda e, cs=cs, oslot=oslot: e.tensor_copy(
                ostg[oslot][:, cs * 512:(cs + 1) * 512], psC4[cs][:, :]), inc=sDV)
        em.wait("tensor", sDV)
        em.wait("gpsimd", sDV)
        em.dma("gpsimd", rs_in_d[tt32 // 8, ts(tt32 % 8), :], ostg[oslot], osem)
    em.wait("gpsimd", sS0)
    em.wait("gpsimd", sS1)
    em.op("gpsimd", lambda e: e.collective_compute(
        "ReduceScatter", OP.add, replica_groups=GRPS,
        ins=[rs_in_d.ap().opt()], outs=[rs_out_d.ap().opt()]), inc=sCC)
    em.flush()
    pctx.close()

    em.wait("sync", sCC)
    rstg = [AR[:, G[1]:G[1] + 2048], AR[:, G[1] + 2048:G[1] + 4096]]
    for tt in range(8):
        em.dma("sync", rstg[tt % 2], rs_out_d[ts(tt), :], wslot_sem[tt % 2])
        rm = em.val(wslot_sem[tt % 2])
        em.wait("vector", wslot_sem[tt % 2], rm)
        em.op("vector", lambda e, tt=tt: e.tensor_add(
            tmt(X2T, tt), tmt(X2T, tt), rstg[tt % 2]), inc=sDV)
        if debug:
            em.wait("gpsimd", sDV)
            em.dma("gpsimd", dbg["dbg_attn"][ts(tt), :], rstg[tt % 2], sXD)
        emit_ln(tmt(X2T, tt), tmt(X2T, tt), 1e-5, D)
        if tt + 1 < 8:
            em.wait("sync", sDV)
            em.wait("sync", sXD)
    em.flush()
    X3T = X2T
    if debug:
        for tt in range(8):
            em.dma("gpsimd", dbg["dbg_x3"][ts(tt), :], tmt(X3T, tt), sXD)
        em.flush()

    if upto < 4:
        em.wait("gpsimd", sXD)
        em.wait("gpsimd", sCC)
        em.flush()
        em.ctx.close()
        return nc, dbg

    pctx = contextlib.ExitStack()
    psT4 = [pctx.enter_context(nc.psum_tensor(f"p4_psT{i}", [128, 128], BF16))
            for i in range(2)]
    for tt in range(8):
        em.dma("gpsimd", x3_spill_d[ts(tt), :], tmt(X3T, tt), sXD)
    X3F = OX
    cnt4 = [0]
    for jt in range(8):
        for ic in range(NG):
            pst = psT4[cnt4[0] % 2]
            em.op("tensor", lambda e, jt=jt, ic=ic, pst=pst: e.transpose(
                pst[:, :], tmt(X3T, jt)[:, ts(ic)], identb), inc=sTE)
            eng = "vector" if cnt4[0] % 2 == 0 else "scalar"
            em.wait(eng, sTE)
            if eng == "vector":
                em.op(eng, lambda e, jt=jt, ic=ic, pst=pst: e.tensor_copy(
                    AR[:, X3F + ic * TL + jt * 128: X3F + ic * TL + (jt + 1) * 128],
                    pst[:, :]), inc=sDV)
            else:
                em.op(eng, lambda e, jt=jt, ic=ic, pst=pst: e.copy(
                    AR[:, X3F + ic * TL + jt * 128: X3F + ic * TL + (jt + 1) * 128],
                    pst[:, :]), inc=sSC)
            cnt4[0] += 1
            if cnt4[0] % 2 == 0:
                em.wait("tensor", sDV)
                em.wait("tensor", sSC)
    em.wait("tensor", sDV)
    em.wait("tensor", sSC)
    em.flush()
    pctx.close()

    def x3f(c):
        return AR[:, X3F + c * TL: X3F + (c + 1) * TL]

    pctx = contextlib.ExitStack()
    psR = pctx.enter_context(nc.psum_tensor("p4_psR", [128, 8], F32))
    psW = pctx.enter_context(nc.psum_tensor("p4_psW", [8, 128], BF16))
    psB = pctx.enter_context(nc.psum_tensor("p4_psB", [128, 512], F32))
    routb = smv(SSP, 128)
    for c in range(NG):
        em.dma("gpsimd", routb[:, c * 8:(c + 1) * 8], router_in[ts(c), :], sPF)
    RT_PF = em.val(sPF)
    W8 = SM[0:8, S2:S2 + 1024]
    wtt = smv(S0, 8)
    em.wait("tensor", sPF, RT_PF)
    for tt in range(8):
        for c in range(NG):
            em.op("tensor", lambda e, tt=tt, c=c: e.matmul(
                psR[:, :], x3f(c)[:, ts(tt)], routb[:, c * 8:(c + 1) * 8],
                start=(c == 0), stop=(c == NG - 1)), inc=(sTE if c == NG - 1 else None))
        em.wait("vector", sTE)
        em.tiny("vector", lambda e: e.tensor_reduce(stat[:, 0:1], psR[:, :], AX,
                                                    OP.max))
        em.tiny("vector", lambda e: e.tensor_scalar_mul(stat[:, 0:1], stat[:, 0:1],
                                                        -1.0), inc=sDV)
        em.wait("scalar", sDV)
        em.op("scalar", lambda e: e.activation(sqscr[:, 0:8], psR[:, :], ACT.Exp,
                                               bias=stat[:, 0:1]), inc=sSC)
        em.wait("vector", sSC)
        em.tiny("vector", lambda e: e.tensor_single_scalar(
            sqscr[:, 8:16], sqscr[:, 0:8], 1.0, OP.is_ge))
        em.tiny("vector", lambda e: e.scalar_tensor_tensor(
            sqscr[:, 8:16], sqscr[:, 8:16], -1e30, sqscr[:, 0:8], OP.mult, OP.add))
        em.tiny("vector", lambda e: e.tensor_reduce(stat[:, 1:2], sqscr[:, 8:16],
                                                    AX, OP.max))
        em.tiny("vector", lambda e: e.tensor_scalar_add(stat[:, 2:3], stat[:, 1:2],
                                                        1.0))
        em.tiny("vector", lambda e: e.reciprocal(stat[:, 2:3], stat[:, 2:3]))
        em.tiny("vector", lambda e: e.tensor_single_scalar(
            wtt, sqscr[:, 0:8], stat[:, 1:2], OP.is_ge))
        em.tiny("vector", lambda e: e.tensor_mul(wtt, wtt, sqscr[:, 0:8]))
        em.tiny("vector", lambda e: e.tensor_single_scalar(
            wtt, wtt, stat[:, 2:3], OP.mult), inc=sDV)
        em.wait("tensor", sDV)
        em.op("tensor", lambda e: e.transpose(psW[:, :], wtt, identb), inc=sTE)
        em.wait("vector", sTE)
        em.op("vector", lambda e, tt=tt: e.tensor_copy(W8[:, ts(tt)], psW[:, :]),
              inc=sDV)
        em.wait("tensor", sDV)
    em.wait("gpsimd", sDV)
    em.dma("gpsimd", w8_d[:, :], W8[:, :], sXD)
    em.flush()
    pctx.close()

    pctx = contextlib.ExitStack()
    ps8 = [pctx.enter_context(nc.psum_tensor(f"p4_a{i}", [128, 512], F32))
           for i in range(8)]
    gumark = {}

    def abank_wait(b):
        if b in gumark:
            em.wait("tensor", gumark[b][0], gumark[b][1])
    gbufs = [smv(512, 2048), smv(2560, 2048)]
    ubufs = [smv(4608, 2048), smv(6656, 2048)]
    wbc = smv(8704, 1024)
    silus = [smv(9728, 512), smv(10240, 512)]
    dstg = smv(10752, 1024)

    def act_ap(e, fc, th):
        base = G[e // 2] + (e % 2) * 8192
        return AR[:, base + fc * 1024 + th * 512: base + fc * 1024 + (th + 1) * 512]

    gu_marks = {}

    def emit_gu_load(e_, fc_, slot):
        em.wait("gpsimd", sTE)
        em.dma("gpsimd",
               bass.AP(SM, 512 + slot * 2048, [[12288, 128], [128, NG], [1, 128]]),
               bass.AP(gate_in, (e_ * 8 + fc_) * D * 128,
                       [[128, 128], [128 * 128, NG], [1, 128]]), sWA)
        gm = em.val(sWA)
        em.dma("gpsimd",
               bass.AP(SM, 4608 + slot * 2048, [[12288, 128], [128, NG], [1, 128]]),
               bass.AP(up_in, (e_ * 8 + fc_) * D * 128,
                       [[128, 128], [128 * 128, NG], [1, 128]]), sWB)
        gu_marks[(e_, fc_)] = (gm, em.val(sWB))

    dcast_seq = [0]

    def emit_downcast_step():
        i = dcast_seq[0]
        if i >= NE * 8 * 2:
            return
        e_, r = divmod(i, 16)
        fc_, half = divmod(r, 2)
        em.wait("gpsimd", sS0)
        em.dma("gpsimd", dstg,
               bass.AP(down_in, e_ * FMOE * D + fc_ * 128 * D + half * 1024,
                       [[D, 128], [1, 1024]]), sDC)
        em.wait("gpsimd", sDC)
        em.dma("gpsimd",
               bass.AP(downb_d, e_ * FMOE * D + fc_ * 128 * D + half * 1024,
                       [[D, 128], [1, 1024]]), dstg, sS0)
        dcast_seq[0] += 1

    w8rows = [w8row, w8rowb]
    w8rm = {}
    w8ld = {}
    silmark = {}

    def w8_prefetch(e_):
        slot = e_ % 2
        if slot in w8rm:
            em.wait("sync", w8rm[slot][0], w8rm[slot][1])
        em.wait("sync", sXD)
        em.dma("sync", w8rows[slot][0:1, :], w8_d[e_:e_ + 1, :], sPF)
        w8ld[e_] = em.val(sPF)

    w8_prefetch(0)
    for e in range(NE):
        wslot = e % 2
        em.wait("tensor", sPF, w8ld[e])
        abank_wait(0)
        for th2 in range(2):
            em.op("tensor", lambda ee, th2=th2, wslot=wslot: ee.matmul(
                ps8[0][:, :], onessq[0:1, 0:128],
                w8rows[wslot][0:1, th2 * 512:(th2 + 1) * 512],
                start=True, stop=True), inc=sTE)
            em.wait("vector", sTE)
            em.op("vector", lambda ee, th2=th2: ee.tensor_copy(
                wbc[:, th2 * 512:(th2 + 1) * 512], ps8[0][:, :]), inc=sDV)
            if th2 == 0:
                em.wait("tensor", sDV)
        gumark[0] = (sDV, em.val(sDV))
        w8rm[wslot] = (sTE, em.val(sTE))
        if e + 1 < NE:
            w8_prefetch(e + 1)
        for fc in range(8):
            gslot = (e * 8 + fc) % 2
            par = fc % 2
            if e == 0 and fc == 0:
                emit_gu_load(0, 0, 0)
            if e * 8 + fc + 1 < 64:
                en, fn = divmod(e * 8 + fc + 1, 8)
                emit_gu_load(en, fn, 1 - gslot)
            gm, um = gu_marks[(e, fc)]
            em.wait("tensor", sWA, gm)
            em.wait("tensor", sWB, um)
            emit_downcast_step()
            emit_downcast_step()
            gbuf, ubuf = gbufs[gslot], ubufs[gslot]
            for th in range(2):
                abank_wait(par * 4 + th)
            for c in range(NG):
                for th in range(2):
                    em.op("tensor", lambda ee, c=c, th=th, gbuf=gbuf,
                          par=par: ee.matmul(
                        ps8[par * 4 + th][:, :], gbuf[:, ts(c)],
                        x3f(c)[:, ts(th, 512)],
                        start=(c == 0), stop=(c == NG - 1)),
                        inc=(sTE if th == 1 else None))
            for th in range(2):
                abank_wait(par * 4 + 2 + th)
            for c in range(NG):
                for th in range(2):
                    em.op("tensor", lambda ee, c=c, th=th, ubuf=ubuf,
                          par=par: ee.matmul(
                        ps8[par * 4 + 2 + th][:, :], ubuf[:, ts(c)],
                        x3f(c)[:, ts(th, 512)],
                        start=(c == 0), stop=(c == NG - 1)),
                        inc=(sTE if th == 1 else None))
            tev = em.val(sTE)
            em.wait("scalar", sTE, tev)
            for th in range(2):
                if th in silmark:
                    em.wait("scalar", sDV, silmark[th])
                em.op("scalar", lambda ee, th=th, par=par: ee.activation(
                    silus[th], ps8[par * 4 + th][:, :], ACT.Silu), inc=sSC)
                gumark[par * 4 + th] = (sSC, em.val(sSC))
                em.wait("vector", sSC)
                em.op("vector", lambda ee, th=th, par=par: ee.tensor_mul(
                    silus[th], silus[th], ps8[par * 4 + 2 + th][:, :]))
                em.op("vector", lambda ee, e=e, fc=fc, th=th: ee.tensor_mul(
                    act_ap(e, fc, th), silus[th],
                    wbc[:, th * 512:(th + 1) * 512]), inc=sDV)
                mv = em.val(sDV)
                gumark[par * 4 + 2 + th] = (sDV, mv)
                silmark[th] = mv
    em.wait("tensor", sDV)
    em.wait("tensor", sSC)
    em.flush()
    pctx.close()

    if debug:
        dbg["dbg_downb"] = nc.dram_tensor("dbg_downb", [NE, FMOE, D], BF16,
                                          kind="ExternalOutput")
        dbg["dbg_act"] = nc.dram_tensor("dbg_act", [128, 8192], BF16,
                                        kind="ExternalOutput")
        dbg["dbg_wbc"] = nc.dram_tensor("dbg_wbc", [128, 1024], BF16,
                                        kind="ExternalOutput")
        em.wait("gpsimd", sS0)
        em.dma("gpsimd", dbg["dbg_downb"][:, :, :], downb_d[:, :, :], sXD)
        em.dma("gpsimd", dbg["dbg_act"][:, :], AR[:, G[0]:G[0] + 8192], sXD)
        em.dma("gpsimd", dbg["dbg_wbc"][:, :], wbc, sXD)
        em.wait("gpsimd", sXD)
        em.flush()

    pctx = contextlib.ExitStack()
    psD = [pctx.enter_context(nc.psum_tensor(f"p4_d{i}", [128, 512], F32))
           for i in range(8)]
    dbuf = [smv(512, 4096), smv(4608, 4096)]
    x3r = smv(8704, 2048)
    while dcast_seq[0] < NE * 8 * 2:
        emit_downcast_step()
    em.wait("gpsimd", sS0)
    em.wait("sync", sS0)
    R4 = OX
    dn_marks = {}

    def load_down_e(ttg, dsg, e_, ds2, slot):
        em.wait("sync", sTE)
        em.dma("sync", bass.AP(SM, 512 + slot * 4096,
                               [[12288, 128], [512, 8], [1, 512]]),
               bass.AP(downb_d, e_ * FMOE * D + dsg * 1024 + ds2 * 512,
                       [[D, 128], [128 * D, 8], [1, 512]]), wslot_sem[slot])
        dn_marks[(ttg, dsg, e_, ds2)] = (slot, em.val(wslot_sem[slot]))

    def dn_key(i):
        ttg, r = divmod(i, 2 * NE * 2)
        dsg, r2 = divmod(r, NE * 2)
        e_, ds2 = divmod(r2, 2)
        return (ttg, dsg, e_, ds2)

    NDN = 2 * 2 * NE * 2
    load_down_e(*dn_key(0), 0)
    dseq = [0]
    for ttg in range(2):
        for dsg in range(2):
            for e in range(NE):
                for ds2 in range(2):
                    i = dseq[0]
                    if i + 1 < NDN:
                        load_down_e(*dn_key(i + 1), (i + 1) % 2)
                    sl, mv = dn_marks[dn_key(i)]
                    em.wait("tensor", wslot_sem[sl], mv)
                    for fc in range(8):
                        first = (e == 0 and fc == 0)
                        last = (e == NE - 1 and fc == 7)
                        for t4 in range(4):
                            tt = ttg * 4 + t4
                            em.op("tensor", lambda ee, e=e, fc=fc, tt=tt, t4=t4,
                                  ds2=ds2, sl=sl, first=first, last=last: ee.matmul(
                                      psD[t4 * 2 + ds2][:, :],
                                      AR[:, G[e // 2] + (e % 2) * 8192 + fc * 1024
                                         + tt * 128: G[e // 2] + (e % 2) * 8192
                                         + fc * 1024 + (tt + 1) * 128],
                                      dbuf[sl][:, fc * 512:(fc + 1) * 512],
                                      start=first, stop=last),
                                  inc=(sTE if t4 == 3 else None))
                    dseq[0] += 1
            tev = em.val(sTE)
            em.wait("vector", sTE, tev)
            for t4 in range(4):
                tt = ttg * 4 + t4
                em.wait("sync", sDV)
                em.dma("sync", x3r, x3_spill_d[ts(tt), :], sPF)
                xm = em.val(sPF)
                em.wait("vector", sPF, xm)
                for ds2 in range(2):
                    em.op("vector", lambda ee, t4=t4, ds2=ds2, tt=tt, dsg=dsg:
                          ee.tensor_add(
                              AR[:, R4 + tt * 2048 + dsg * 1024 + ds2 * 512:
                                 R4 + tt * 2048 + dsg * 1024 + (ds2 + 1) * 512],
                              psD[t4 * 2 + ds2][:, :],
                              x3r[:, dsg * 1024 + ds2 * 512:
                                  dsg * 1024 + (ds2 + 1) * 512]), inc=sDV)
            em.wait("tensor", sDV)
        for t4 in range(4):
            tt = ttg * 4 + t4
            r4t = AR[:, R4 + tt * 2048: R4 + (tt + 1) * 2048]
            em.wait("vector", sS1)
            em.wait("scalar", sS1)
            emit_ln(r4t, outstg, 1e-5, D, sq_ap=smv(10752, 1024), sq_w=1024)
            em.wait("gpsimd", sDV)
            em.dma("gpsimd", out_d[ts(tt), :], outstg, sS1)
            if debug:
                em.dma("gpsimd", dbg["dbg_moe"][ts(tt), :], r4t, sXD)
    em.flush()
    pctx.close()

    em.wait("gpsimd", sS1)
    em.wait("gpsimd", sS0)
    em.wait("gpsimd", sXD)
    em.wait("gpsimd", sCC)
    em.flush()
    em.ctx.close()
    return nc, dbg


def _stage(inputs):
    f = np.ascontiguousarray
    x = np.asarray(inputs["L0_attn_x"], dtype=np.float32)
    wkT = np.asarray(inputs["L0_attn_Wk"], np.float32).T
    wvT = np.asarray(inputs["L0_attn_Wv"], np.float32).T
    wrT = np.asarray(inputs["L0_attn_Wr"], np.float32).T
    woT = np.asarray(inputs["L0_attn_Wo"], np.float32).T
    blk = lambda w: f(w.reshape(D, NG, 128).transpose(1, 0, 2))
    wkB, wvB, wrB, woB = blk(wkT), blk(wvT), blk(wrT), blk(woT)
    col = lambda v: f(np.asarray(v, np.float32).reshape(D, 1))
    wfcT = f(np.asarray(inputs["L0_ffn_w_fc"], np.float32).T)
    wprojT = f(np.asarray(inputs["L0_ffn_w_proj"], np.float32).T)
    wqT = f(np.asarray(inputs["L1_attn_wq"], np.float32).T)
    wk1T = f(np.asarray(inputs["L1_attn_wk"], np.float32).T)
    wv1T = f(np.asarray(inputs["L1_attn_wv"], np.float32).T)
    wo1T = f(np.asarray(inputs["L1_attn_wo"], np.float32).T)
    router = f(np.asarray(inputs["L1_ffn_router_w"], np.float32))
    gate = np.asarray(inputs["L1_ffn_gate_proj"], np.float32)
    up = np.asarray(inputs["L1_ffn_up_proj"], np.float32)
    down = f(np.asarray(inputs["L1_ffn_down_proj"], np.float32))
    gateB = f(gate.reshape(NE, D, 8, 128).transpose(0, 2, 1, 3))
    upB = f(up.reshape(NE, D, 8, 128).transpose(0, 2, 1, 3))
    identb = np.eye(128, dtype=ml_dtypes.bfloat16)
    maskst = np.triu(np.ones((128, 128), np.float32)).astype(ml_dtypes.bfloat16)
    ones_col = np.ones((128, 1), np.float32)
    ones_row = np.ones((1, 128), np.float32)
    ones_sq = np.ones((128, 128), dtype=ml_dtypes.bfloat16)
    attn_scale = np.full((128, 1), 1.0 / np.sqrt(128.0), np.float32)

    in_maps = []
    for c in range(8):
        b, q = c // 4, c % 4
        t0 = q * TL
        xloc = np.zeros((D, TL + 1), np.float32)
        xs = x[b, max(t0 - 1, 0):t0 + TL, :]
        xloc[:, (1 if t0 == 0 else 0):] = xs.T
        cm = np.full((1, 4), 1e6, np.float32)
        for j in range(q):
            cm[0, j] = 1024.0 * (q - 1 - j)
        hsl = slice(c * 1024, (c + 1) * 1024)
        heads = slice((c % 4) * 512, (c % 4) * 512 + 512)
        im = {
            "x_f": f(xloc),
            "l0_wk": wkB, "l0_wv": wvB, "l0_wr": wrB, "l0_wo": woB,
            "l0_decay": col(inputs["L0_attn_time_decay"]),
            "l0_first": col(inputs["L0_attn_time_first"]),
            "l0_mixk": col(inputs["L0_attn_time_mix_k"]),
            "l0_mixv": col(inputs["L0_attn_time_mix_v"]),
            "l0_mixr": col(inputs["L0_attn_time_mix_r"]),
            "carry_mult": cm,
            "wfc_slice": f(wfcT[:, hsl]),
            "wproj_slice": f(wprojT[hsl, :]),
            "l1_wq": f(wqT[:, heads]), "l1_wk": f(wk1T[:, heads]),
            "l1_wv": f(wv1T[:, heads]), "l1_wo": f(wo1T[heads, :]),
            "router_w": router,
            "moe_gate": gateB, "moe_up": upB, "moe_down": down,
            "ident_bf": identb, "mask_st": maskst, "ones_col": ones_col,
            "ones_row": ones_row, "ones_sq": ones_sq, "attn_scale": attn_scale,
        }
        in_maps.append(im)
    return in_maps


def kernel(**inputs):
    nc, _ = build_program(upto=99, debug=False)
    in_maps = _stage(inputs)
    res = run_bass_kernel_spmd(nc, in_maps, core_ids=list(range(8)))
    shards = [res.results[c]["out"] for c in range(8)]
    out = np.stack(shards).reshape(BATCH, SEQ, D).astype(np.float32)
    return out


# version 04
# speedup vs baseline: 1.0130x; 1.0010x over previous
import contextlib
import numpy as np
import ml_dtypes
import concourse.bass as bass
import concourse.mybir as mybir
from concourse.bass_utils import run_bass_kernel_spmd

F32 = mybir.dt.float32
BF16 = mybir.dt.bfloat16
AX = mybir.AxisListType.X
OP = mybir.AluOpType
ACT = mybir.ActivationFunctionType

BATCH, SEQ, D = 2, 4096, 2048
TL = 1024
NG = D // 128
HID = 8192
NHC, HDIM = 4, 128
NE, FMOE = 8, 1024
GRPS = [[0, 1, 2, 3], [4, 5, 6, 7]]
ALL8 = [[0, 1, 2, 3, 4, 5, 6, 7]]
ENGS = ("sync", "tensor", "vector", "scalar", "gpsimd")

OX = 0
G = [16400, 32784, 49168, 65552]
ARENA_COLS = 81936


def ts(i, n=128):
    return slice(i * n, (i + 1) * n)


class Emitter:
    def __init__(self, nc):
        self.nc = nc
        self.q = {e: [] for e in ENGS}
        self.semv = {}
        self.ctx = contextlib.ExitStack()

    def sem(self, name):
        s = self.ctx.enter_context(self.nc.semaphore(name))
        self.semv[s] = 0
        return s

    def sbuf(self, name, shape, dtype):
        return self.ctx.enter_context(self.nc.sbuf_tensor(name, list(shape), dtype))

    def op(self, eng, fn, inc=None, incn=1):
        if inc is not None:
            self.semv[inc] += incn

            def g(e, fn=fn, inc=inc, incn=incn):
                fn(e).then_inc(inc, incn)
            self.q[eng].append(g)
        else:
            self.q[eng].append(fn)

    def dma(self, eng, out, in_, inc, **kw):
        self.op(eng, lambda e, o=out, i=in_, kw=kw: e.dma_start(out=o, in_=i, **kw),
                inc=inc, incn=16)

    def tiny(self, eng, fn, inc=None, incn=1):
        self.op(eng, fn, inc=inc, incn=incn)
        self.q[eng].append(lambda e: e.drain())

    def wait(self, eng, sem, val=None):
        v = self.semv[sem] if val is None else val
        self.q[eng].append(lambda e, s=sem, v=v: e.wait_ge(s, v))

    def val(self, sem):
        return self.semv[sem]

    def flush(self):
        blk = self.nc.Block()
        with blk:
            for name in ENGS:
                if self.q[name]:
                    fs = self.q[name]

                    def run(eng, fs=fs):
                        for f in fs:
                            f(eng)
                    getattr(blk, name)(run)
        self.q = {e: [] for e in ENGS}


def build_program(upto=99, debug=False):
    nc = bass.Bass("TRN2", target_bir_lowering=False)

    def din(name, shape, dtype=F32):
        return nc.dram_tensor(name, list(shape), dtype, kind="ExternalInput")

    def dint(name, shape, dtype):
        return nc.dram_tensor(name, list(shape), dtype, kind="Internal")

    x_in = din("x_f", [D, TL + 1])
    wk_in = din("l0_wk", [NG, D, 128])
    wv_in = din("l0_wv", [NG, D, 128])
    wr_in = din("l0_wr", [NG, D, 128])
    wo_in = din("l0_wo", [NG, D, 128])
    decay_in = din("l0_decay", [D, 1])
    first_in = din("l0_first", [D, 1])
    mixk_in = din("l0_mixk", [D, 1])
    mixv_in = din("l0_mixv", [D, 1])
    mixr_in = din("l0_mixr", [D, 1])
    cmult_in = din("carry_mult", [1, 4])
    wfc_in = din("wfc_slice", [D, 1024])
    wproj_in = din("wproj_slice", [1024, D])
    wq_in = din("l1_wq", [D, 512])
    wk1_in = din("l1_wk", [D, 512])
    wv1_in = din("l1_wv", [D, 512])
    wo1_in = din("l1_wo", [512, D])
    router_in = din("router_w", [D, NE])
    gate_in = din("moe_gate", [NE, 8, D, 128])
    up_in = din("moe_up", [NE, 8, D, 128])
    down_in = din("moe_down", [NE, FMOE, D])
    identb_in = din("ident_bf", [128, 128], BF16)
    maskst_in = din("mask_st", [128, 128], BF16)
    ones_in = din("ones_col", [128, 1])
    onesrow_in = din("ones_row", [1, 128])
    onessq_in = din("ones_sq", [128, 128], BF16)
    ascale_in = din("attn_scale", [128, 1])

    out_d = nc.dram_tensor("out", [TL, D], F32, kind="ExternalOutput")
    dbg = {}
    if debug:
        for name, shape, dt in [
            ("dbg_x1", [TL, D], BF16), ("dbg_x2", [TL, D], BF16),
            ("dbg_x3", [TL, D], BF16), ("dbg_srw", [D, TL], BF16),
            ("dbg_ek", [D, TL], BF16), ("dbg_r1", [D, TL], BF16),
            ("dbg_xhat", [TL, D], BF16), ("dbg_h", [TL, HID], BF16),
            ("dbg_attn", [TL, D], BF16), ("dbg_moe", [TL, D], BF16),
            ("dbg_r1tok", [TL, D], BF16),
        ]:
            dbg[name] = nc.dram_tensor(name, shape, dt, kind="ExternalOutput")

    tot_in_d = dint("wkv_tot_in", [D, 2], F32)
    tot_ag_d = dint("wkv_tot_ag", [4, D, 2], F32)
    gam_in_d = dint("gam_in", [1, 2], F32)
    gam_out_d = dint("gam_out", [1, 2], F32)
    qfc_in_d = dint("qfc_in", [D, 1024], BF16)
    qfc_d = dint("qfc_ag", [8, D, 1024], BF16)
    qproj_in_d = dint("qproj_in", [1024, D], BF16)
    qproj_d = dint("qproj_ag", [8, 1024, D], BF16)
    x2_in_d = dint("x2_in", [D, TL], BF16)
    x2_ag_d = dint("x2_ag", [4, D, TL], BF16)
    rs_in_d = dint("rs_in", [4, TL, D], BF16)
    rs_out_d = dint("rs_out", [TL, D], BF16)
    hhat_d = dint("hhat", [TL, HID], BF16)
    hf_d = dint("hf", [HID, TL], BF16)
    ffn_d = dint("ffn", [D, TL], BF16)
    downb_d = dint("down_bf", [NE, FMOE, D], BF16)
    x1_spill_d = dint("x1_spill", [TL, D], BF16)
    x2_spill_d = dint("x2_spill", [TL, D], BF16)
    x3_spill_d = dint("x3_spill", [TL, D], BF16)
    bc_d = dint("bc_scratch", [1, 16], F32)
    w8_d = dint("w8", [8, 1024], BF16)

    em = Emitter(nc)
    sLD = em.sem("sLD")
    sXD = em.sem("sXD")
    sTE = em.sem("sTE")
    sDV = em.sem("sDV")
    sSC = em.sem("sSC")
    sCC = em.sem("sCC")
    sPF = em.sem("sPF")
    sWA = em.sem("sWA")
    sWB = em.sem("sWB")
    sS0 = em.sem("sS0")
    sS1 = em.sem("sS1")
    sDC = em.sem("sDC")

    AR = em.sbuf("AR", [128, ARENA_COLS], BF16)
    ARFW = 3520
    ARF = em.sbuf("ARF", [128, ARFW], F32)
    SM = em.sbuf("SM", [128, 12288], BF16)
    w8row = em.sbuf("w8row", [1, 1024], BF16)
    w8rowb = em.sbuf("w8rowb", [1, 1024], BF16)

    identb = SM[:, 0:128]
    maskst = SM[:, 128:256]
    onessq = SM[:, 272:400]
    onescol_b = SM[:, 256:257]
    S0, S1, S2, SSP = 512, 2560, 4608, 6656

    def smv(a, n):
        return SM[:, a:a + n]

    def fv(a, n):
        return ARF[:, a:a + n]
    onescol = fv(3074, 1)
    vec_dec = fv(3076, NG)
    vec_w = fv(3092, NG)
    vec_lam = fv(3108, NG)
    vec_eu = fv(3124, NG)
    vec_mixk = fv(3140, NG)
    vec_mixv = fv(3156, NG)
    vec_mixr = fv(3172, NG)
    cmult = fv(3188, 4)
    scalars = fv(3192, 8)
    negthr = fv(3200, 2)
    stat = fv(3204, 16)
    tot_sb = fv(3220, 2 * NG)
    init_sb = fv(3252, 2 * NG)
    tots_g = fv(3284, 8)
    coef_g = fv(3292, 4)
    attC = fv(3296, 8)
    ascale = fv(3310, 1)
    attK = fv(3312, 8)
    onesrow = ARF[0:1, 3392:3520]
    lamfull = ARF[:, 0:TL]
    sa_full = ARF[:, TL:2 * TL + 1]
    sb_full = ARF[:, 2 * TL + 1:3 * TL + 2]
    wt0 = ARF[:, 0:1024]
    wt1 = ARF[:, 1024:2048]
    outstg = ARF[:, 0:2048]

    def fm(base, g, w=TL):
        return AR[:, base + g * w: base + g * w + w]

    def tmt(base, t):
        return AR[:, base + t * D: base + t * D + D]

    def xg(g, a, b):
        return AR[:, OX + g * (TL + 1) + a: OX + g * (TL + 1) + b]

    pctx = contextlib.ExitStack()
    ps1 = pctx.enter_context(nc.psum_tensor("p0_ps1", [1, 512], F32))
    psB16 = pctx.enter_context(nc.psum_tensor("p0_psB16", [128, 16], F32))

    for buf, src in [(identb, identb_in), (maskst, maskst_in), (onessq, onessq_in)]:
        em.dma("sync", buf, src[:, :], sLD)
    em.dma("sync", ascale, ascale_in[:, :], sLD)
    em.dma("sync", onescol, ones_in[:, :], sLD)
    em.dma("sync", onesrow, onesrow_in[:, :], sLD)
    for g in range(NG):
        em.dma("sync", vec_dec[:, g:g + 1], decay_in[ts(g), :], sLD)
        em.dma("sync", vec_eu[:, g:g + 1], first_in[ts(g), :], sLD)
        em.dma("sync", vec_mixk[:, g:g + 1], mixk_in[ts(g), :], sLD)
        em.dma("sync", vec_mixv[:, g:g + 1], mixv_in[ts(g), :], sLD)
        em.dma("sync", vec_mixr[:, g:g + 1], mixr_in[ts(g), :], sLD)
    em.dma("sync", cmult[:, :], bass.AP(cmult_in, 0, [[0, 128], [1, 4]]), sLD)
    for g in range(NG):
        em.dma("gpsimd", xg(g, 0, TL + 1), x_in[ts(g), :], sLD)
    em.wait("scalar", sLD)
    em.op("scalar", lambda e: e.activation(vec_w[:, :], vec_dec[:, :], ACT.Exp), inc=sSC)
    em.wait("vector", sSC)
    em.op("vector", lambda e: e.tensor_scalar_mul(vec_w[:, :], vec_w[:, :], -1.0), inc=sDV)
    em.wait("scalar", sDV)
    em.op("scalar", lambda e: e.activation(vec_lam[:, :], vec_w[:, :], ACT.Exp))
    em.op("scalar", lambda e: e.activation(vec_eu[:, :], vec_eu[:, :], ACT.Exp), inc=sSC)
    em.wait("vector", sSC)
    em.op("vector", lambda e: e.tensor_copy(onescol_b, onescol), inc=sDV)

    def wtile_src(i):
        if i < NG:
            return wfc_in[ts(i), :]
        j, half = divmod(i - NG, 2)
        return bass.AP(wproj_in, j * 128 * D + half * 1024, [[D, 128], [1, 1024]])

    wsem = [sWA, sWB]
    ld_marks = [None] * 33
    em.dma("sync", wt0[:, :], wtile_src(0), wsem[0])
    ld_marks[0] = em.val(wsem[0])
    for i in range(32):
        cur = wt0 if i % 2 == 0 else wt1
        if i + 1 < 32:
            em.dma("sync", (wt1 if i % 2 == 0 else wt0)[:, :], wtile_src(i + 1),
                   wsem[(i + 1) % 2])
            ld_marks[i + 1] = em.val(wsem[(i + 1) % 2])
        em.wait("vector", wsem[i % 2], ld_marks[i])
        em.tiny("vector", lambda e, cur=cur: e.tensor_reduce(
            stat[:, 1:2], cur[:, :], AX, OP.add, apply_absolute_value=True))
        dst = stat[:, 2:3] if i < NG else stat[:, 3:4]
        if i == 0 or i == NG:
            em.tiny("vector", lambda e, dst=dst: e.tensor_copy(dst, stat[:, 1:2]),
                    inc=sDV)
        else:
            em.tiny("vector", lambda e, dst=dst: e.tensor_add(dst, dst, stat[:, 1:2]),
                    inc=sDV)
        if i + 1 < 32:
            em.wait("sync", sDV)
    em.tiny("vector", lambda e: e.tensor_copy(stat[:, 4:6], stat[:, 2:4]), inc=sDV)
    em.wait("tensor", sDV)
    em.op("tensor", lambda e: e.matmul(ps1[0:1, 0:2], onescol, stat[:, 4:6],
                                       start=True, stop=True), inc=sTE)
    em.wait("vector", sTE)
    em.tiny("vector", lambda e: e.tensor_copy(stat[0:1, 6:8], ps1[0:1, 0:2]), inc=sDV)
    em.wait("gpsimd", sDV)
    em.dma("gpsimd", gam_in_d[:, :], stat[0:1, 6:8], sXD)
    em.wait("gpsimd", sXD)
    em.op("gpsimd", lambda e: e.collective_compute(
        "AllReduce", OP.add, replica_groups=ALL8,
        ins=[gam_in_d.ap().opt()], outs=[gam_out_d.ap().opt()]), inc=sCC)
    em.wait("sync", sCC)
    em.dma("sync", stat[0:1, 6:8], gam_out_d[:, :], sLD)
    em.wait("vector", sLD)
    NW = float(HID * D)
    em.tiny("vector", lambda e: e.tensor_scalar_mul(stat[0:1, 8:10], stat[0:1, 6:8],
                                                    1.0 / NW))
    em.tiny("vector", lambda e: e.tensor_scalar_add(stat[0:1, 10:12], stat[0:1, 8:10],
                                                    1e-6))
    em.tiny("vector", lambda e: e.tensor_scalar_mul(stat[0:1, 10:12], stat[0:1, 10:12],
                                                    0.5))
    em.tiny("vector", lambda e: e.tensor_scalar_mul(stat[0:1, 12:14], stat[0:1, 8:10],
                                                    0.5), inc=sDV)
    em.wait("tensor", sDV)
    em.op("tensor", lambda e: e.matmul(psB16[:, 0:4], onesrow, stat[0:1, 10:14],
                                       start=True, stop=True), inc=sTE)
    em.wait("vector", sTE)
    em.tiny("vector", lambda e: e.tensor_copy(scalars[:, 0:4], psB16[:, 0:4]), inc=sDV)
    em.tiny("vector", lambda e: e.tensor_scalar_mul(negthr[:, 0:2], scalars[:, 0:2],
                                                    -1.0), inc=sDV)
    em.wait("scalar", sDV)
    em.wait("sync", sDV)

    qa = AR[:, G[0]:G[0] + 1024]
    qb = AR[:, G[0] + 1024:G[0] + 2048]

    def qtile_dst(i):
        if i < NG:
            return qfc_in_d[ts(i), :]
        j, half = divmod(i - NG, 2)
        return bass.AP(qproj_in_d, j * 128 * D + half * 1024, [[D, 128], [1, 1024]])

    em.dma("sync", wt0[:, :], wtile_src(0), wsem[0])
    ld_marks[0] = em.val(wsem[0])
    for i in range(32):
        cur = wt0 if i % 2 == 0 else wt1
        if i + 1 < 32:
            em.dma("sync", (wt1 if i % 2 == 0 else wt0)[:, :], wtile_src(i + 1),
                   wsem[(i + 1) % 2])
            ld_marks[i + 1] = em.val(wsem[(i + 1) % 2])
        em.wait("scalar", wsem[i % 2], ld_marks[i])
        thr_pos = scalars[:, 0:1] if i < NG else scalars[:, 1:2]
        thr_neg = negthr[:, 0:1] if i < NG else negthr[:, 1:2]
        em.op("scalar", lambda e, cur=cur, b=thr_neg: e.activation(
            qa, cur[:, :], ACT.Sign, bias=b))
        em.op("scalar", lambda e, cur=cur, b=thr_pos: e.activation(
            qb, cur[:, :], ACT.Sign, bias=b), inc=sSC)
        em.wait("vector", sSC)
        em.op("vector", lambda e: e.tensor_add(qa, qa, qb), inc=sDV)
        em.wait("gpsimd", sDV)
        em.dma("gpsimd", qtile_dst(i), qa, sXD)
        if i + 1 < 32:
            em.wait("sync", sSC)
            em.wait("scalar", sXD)
    em.wait("gpsimd", sXD)
    em.flush()
    pctx.close()

    if upto < 1:
        em.ctx.close()
        return nc, dbg

    pctx = contextlib.ExitStack()
    ps4 = [pctx.enter_context(nc.psum_tensor(f"p1_ps{i}", [128, 512], F32))
           for i in range(4)]
    bankmark = {}

    def bank_wait(b, _unused=None):
        if b in bankmark:
            em.wait("tensor", bankmark[b][0], bankmark[b][1])

    MIX, EK, YV, SRW = G[0], G[1], G[2], G[3]
    numb = AR[:, MIX:MIX + TL]
    wbuf = [smv(S0, 2048), smv(S1, 2048)]
    wslot_sem = [sWA, sWB]
    wmarks = {}

    def prefetch_w(src, g, slot):
        em.wait("gpsimd", sTE)
        em.dma("gpsimd", bass.AP(SM, 512 + slot * 2048, [[12288, 128], [128, NG], [1, 128]]),
               bass.AP(src, g * D * 128, [[128, 128], [128 * 128, NG], [1, 128]]),
               wslot_sem[slot])
        wmarks[g] = (slot, em.val(wslot_sem[slot]))

    def emit_mix(mixvec):
        for g in range(NG):
            em.op("vector", lambda e, g=g: e.tensor_sub(
                fm(MIX, g), xg(g, 1, TL + 1), xg(g, 0, TL)))
            em.op("vector", lambda e, g=g, mv=mixvec: e.scalar_tensor_tensor(
                fm(MIX, g), fm(MIX, g), mv[:, g:g + 1], xg(g, 0, TL),
                OP.mult, OP.add), inc=sDV)

    def emit_kvr(kind, src_in):
        for g in range(NG):
            if g + 1 < NG:
                prefetch_w(src_in, g + 1, (g + 1) % 2)
            sl, mv = wmarks[g]
            em.wait("tensor", wslot_sem[sl], mv)
            wb = wbuf[g % 2]
            for th in range(2):
                b = (g % 2) * 2 + th
                ps = ps4[b]
                bank_wait(b, None)
                for c in range(NG):
                    em.op("tensor", lambda e, ps=ps, wb=wb, c=c, th=th: e.matmul(
                        ps[:, :], wb[:, ts(c)], fm(MIX, c)[:, ts(th, 512)],
                        start=(c == 0), stop=(c == NG - 1)),
                        inc=(sTE if c == NG - 1 else None))
                tev = em.val(sTE)
                if kind == "k":
                    em.wait("scalar", sTE, tev)
                    em.op("scalar", lambda e, ps=ps, g=g, th=th: e.activation(
                        fm(EK, g)[:, ts(th, 512)], ps[:, :], ACT.Exp), inc=sSC)
                    bankmark[b] = (sSC, em.val(sSC))
                elif kind == "v":
                    em.wait("vector", sTE, tev)
                    em.op("vector", lambda e, ps=ps, g=g, th=th: e.tensor_mul(
                        fm(YV, g)[:, ts(th, 512)], fm(EK, g)[:, ts(th, 512)], ps[:, :]),
                        inc=sDV)
                    bankmark[b] = (sDV, em.val(sDV))
                else:
                    em.wait("scalar", sTE, tev)
                    em.op("scalar", lambda e, ps=ps, g=g, th=th: e.activation(
                        fm(SRW, g)[:, ts(th, 512)], ps[:, :], ACT.Sigmoid), inc=sSC)
                    bankmark[b] = (sSC, em.val(sSC))

    prefetch_w(wk_in, 0, 0)
    emit_mix(vec_mixk)
    em.wait("tensor", sDV)
    emit_kvr("k", wk_in)
    prefetch_w(wv_in, 0, 0)
    em.wait("vector", sTE)
    emit_mix(vec_mixv)
    em.wait("tensor", sDV)
    em.wait("vector", sSC)
    emit_kvr("v", wv_in)
    prefetch_w(wr_in, 0, 0)
    em.wait("vector", sTE)
    emit_mix(vec_mixr)
    em.wait("tensor", sDV)
    emit_kvr("r", wr_in)
    em.flush()

    def lam_bcast(g):
        return bass.AP(ARF, 3108 + g, [[ARFW, 128], [0, TL]])

    for g in range(NG):
        em.op("vector", lambda e, g=g: e.tensor_copy(lamfull, lam_bcast(g)))
        em.tiny("vector", lambda e, g=g: e.tensor_tensor_scan(
            sa_full[:, 1:TL + 1], lamfull, fm(YV, g), 0.0, OP.mult, OP.add))
        em.tiny("vector", lambda e, g=g: e.tensor_copy(tot_sb[:, 2 * g:2 * g + 1],
                                                       sa_full[:, TL:TL + 1]))
        em.tiny("vector", lambda e, g=g: e.tensor_tensor_scan(
            sa_full[:, 1:TL + 1], lamfull, fm(EK, g), 0.0, OP.mult, OP.add))
        em.tiny("vector", lambda e, g=g: e.tensor_copy(tot_sb[:, 2 * g + 1:2 * g + 2],
                                                       sa_full[:, TL:TL + 1]), inc=sDV)
    em.wait("gpsimd", sDV)
    em.op("gpsimd", lambda e: e.dma_start(
        out=bass.AP(tot_in_d, 0, [[2, 128], [256, NG], [1, 2]]),
        in_=bass.AP(ARF, 3220, [[ARFW, 128], [2, NG], [1, 2]])), inc=sXD, incn=16)
    em.wait("gpsimd", sXD)
    em.op("gpsimd", lambda e: e.collective_compute(
        "AllGather", OP.bypass, replica_groups=GRPS,
        ins=[tot_in_d.ap().opt()], outs=[tot_ag_d.ap().opt()]), inc=sCC)
    em.wait("sync", sCC)
    em.op("gpsimd", lambda e: e.collective_compute(
        "AllGather", OP.bypass, replica_groups=ALL8,
        ins=[qfc_in_d.ap().opt()], outs=[qfc_d.ap().opt()]), inc=sCC)
    em.op("gpsimd", lambda e: e.collective_compute(
        "AllGather", OP.bypass, replica_groups=ALL8,
        ins=[qproj_in_d.ap().opt()], outs=[qproj_d.ap().opt()]), inc=sCC)
    for g in range(NG):
        em.dma("sync", tots_g[:, :],
               bass.AP(tot_ag_d, g * 256, [[2, 128], [D * 2, 4], [1, 2]]), sLD)
        em.wait("scalar", sLD)
        for j in range(4):
            em.op("scalar", lambda e, g=g, j=j: e.activation(
                coef_g[:, j:j + 1], vec_w[:, g:g + 1], ACT.Exp,
                scale=cmult[:, j:j + 1]), inc=(sSC if j == 3 else None))
        em.wait("vector", sSC)
        em.tiny("vector", lambda e, g=g: e.tensor_mul(
            init_sb[:, 2 * g:2 * g + 2], tots_g[:, 0:2],
            bass.AP(ARF, 3292, [[ARFW, 128], [0, 2]])))
        for j in range(1, 4):
            em.tiny("vector", lambda e, g=g, j=j: e.scalar_tensor_tensor(
                init_sb[:, 2 * g:2 * g + 2], tots_g[:, 2 * j:2 * j + 2],
                coef_g[:, j:j + 1], init_sb[:, 2 * g:2 * g + 2], OP.mult, OP.add),
                inc=(sDV if j == 3 else None))
        em.wait("sync", sDV)
    em.flush()

    for g in range(NG):
        em.op("vector", lambda e, g=g: e.tensor_copy(lamfull, lam_bcast(g)))
        em.tiny("vector", lambda e, g=g: e.tensor_copy(sa_full[:, 0:1],
                                                       init_sb[:, 2 * g:2 * g + 1]))
        em.tiny("vector", lambda e, g=g: e.tensor_copy(sb_full[:, 0:1],
                                                       init_sb[:, 2 * g + 1:2 * g + 2]))
        em.op("vector", lambda e, g=g: e.tensor_tensor_scan(
            sa_full[:, 1:TL + 1], lamfull, fm(YV, g),
            init_sb[:, 2 * g:2 * g + 1], OP.mult, OP.add))
        em.tiny("vector", lambda e, g=g: e.tensor_tensor_scan(
            sb_full[:, 1:TL + 1], lamfull, fm(EK, g),
            init_sb[:, 2 * g + 1:2 * g + 2], OP.mult, OP.add))
        em.op("vector", lambda e, g=g: e.scalar_tensor_tensor(
            numb, fm(YV, g), vec_eu[:, g:g + 1], sa_full[:, 0:TL], OP.mult, OP.add))
        em.op("vector", lambda e, g=g: e.scalar_tensor_tensor(
            sa_full[:, 0:TL], fm(EK, g), vec_eu[:, g:g + 1], sb_full[:, 0:TL],
            OP.mult, OP.add))
        em.op("vector", lambda e: e.reciprocal(sa_full[:, 0:TL], sa_full[:, 0:TL]))
        em.op("vector", lambda e, g=g: e.tensor_mul(fm(YV, g), numb, sa_full[:, 0:TL]))
        em.op("vector", lambda e, g=g: e.tensor_mul(fm(SRW, g), fm(SRW, g), fm(YV, g)),
              inc=sDV)
    em.flush()

    if debug:
        dbg["dbg_init"] = nc.dram_tensor("dbg_init", [128, 2 * NG], F32,
                                         kind="ExternalOutput")
        dbg["dbg_tots"] = nc.dram_tensor("dbg_tots", [4, D, 2], F32,
                                         kind="ExternalOutput")
        for g in range(NG):
            em.dma("gpsimd", dbg["dbg_ek"][ts(g), :], fm(EK, g), sXD)
            em.dma("gpsimd", dbg["dbg_srw"][ts(g), :], fm(SRW, g), sXD)
        em.dma("gpsimd", dbg["dbg_init"][:, :], init_sb[:, :], sXD)
        em.dma("gpsimd", dbg["dbg_tots"][:, :, :], tot_ag_d[:, :, :], sXD)
        em.flush()

    prefetch_w(wo_in, 0, 0)
    for c in range(NG):
        if c + 1 < NG:
            prefetch_w(wo_in, c + 1, (c + 1) % 2)
        sl, mv = wmarks[c]
        em.wait("tensor", wslot_sem[sl], mv)
        wb = wbuf[c % 2]
        for th in range(2):
            b = (c % 2) * 2 + th
            ps = ps4[b]
            bank_wait(b, None)
            for dd in range(NG):
                em.op("tensor", lambda e, ps=ps, wb=wb, dd=dd, th=th: e.matmul(
                    ps[:, :], wb[:, ts(dd)], fm(SRW, dd)[:, ts(th, 512)],
                    start=(dd == 0), stop=(dd == NG - 1)),
                    inc=(sTE if dd == NG - 1 else None))
            em.wait("vector", sTE)
            em.op("vector", lambda e, ps=ps, c=c, th=th: e.tensor_add(
                xg(c, 1 + th * 512, 1 + th * 512 + 512), ps[:, :],
                xg(c, 1 + th * 512, 1 + th * 512 + 512)), inc=sDV)
            bankmark[b] = (sDV, em.val(sDV))
    em.flush()
    if debug:
        for g in range(NG):
            em.dma("gpsimd", dbg["dbg_r1"][ts(g), :], xg(g, 1, TL + 1), sXD)
        em.flush()
    pctx.close()

    pctx = contextlib.ExitStack()
    psT = [pctx.enter_context(nc.psum_tensor(f"tr_psT{i}", [128, 128], BF16))
           for i in range(4)]
    R1TOK, X1TOK, XNT = G[0], G[1], G[2]

    def emit_transpose(src_tile_fn, dst_tile_fn, n_i, n_j):
        cnt = 0
        for i in range(n_i):
            for j in range(n_j):
                pst = psT[cnt % 4]
                em.op("tensor", lambda e, i=i, j=j, pst=pst: e.transpose(
                    pst[:, :], src_tile_fn(i, j), identb), inc=sTE)
                eng = "vector" if cnt % 2 == 0 else "scalar"
                em.wait(eng, sTE)
                if eng == "vector":
                    em.op(eng, lambda e, i=i, j=j, pst=pst: e.tensor_copy(
                        dst_tile_fn(i, j), pst[:, :]), inc=sDV)
                else:
                    em.op(eng, lambda e, i=i, j=j, pst=pst: e.copy(
                        dst_tile_fn(i, j), pst[:, :]), inc=sSC)
                cnt += 1
                if cnt % 4 == 0:
                    em.wait("tensor", sDV)
                    em.wait("tensor", sSC)
        em.wait("tensor", sDV)
        em.wait("tensor", sSC)

    emit_transpose(lambda i, j: xg(i, 1 + j * 128, 1 + (j + 1) * 128),
                   lambda i, j: tmt(R1TOK, j)[:, ts(i)], NG, 8)
    em.flush()
    if debug:
        for tt in range(8):
            em.dma("gpsimd", dbg["dbg_r1tok"][ts(tt), :], tmt(R1TOK, tt), sXD)
        em.flush()

    sqscr = smv(S1, 2048)

    def emit_ln(src_ap, dst_ap, eps, ln_dim, dst2_ap=None, eps2=None, sq_w=2048,
                sq_ap=None):
        sqd = sqscr if sq_ap is None else sq_ap
        em.wait("scalar", sDV)
        em.tiny("vector", lambda e: e.tensor_reduce(stat[:, 0:1], src_ap, AX, OP.add),
                inc=sDV)
        nchunk = ln_dim // sq_w
        for ci in range(nchunk):
            em.op("scalar", lambda e, ci=ci: e.activation(
                sqd[:, 0:sq_w], src_ap[:, ci * sq_w:(ci + 1) * sq_w],
                ACT.Square, accum_out=stat[:, 8 + ci:9 + ci]), inc=sSC)
        em.wait("vector", sSC)
        for ci in range(1, nchunk):
            em.tiny("vector", lambda e, ci=ci: e.tensor_add(
                stat[:, 8:9], stat[:, 8:9], stat[:, 8 + ci:9 + ci]))
        em.tiny("vector", lambda e: e.tensor_scalar_mul(stat[:, 2:3], stat[:, 0:1],
                                                        1.0 / ln_dim))
        em.tiny("vector", lambda e: e.tensor_scalar_mul(stat[:, 1:2], stat[:, 8:9],
                                                        1.0 / ln_dim))
        em.tiny("vector", lambda e: e.tensor_mul(stat[:, 3:4], stat[:, 2:3],
                                                 stat[:, 2:3]))
        em.tiny("vector", lambda e: e.tensor_sub(stat[:, 3:4], stat[:, 1:2],
                                                 stat[:, 3:4]), inc=sDV)
        em.tiny("vector", lambda e, eps=eps: e.tensor_scalar_add(stat[:, 9:10],
                                                                 stat[:, 3:4], eps),
                inc=sDV)
        em.wait("scalar", sDV)
        em.op("scalar", lambda e: e.activation(stat[:, 4:5], stat[:, 9:10], ACT.Ln),
              inc=sSC)
        em.wait("vector", sSC)
        em.tiny("vector", lambda e: e.tensor_scalar_mul(stat[:, 4:5], stat[:, 4:5],
                                                        -0.5), inc=sDV)
        em.wait("scalar", sDV)
        em.op("scalar", lambda e: e.activation(stat[:, 4:5], stat[:, 4:5], ACT.Exp),
              inc=sSC)
        em.wait("vector", sSC)
        em.op("vector", lambda e: e.tensor_single_scalar(dst_ap, src_ap, stat[:, 2:3],
                                                         OP.subtract))
        em.op("vector", lambda e: e.tensor_single_scalar(dst_ap, dst_ap, stat[:, 4:5],
                                                         OP.mult), inc=sDV)
        if dst2_ap is not None:
            em.tiny("vector", lambda e, eps=eps: e.tensor_scalar_add(stat[:, 6:7],
                                                                     stat[:, 3:4], eps))
            em.tiny("vector", lambda e: e.reciprocal(stat[:, 6:7], stat[:, 6:7]))
            em.tiny("vector", lambda e: e.tensor_mul(stat[:, 6:7], stat[:, 3:4],
                                                     stat[:, 6:7]))
            em.tiny("vector", lambda e, eps2=eps2: e.tensor_scalar_add(
                stat[:, 6:7], stat[:, 6:7], eps2), inc=sDV)
            em.wait("scalar", sDV)
            em.op("scalar", lambda e: e.activation(stat[:, 6:7], stat[:, 6:7], ACT.Ln),
                  inc=sSC)
            em.wait("vector", sSC)
            em.tiny("vector", lambda e: e.tensor_scalar_mul(stat[:, 6:7], stat[:, 6:7],
                                                            -0.5), inc=sDV)
            em.wait("scalar", sDV)
            em.op("scalar", lambda e: e.activation(stat[:, 6:7], stat[:, 6:7], ACT.Exp),
                  inc=sSC)
            em.wait("vector", sSC)
            em.tiny("vector", lambda e: e.tensor_mul(stat[:, 6:7], stat[:, 4:5],
                                                     stat[:, 6:7]))
            em.op("vector", lambda e: e.tensor_single_scalar(dst2_ap, src_ap,
                                                             stat[:, 2:3], OP.subtract))
            em.op("vector", lambda e: e.tensor_single_scalar(dst2_ap, dst2_ap,
                                                             stat[:, 6:7], OP.mult),
                  inc=sDV)

    for tt in range(8):
        emit_ln(tmt(R1TOK, tt), tmt(X1TOK, tt), 1e-5, D,
                dst2_ap=tmt(XNT, tt), eps2=1e-6)
        if debug and tt == 3:
            if "dbg_stat" not in dbg:
                dbg["dbg_stat"] = nc.dram_tensor("dbg_stat", [128, 16], F32,
                                                 kind="ExternalOutput")
            em.wait("gpsimd", sDV)
            em.wait("gpsimd", sSC)
            em.dma("gpsimd", dbg["dbg_stat"][:, :], stat[:, :], sXD)
            em.wait("vector", sXD)
            em.wait("scalar", sXD)
    em.flush()
    if debug:
        for tt in range(8):
            em.dma("gpsimd", dbg["dbg_x1"][ts(tt), :], tmt(X1TOK, tt), sXD)
        em.flush()
    pctx.close()

    if upto < 2:
        em.wait("gpsimd", sXD)
        em.wait("gpsimd", sCC)
        em.flush()
        em.ctx.close()
        return nc, dbg

    pctx = contextlib.ExitStack()
    psT2 = [pctx.enter_context(nc.psum_tensor(f"p2_psT{i}", [128, 128], BF16))
            for i in range(4)]
    XHF = G[3]

    for tt in range(8):
        srcx = tmt(XNT, tt)
        em.tiny("vector", lambda e, s=srcx: e.tensor_reduce(
            stat[:, 0:1], s, AX, OP.max, apply_absolute_value=True))
        em.tiny("vector", lambda e: e.tensor_scalar_mul(
            stat[:, 1:2], stat[:, 0:1], (128.0 - 1e-6) / 128.0))
        em.tiny("vector", lambda e: e.tensor_scalar_mul(stat[:, 2:3], stat[:, 1:2],
                                                        -1.0))
        em.op("vector", lambda e, s=srcx: e.tensor_single_scalar(
            s, s, stat[:, 1:2], OP.min))
        em.op("vector", lambda e, s=srcx: e.tensor_single_scalar(
            s, s, stat[:, 2:3], OP.max), inc=sDV)
    em.flush()
    if debug:
        for tt in range(8):
            em.dma("gpsimd", dbg["dbg_xhat"][ts(tt), :], tmt(XNT, tt), sXD)
        em.flush()

    cnt2 = [0]
    for jt in range(8):
        for ic in range(NG):
            pst = psT2[cnt2[0] % 4]
            em.op("tensor", lambda e, jt=jt, ic=ic, pst=pst: e.transpose(
                pst[:, :], tmt(XNT, jt)[:, ts(ic)], identb), inc=sTE)
            eng = "vector" if cnt2[0] % 2 == 0 else "scalar"
            em.wait(eng, sTE)
            if eng == "vector":
                em.op(eng, lambda e, jt=jt, ic=ic, pst=pst: e.tensor_copy(
                    fm(XHF, ic)[:, ts(jt)], pst[:, :]), inc=sDV)
            else:
                em.op(eng, lambda e, jt=jt, ic=ic, pst=pst: e.copy(
                    fm(XHF, ic)[:, ts(jt)], pst[:, :]), inc=sSC)
            cnt2[0] += 1
            if cnt2[0] % 4 == 0:
                em.wait("tensor", sDV)
                em.wait("tensor", sSC)
    em.wait("tensor", sDV)
    em.wait("tensor", sSC)
    em.flush()
    pctx.close()

    pctx = contextlib.ExitStack()
    psF = [pctx.enter_context(nc.psum_tensor(f"fc_ps{i}", [128, 512], F32))
           for i in range(4)]
    fcmark = {}
    em.wait("gpsimd", sCC)
    wtile = [smv(S0, 128), smv(S0 + 128, 128)]
    hrow = [smv(S1, 1024), smv(S1 + 1024, 1024)]
    wm = {}

    def load_qfc_tile(hc, c, slot):
        sh, off = divmod(hc * 128, 1024)
        em.dma("gpsimd", wtile[slot],
               bass.AP(qfc_d, sh * D * 1024 + c * 128 * 1024 + off,
                       [[1024, 128], [1, 128]]), wslot_sem[slot])
        wm[(hc, c)] = (slot, em.val(wslot_sem[slot]))

    nseq = [0]
    load_qfc_tile(0, 0, 0)
    for hc in range(64):
        for c in range(NG):
            idx = nseq[0]
            slot = idx % 2
            if idx + 1 < 64 * NG:
                hn, cn = divmod(idx + 1, NG)
                em.wait("gpsimd", sTE)
                load_qfc_tile(hn, cn, 1 - slot)
            sl, mv = wm[(hc, c)]
            em.wait("tensor", wslot_sem[sl], mv)
            for th in range(2):
                b = (hc % 2) * 2 + th
                if c == 0 and b in fcmark:
                    em.wait("tensor", fcmark[b][0], fcmark[b][1])
                em.op("tensor", lambda e, ps=psF[(hc % 2) * 2 + th], c=c, th=th,
                      sl=sl: e.matmul(
                    ps[:, :], wtile[sl], fm(XHF, c)[:, th * 512:(th + 1) * 512],
                    start=(c == 0), stop=(c == NG - 1)),
                    inc=(sTE if th == 1 else None))
            nseq[0] += 1
        tev = em.val(sTE)
        hslot = hc % 2
        ssem = sS0 if hslot == 0 else sS1
        em.wait("scalar", sTE, tev)
        em.wait("scalar", ssem)
        for th in range(2):
            b = (hc % 2) * 2 + th
            em.op("scalar", lambda e, ps=psF[(hc % 2) * 2 + th], th=th,
                  hs=hrow[hslot]: e.activation(
                hs[:, th * 512:(th + 1) * 512], ps[:, :], ACT.Gelu,
                scale=scalars[:, 2:3]), inc=sSC)
            fcmark[b] = (sSC, em.val(sSC))
        em.wait("gpsimd", sSC)
        em.dma("gpsimd", hf_d[ts(hc), :], hrow[hslot], ssem)

    em.flush()
    pctx.close()

    if debug:
        dbg["dbg_hfr"] = nc.dram_tensor("dbg_hfr", [HID, TL], BF16,
                                        kind="ExternalOutput")
        dbg["dbg_qfc"] = nc.dram_tensor("dbg_qfc", [D, 1024], BF16,
                                        kind="ExternalOutput")
        dbg["dbg_scal"] = nc.dram_tensor("dbg_scal", [128, 8], F32,
                                         kind="ExternalOutput")
        em.wait("gpsimd", sS0)
        em.wait("gpsimd", sS1)
        em.dma("gpsimd", dbg["dbg_hfr"][:, :], hf_d[:, :], sXD)
        em.dma("gpsimd", dbg["dbg_qfc"][:, :], qfc_in_d[:, :], sXD)
        em.dma("gpsimd", dbg["dbg_scal"][:, :], scalars[:, :], sXD)
        em.flush()

    em.wait("sync", sS0)
    em.wait("sync", sS1)
    htok = [AR[:, G[0]:G[0] + HID], AR[:, G[0] + HID:G[0] + 2 * HID]]
    hm = []
    for tt in range(8):
        em.wait("sync", sS0 if tt % 2 == 0 else sS1)
        em.dma("sync", htok[tt % 2], bass.AP(hf_d, tt * 128, [[TL, HID], [1, 128]]),
               wslot_sem[tt % 2], transpose=True)
        hm.append(em.val(wslot_sem[tt % 2]))
        em.wait("vector", wslot_sem[tt % 2], hm[tt])
        em.wait("scalar", wslot_sem[tt % 2], hm[tt])
        emit_ln(htok[tt % 2], htok[tt % 2], 1e-6, HID)
        em.wait("vector", sSC)
        em.tiny("vector", lambda e, s=htok[tt % 2]: e.tensor_reduce(
            stat[:, 0:1], s, AX, OP.max, apply_absolute_value=True))
        em.tiny("vector", lambda e: e.tensor_scalar_mul(
            stat[:, 1:2], stat[:, 0:1], (128.0 - 1e-6) / 128.0))
        em.tiny("vector", lambda e: e.tensor_scalar_mul(stat[:, 2:3], stat[:, 1:2],
                                                        -1.0))
        em.op("vector", lambda e, s=htok[tt % 2]: e.tensor_single_scalar(
            s, s, stat[:, 1:2], OP.min))
        em.op("vector", lambda e, s=htok[tt % 2]: e.tensor_single_scalar(
            s, s, stat[:, 2:3], OP.max), inc=sDV)
        em.wait("gpsimd", sDV)
        em.dma("gpsimd", hhat_d[ts(tt), :], htok[tt % 2], sS0 if tt % 2 == 0 else sS1)
        if debug:
            em.dma("gpsimd", dbg["dbg_h"][ts(tt), :], htok[tt % 2], sXD)
        if tt + 1 < 8:
            em.wait("sync", sDV)
    em.flush()

    pctx = contextlib.ExitStack()
    psP = [pctx.enter_context(nc.psum_tensor(f"pj_ps{i}", [128, 512], F32))
           for i in range(4)]
    pjmark = {}
    em.wait("sync", sS0)
    em.wait("sync", sS1)

    def hfv(hc):
        base = [G[0], G[2], G[3], OX][hc // 16]
        return AR[:, base + (hc % 16) * 1024: base + (hc % 16) * 1024 + 1024]

    hfm = []
    for hc in range(64):
        em.dma("sync", hfv(hc), bass.AP(hhat_d, hc * 128, [[HID, TL], [1, 128]]),
               sPF, transpose=True)
        hfm.append(em.val(sPF))
    frow = [smv(S1, 1024), smv(S1 + 1024, 1024)]
    wm2 = {}

    def load_qproj_tile(dc, hc, slot):
        sh, off = divmod(hc * 128, 1024)
        em.dma("gpsimd", wtile[slot],
               bass.AP(qproj_d, sh * 1024 * D + off * D + dc * 128,
                       [[D, 128], [1, 128]]), wslot_sem[slot])
        wm2[(dc, hc)] = (slot, em.val(wslot_sem[slot]))

    nseq = [0]
    load_qproj_tile(0, 0, 0)
    for dc in range(NG):
        for hc in range(64):
            idx = nseq[0]
            slot = idx % 2
            if idx + 1 < NG * 64:
                dn, hn = divmod(idx + 1, 64)
                em.wait("gpsimd", sTE)
                load_qproj_tile(dn, hn, 1 - slot)
            sl, mv = wm2[(dc, hc)]
            em.wait("tensor", wslot_sem[sl], mv)
            if dc == 0 and hc == 0:
                em.wait("tensor", sPF, hfm[63])
            for th in range(2):
                b = (dc % 2) * 2 + th
                if hc == 0 and b in pjmark:
                    em.wait("tensor", pjmark[b][0], pjmark[b][1])
                em.op("tensor", lambda e, ps=psP[(dc % 2) * 2 + th], th=th, sl=sl,
                      hc=hc: e.matmul(
                    ps[:, :], wtile[sl], hfv(hc)[:, th * 512:(th + 1) * 512],
                    start=(hc == 0), stop=(hc == 63)),
                    inc=(sTE if th == 1 else None))
            nseq[0] += 1
        tev = em.val(sTE)
        fslot = dc % 2
        fsem = sS0 if fslot == 0 else sS1
        em.wait("vector", sTE, tev)
        em.wait("vector", fsem)
        for th in range(2):
            b = (dc % 2) * 2 + th
            em.op("vector", lambda e, ps=psP[(dc % 2) * 2 + th], th=th,
                  fs=frow[fslot]: e.tensor_copy(
                fs[:, th * 512:(th + 1) * 512], ps[:, :]), inc=sDV)
            pjmark[b] = (sDV, em.val(sDV))
        em.wait("gpsimd", sDV)
        em.dma("gpsimd", ffn_d[ts(dc), :], frow[fslot], fsem)
    em.flush()
    pctx.close()

    if debug:
        dbg["dbg_ffn"] = nc.dram_tensor("dbg_ffn", [D, TL], BF16,
                                        kind="ExternalOutput")
        em.wait("gpsimd", sS0)
        em.wait("gpsimd", sS1)
        em.dma("gpsimd", dbg["dbg_ffn"][:, :], ffn_d[:, :], sXD)
        em.flush()

    X2T = G[0]
    em.wait("sync", sS0)
    em.wait("sync", sS1)
    f2m = []
    ftok = [AR[:, G[2]:G[2] + 2048], AR[:, G[2] + 2048:G[2] + 4096]]
    for tt in range(8):
        em.dma("sync", ftok[tt % 2], bass.AP(ffn_d, tt * 128, [[TL, D], [1, 128]]),
               wslot_sem[tt % 2], transpose=True)
        f2m.append(em.val(wslot_sem[tt % 2]))
        em.wait("vector", wslot_sem[tt % 2], f2m[tt])
        em.op("vector", lambda e, tt=tt: e.scalar_tensor_tensor(
            tmt(X2T, tt), ftok[tt % 2], scalars[:, 3:4], tmt(X1TOK, tt),
            OP.mult, OP.add), inc=sDV)
        emit_ln(tmt(X2T, tt), tmt(X2T, tt), 1e-5, D)
        if tt + 1 < 8:
            em.wait("sync", sDV)
    em.flush()
    if debug:
        for tt in range(8):
            em.dma("gpsimd", dbg["dbg_x2"][ts(tt), :], tmt(X2T, tt), sXD)
        em.flush()

    if upto < 3:
        em.wait("gpsimd", sXD)
        em.wait("gpsimd", sCC)
        em.flush()
        em.ctx.close()
        return nc, dbg

    pctx = contextlib.ExitStack()
    psT3 = [pctx.enter_context(nc.psum_tensor(f"p3_psT{i}", [128, 128], BF16))
            for i in range(4)]
    X2F = G[1]
    cnt3 = [0]
    for jt in range(8):
        for ic in range(NG):
            pst = psT3[cnt3[0] % 4]
            em.op("tensor", lambda e, jt=jt, ic=ic, pst=pst: e.transpose(
                pst[:, :], tmt(X2T, jt)[:, ts(ic)], identb), inc=sTE)
            eng = "vector" if cnt3[0] % 2 == 0 else "scalar"
            em.wait(eng, sTE)
            if eng == "vector":
                em.op(eng, lambda e, jt=jt, ic=ic, pst=pst: e.tensor_copy(
                    fm(X2F, ic)[:, ts(jt)], pst[:, :]), inc=sDV)
            else:
                em.op(eng, lambda e, jt=jt, ic=ic, pst=pst: e.copy(
                    fm(X2F, ic)[:, ts(jt)], pst[:, :]), inc=sSC)
            cnt3[0] += 1
            if cnt3[0] % 4 == 0:
                em.wait("tensor", sDV)
                em.wait("tensor", sSC)
    em.wait("tensor", sDV)
    em.wait("tensor", sSC)
    em.wait("gpsimd", sDV)
    em.wait("gpsimd", sSC)
    for g in range(NG):
        em.dma("gpsimd", x2_in_d[ts(g), :], fm(X2F, g), sXD)
    em.wait("gpsimd", sXD)
    em.op("gpsimd", lambda e: e.collective_compute(
        "AllGather", OP.bypass, replica_groups=GRPS,
        ins=[x2_in_d.ap().opt()], outs=[x2_ag_d.ap().opt()]), inc=sCC)
    em.flush()
    pctx.close()

    QF, KF, VA, SLAB = G[2], G[2] + 4096, G[2] + 8192, G[2] + 12320
    NKB = SEQ // 128
    whead = [smv(S0, 2048), smv(S1, 2048)]

    def p3_loadw(src_in, h, slot):
        em.wait("gpsimd", sTE)
        em.dma("gpsimd", bass.AP(SM, 512 + slot * 2048,
                                 [[12288, 128], [128, NG], [1, 128]]),
               bass.AP(src_in, h * 128, [[512, 128], [512 * 128, NG], [1, 128]]),
               wslot_sem[slot])
        return (slot, em.val(wslot_sem[slot]))

    pctx = contextlib.ExitStack()
    psQ = [pctx.enter_context(nc.psum_tensor(f"qk_ps{i}", [128, 512], F32))
           for i in range(2)]
    psVt = [psQ[0][:, 0:128], psQ[1][:, 0:128]]
    psO = [pctx.enter_context(nc.psum_tensor(f"o_ps{i}", [128, 129], F32))
           for i in range(4)]
    psN = psQ[0][:, 0:512]
    psT3b = [pctx.enter_context(nc.psum_tensor("p3_ot", [128, 128], BF16))]
    xbm = {}

    def load_x2blk(sp, slot):
        j, half = divmod(sp, 2)
        em.wait("sync", sTE)
        em.dma("sync", bass.AP(AR, G[1] + slot * 8192,
                               [[ARENA_COLS, 128], [512, NG], [1, 512]]),
               bass.AP(x2_ag_d, j * D * TL + half * 512,
                       [[TL, 128], [128 * TL, NG], [1, 512]]), wslot_sem[slot])
        xbm[sp] = (slot, em.val(wslot_sem[slot]))

    def x2blk_tile(slot, c):
        return AR[:, G[1] + slot * 8192 + c * 512: G[1] + slot * 8192 + (c + 1) * 512]

    em.wait("sync", sCC)
    for h in range(NHC):
        wq_m = p3_loadw(wq_in, h, 0)
        wk_m = p3_loadw(wk1_in, h, 1)
        em.op("vector", lambda e: e.tensor_copy(
            bass.AP(AR, VA + 128, [[ARENA_COLS, 128], [129, 32]]),
            bass.AP(SM, 256, [[12288, 128], [0, 32]])), inc=sDV)
        load_x2blk(0, 0)
        for sp in range(8):
            if sp + 1 < 8:
                load_x2blk(sp + 1, (sp + 1) % 2)
            xsl, xmv = xbm[sp]
            em.wait("tensor", wslot_sem[xsl], xmv)
            em.wait("tensor", wslot_sem[wq_m[0]], wq_m[1])
            em.wait("tensor", wslot_sem[wk_m[0]], wk_m[1])
            for c in range(NG):
                em.op("tensor", lambda e, c=c, xsl=xsl: e.matmul(
                    psQ[0][:, :], whead[0][:, ts(c)], x2blk_tile(xsl, c),
                    start=(c == 0), stop=(c == NG - 1)), inc=sTE)
            tevq = em.val(sTE)
            for c in range(NG):
                em.op("tensor", lambda e, c=c, xsl=xsl: e.matmul(
                    psQ[1][:, :], whead[1][:, ts(c)], x2blk_tile(xsl, c),
                    start=(c == 0), stop=(c == NG - 1)), inc=sTE)
            tevk = em.val(sTE)
            em.wait("vector", sTE, tevq)
            em.op("vector", lambda e, sp=sp: e.tensor_copy(
                AR[:, QF + sp * 512: QF + (sp + 1) * 512], psQ[0][:, :]), inc=sDV)
            em.wait("scalar", sTE, tevk)
            em.op("scalar", lambda e, sp=sp: e.copy(
                AR[:, KF + sp * 512: KF + (sp + 1) * 512], psQ[1][:, :]), inc=sSC)
            em.wait("tensor", sDV)
            em.wait("tensor", sSC)
        wv_m = p3_loadw(wv1_in, h, 0)
        for sp in range(8):
            xsl, xmv = xbm[sp]
            load_x2blk(sp, sp % 2)
            xsl, xmv = xbm[sp]
            em.wait("tensor", wslot_sem[xsl], xmv)
            em.wait("tensor", wslot_sem[wv_m[0]], wv_m[1])
            for t4 in range(4):
                kb = sp * 4 + t4
                ps = psVt[t4 % 2]
                for c in range(NG):
                    em.op("tensor", lambda e, c=c, xsl=xsl, t4=t4, ps=ps: e.matmul(
                        ps[:, :],
                        x2blk_tile(xsl, c)[:, t4 * 128:(t4 + 1) * 128],
                        whead[0][:, ts(c)],
                        start=(c == 0), stop=(c == NG - 1)), inc=sTE)
                tev = em.val(sTE)
                eng = "vector" if t4 % 2 == 0 else "scalar"
                em.wait(eng, sTE, tev)
                if eng == "vector":
                    em.op(eng, lambda e, kb=kb, ps=ps: e.tensor_copy(
                        AR[:, VA + kb * 129: VA + kb * 129 + 128], ps[:, :]), inc=sDV)
                else:
                    em.op(eng, lambda e, kb=kb, ps=ps: e.copy(
                        AR[:, VA + kb * 129: VA + kb * 129 + 128], ps[:, :]), inc=sSC)
                em.wait("tensor", sDV)
                em.wait("tensor", sSC)
        em.flush()

        qsq = AR[:, SLAB:SLAB + 512]
        for which, base in [(0, QF), (1, KF)]:
            dstv = attC if which == 0 else attK
            for chunk in range(8):
                em.op("scalar", lambda e, base=base, chunk=chunk: e.activation(
                    qsq, AR[:, base + chunk * 512: base + (chunk + 1) * 512],
                    ACT.Square), inc=sSC)
                em.wait("tensor", sSC)
                em.op("tensor", lambda e: e.matmul(
                    psN, onessq, qsq, start=True, stop=True), inc=sTE)
                em.wait("vector", sTE)
                em.tiny("vector", lambda e, dstv=dstv, chunk=chunk: e.tensor_reduce(
                    dstv[:, chunk:chunk + 1], psN, AX, OP.max))
                em.op("vector", lambda e: e.tensor_copy(qsq[:, 0:1], qsq[:, 0:1]),
                      inc=sDV)
                em.wait("scalar", sDV)
                em.wait("tensor", sDV)
        for sfx in range(1, 8):
            em.tiny("vector", lambda e, sfx=sfx: e.tensor_max(
                attK[:, sfx:sfx + 1], attK[:, sfx:sfx + 1], attK[:, sfx - 1:sfx]),
                inc=(sDV if sfx == 7 else None))

        for s in range(8):
            em.wait("tensor", sDV)
            em.wait("tensor", sSC)
            em.tiny("vector", lambda e, s=s: e.tensor_mul(
                stat[:, 2:3], attC[:, s:s + 1], attK[:, s:s + 1]), inc=sDV)
            em.wait("scalar", sDV)
            em.op("scalar", lambda e: e.activation(stat[:, 3:4], stat[:, 2:3],
                                                   ACT.Sqrt), inc=sSC)
            em.wait("vector", sSC)
            em.tiny("vector", lambda e: e.tensor_mul(stat[:, 4:5], stat[:, 3:4],
                                                     ascale))
            em.tiny("vector", lambda e: e.tensor_scalar_mul(stat[:, 4:5],
                                                            stat[:, 4:5], -1.0),
                    inc=sDV)
            em.wait("scalar", sDV)
            nkb = 4 * s + 4
            for kb in range(nkb):
                ps = psQ[kb % 2]
                em.op("tensor", lambda e, kb=kb, s=s, ps=ps: e.matmul(
                    ps[:, :], AR[:, KF + kb * 128: KF + (kb + 1) * 128],
                    AR[:, QF + s * 512: QF + (s + 1) * 512],
                    start=True, stop=True), inc=sTE)
                tev = em.val(sTE)
                slab = AR[:, SLAB + (kb % 2) * 512: SLAB + (kb % 2) * 512 + 512]
                em.wait("scalar", sTE, tev)
                em.op("scalar", lambda e, ps=ps, slab=slab: e.activation(
                    slab, ps[:, :], ACT.Exp, bias=stat[:, 4:5], scale=ascale),
                    inc=sSC)
                jd = kb - 4 * s
                if 0 <= jd < 4:
                    em.wait("vector", sSC)
                    em.op("vector", lambda e, slab=slab, jd=jd: e.tensor_mul(
                        slab[:, jd * 128:(jd + 1) * 128],
                        slab[:, jd * 128:(jd + 1) * 128], maskst), inc=sDV)
                    em.wait("tensor", sDV)
                em.wait("tensor", sSC)
                if debug and h == NHC - 1 and s == 0 and kb == 0:
                    if "dbg_slab" not in dbg:
                        dbg["dbg_slab"] = nc.dram_tensor("dbg_slab", [128, 512], BF16,
                                                         kind="ExternalOutput")
                    em.wait("gpsimd", sDV)
                    em.wait("gpsimd", sSC)
                    em.dma("gpsimd", dbg["dbg_slab"][:, :], slab, sXD)
                    em.wait("tensor", sXD)
                for j in range(4):
                    if kb <= 4 * s + j:
                        em.op("tensor", lambda e, j=j, kb=kb, slab=slab: e.matmul(
                            psO[j][:, :], slab[:, j * 128:(j + 1) * 128],
                            AR[:, VA + kb * 129: VA + kb * 129 + 129],
                            start=(kb == 0), stop=(kb == 4 * s + j)), inc=sTE)
            tev = em.val(sTE)
            for j in range(4):
                qt = 4 * s + j
                em.wait("vector", sTE, tev)
                em.tiny("vector", lambda e, j=j: e.tensor_scalar_mul(
                    stat[:, 6:7], psO[j][:, 128:129], float(2.0 ** 44)), inc=sDV)
                em.wait("scalar", sDV)
                em.op("scalar", lambda e: e.activation(
                    stat[:, 6:7], stat[:, 6:7], ACT.Ln), inc=sSC)
                em.wait("vector", sSC)
                em.tiny("vector", lambda e: e.tensor_scalar_mul(
                    stat[:, 6:7], stat[:, 6:7], -1.0), inc=sDV)
                em.wait("scalar", sDV)
                em.op("scalar", lambda e: e.activation(stat[:, 7:8], stat[:, 6:7],
                                                       ACT.Exp), inc=sSC)
                em.wait("vector", sSC)
                em.tiny("vector", lambda e: e.tensor_scalar_mul(
                    stat[:, 7:8], stat[:, 7:8], float(2.0 ** 44)), inc=sDV)
                em.op("vector", lambda e, j=j: e.tensor_single_scalar(
                    qsq[:, 0:128], psO[j][:, 0:128], stat[:, 7:8], OP.mult),
                    inc=sDV)
                em.wait("tensor", sDV)
                em.op("tensor", lambda e: e.transpose(psT3b[0][:, :], qsq[:, 0:128], identb),
                      inc=sTE)
                em.wait("vector", sTE)
                em.op("vector", lambda e, h=h, qt=qt: e.tensor_copy(
                    AR[:, G[3] + h * 4096 + qt * 128: G[3] + h * 4096 + (qt + 1) * 128],
                    psT3b[0][:, :][:, :]), inc=sDV)
                em.wait("tensor", sDV)
                em.wait("scalar", sDV)
        em.flush()
        if debug and h == NHC - 1:
            dbg["dbg_qf"] = nc.dram_tensor("dbg_qf", [128, SEQ], BF16,
                                           kind="ExternalOutput")
            dbg["dbg_kf"] = nc.dram_tensor("dbg_kf", [128, SEQ], BF16,
                                           kind="ExternalOutput")
            dbg["dbg_va"] = nc.dram_tensor("dbg_va", [128, 4128], BF16,
                                           kind="ExternalOutput")
            dbg["dbg_of"] = nc.dram_tensor("dbg_of", [128, SEQ], BF16,
                                           kind="ExternalOutput")
            dbg["dbg_cst"] = nc.dram_tensor("dbg_cst", [128, 16], F32,
                                            kind="ExternalOutput")
            em.dma("gpsimd", dbg["dbg_qf"][:, :], AR[:, QF:QF + SEQ], sXD)
            em.dma("gpsimd", dbg["dbg_kf"][:, :], AR[:, KF:KF + SEQ], sXD)
            em.dma("gpsimd", dbg["dbg_va"][:, :], AR[:, VA:VA + 4128], sXD)
            em.dma("gpsimd", dbg["dbg_of"][:, :],
                   AR[:, G[3] + h * 4096: G[3] + (h + 1) * 4096], sXD)
            em.dma("gpsimd", dbg["dbg_cst"][:, :], stat[:, :], sXD)
            em.wait("gpsimd", sXD)
            em.flush()
    pctx.close()

    pctx = contextlib.ExitStack()
    psC4 = [pctx.enter_context(nc.psum_tensor(f"op_ps{i}", [128, 512], F32))
            for i in range(4)]
    em.wait("gpsimd", sTE)
    for hh in range(4):
        for cs in range(4):
            em.dma("gpsimd", AR[:, G[1] + (hh * 4 + cs) * 512:
                                G[1] + (hh * 4 + cs) * 512 + 512],
                   bass.AP(wo1_in, hh * 128 * D + cs * 512, [[D, 128], [1, 512]]),
                   sPF)
    WO_PF = em.val(sPF)
    ostg = [smv(S0, 2048), smv(S1, 2048)]
    em.wait("tensor", sPF, WO_PF)
    for tt32 in range(32):
        for hh in range(4):
            for cs in range(4):
                em.op("tensor", lambda e, hh=hh, cs=cs, tt32=tt32: e.matmul(
                    psC4[cs][:, :],
                    AR[:, G[3] + hh * 4096 + tt32 * 128:
                       G[3] + hh * 4096 + (tt32 + 1) * 128],
                    AR[:, G[1] + (hh * 4 + cs) * 512: G[1] + (hh * 4 + cs) * 512 + 512],
                    start=(hh == 0), stop=(hh == 3)),
                    inc=(sTE if hh == 3 else None))
        tev = em.val(sTE)
        oslot = tt32 % 2
        osem = sS0 if oslot == 0 else sS1
        em.wait("vector", sTE, tev)
        em.wait("vector", osem)
        for cs in range(4):
            em.op("vector", lambda e, cs=cs, oslot=oslot: e.tensor_copy(
                ostg[oslot][:, cs * 512:(cs + 1) * 512], psC4[cs][:, :]), inc=sDV)
        em.wait("tensor", sDV)
        em.wait("gpsimd", sDV)
        em.dma("gpsimd", rs_in_d[tt32 // 8, ts(tt32 % 8), :], ostg[oslot], osem)
    em.wait("gpsimd", sS0)
    em.wait("gpsimd", sS1)
    em.op("gpsimd", lambda e: e.collective_compute(
        "ReduceScatter", OP.add, replica_groups=GRPS,
        ins=[rs_in_d.ap().opt()], outs=[rs_out_d.ap().opt()]), inc=sCC)
    em.flush()
    pctx.close()

    em.wait("sync", sCC)
    rstg = [AR[:, G[1]:G[1] + 2048], AR[:, G[1] + 2048:G[1] + 4096]]
    for tt in range(8):
        em.dma("sync", rstg[tt % 2], rs_out_d[ts(tt), :], wslot_sem[tt % 2])
        rm = em.val(wslot_sem[tt % 2])
        em.wait("vector", wslot_sem[tt % 2], rm)
        em.op("vector", lambda e, tt=tt: e.tensor_add(
            tmt(X2T, tt), tmt(X2T, tt), rstg[tt % 2]), inc=sDV)
        if debug:
            em.wait("gpsimd", sDV)
            em.dma("gpsimd", dbg["dbg_attn"][ts(tt), :], rstg[tt % 2], sXD)
        emit_ln(tmt(X2T, tt), tmt(X2T, tt), 1e-5, D)
        if tt + 1 < 8:
            em.wait("sync", sDV)
            em.wait("sync", sXD)
    em.flush()
    X3T = X2T
    if debug:
        for tt in range(8):
            em.dma("gpsimd", dbg["dbg_x3"][ts(tt), :], tmt(X3T, tt), sXD)
        em.flush()

    if upto < 4:
        em.wait("gpsimd", sXD)
        em.wait("gpsimd", sCC)
        em.flush()
        em.ctx.close()
        return nc, dbg

    pctx = contextlib.ExitStack()
    psT4 = [pctx.enter_context(nc.psum_tensor(f"p4_psT{i}", [128, 128], BF16))
            for i in range(2)]
    for tt in range(8):
        em.dma("gpsimd", x3_spill_d[ts(tt), :], tmt(X3T, tt), sXD)
    X3F = OX
    cnt4 = [0]
    for jt in range(8):
        for ic in range(NG):
            pst = psT4[cnt4[0] % 2]
            em.op("tensor", lambda e, jt=jt, ic=ic, pst=pst: e.transpose(
                pst[:, :], tmt(X3T, jt)[:, ts(ic)], identb), inc=sTE)
            eng = "vector" if cnt4[0] % 2 == 0 else "scalar"
            em.wait(eng, sTE)
            if eng == "vector":
                em.op(eng, lambda e, jt=jt, ic=ic, pst=pst: e.tensor_copy(
                    AR[:, X3F + ic * TL + jt * 128: X3F + ic * TL + (jt + 1) * 128],
                    pst[:, :]), inc=sDV)
            else:
                em.op(eng, lambda e, jt=jt, ic=ic, pst=pst: e.copy(
                    AR[:, X3F + ic * TL + jt * 128: X3F + ic * TL + (jt + 1) * 128],
                    pst[:, :]), inc=sSC)
            cnt4[0] += 1
            if cnt4[0] % 2 == 0:
                em.wait("tensor", sDV)
                em.wait("tensor", sSC)
    em.wait("tensor", sDV)
    em.wait("tensor", sSC)
    em.flush()
    pctx.close()

    def x3f(c):
        return AR[:, X3F + c * TL: X3F + (c + 1) * TL]

    pctx = contextlib.ExitStack()
    psR = pctx.enter_context(nc.psum_tensor("p4_psR", [128, 8], F32))
    psW = pctx.enter_context(nc.psum_tensor("p4_psW", [8, 128], BF16))
    psB = pctx.enter_context(nc.psum_tensor("p4_psB", [128, 512], F32))
    routb = smv(SSP, 128)
    for c in range(NG):
        em.dma("gpsimd", routb[:, c * 8:(c + 1) * 8], router_in[ts(c), :], sPF)
    RT_PF = em.val(sPF)
    W8 = SM[0:8, S2:S2 + 1024]
    wtt = smv(S0, 8)
    em.wait("tensor", sPF, RT_PF)
    for tt in range(8):
        for c in range(NG):
            em.op("tensor", lambda e, tt=tt, c=c: e.matmul(
                psR[:, :], x3f(c)[:, ts(tt)], routb[:, c * 8:(c + 1) * 8],
                start=(c == 0), stop=(c == NG - 1)), inc=(sTE if c == NG - 1 else None))
        em.wait("vector", sTE)
        em.tiny("vector", lambda e: e.tensor_reduce(stat[:, 0:1], psR[:, :], AX,
                                                    OP.max))
        em.tiny("vector", lambda e: e.tensor_scalar_mul(stat[:, 0:1], stat[:, 0:1],
                                                        -1.0), inc=sDV)
        em.wait("scalar", sDV)
        em.op("scalar", lambda e: e.activation(sqscr[:, 0:8], psR[:, :], ACT.Exp,
                                               bias=stat[:, 0:1]), inc=sSC)
        em.wait("vector", sSC)
        em.tiny("vector", lambda e: e.tensor_single_scalar(
            sqscr[:, 8:16], sqscr[:, 0:8], 1.0, OP.is_ge))
        em.tiny("vector", lambda e: e.scalar_tensor_tensor(
            sqscr[:, 8:16], sqscr[:, 8:16], -1e30, sqscr[:, 0:8], OP.mult, OP.add))
        em.tiny("vector", lambda e: e.tensor_reduce(stat[:, 1:2], sqscr[:, 8:16],
                                                    AX, OP.max))
        em.tiny("vector", lambda e: e.tensor_scalar_add(stat[:, 2:3], stat[:, 1:2],
                                                        1.0))
        em.tiny("vector", lambda e: e.reciprocal(stat[:, 2:3], stat[:, 2:3]))
        em.tiny("vector", lambda e: e.tensor_single_scalar(
            wtt, sqscr[:, 0:8], stat[:, 1:2], OP.is_ge))
        em.tiny("vector", lambda e: e.tensor_mul(wtt, wtt, sqscr[:, 0:8]))
        em.tiny("vector", lambda e: e.tensor_single_scalar(
            wtt, wtt, stat[:, 2:3], OP.mult), inc=sDV)
        em.wait("tensor", sDV)
        em.op("tensor", lambda e: e.transpose(psW[:, :], wtt, identb), inc=sTE)
        em.wait("vector", sTE)
        em.op("vector", lambda e, tt=tt: e.tensor_copy(W8[:, ts(tt)], psW[:, :]),
              inc=sDV)
        em.wait("tensor", sDV)
    em.wait("gpsimd", sDV)
    em.dma("gpsimd", w8_d[:, :], W8[:, :], sXD)
    em.flush()
    pctx.close()

    pctx = contextlib.ExitStack()
    ps8 = [pctx.enter_context(nc.psum_tensor(f"p4_a{i}", [128, 512], F32))
           for i in range(8)]
    gumark = {}

    def abank_wait(b):
        if b in gumark:
            em.wait("tensor", gumark[b][0], gumark[b][1])
    gbufs = [smv(512, 2048), smv(2560, 2048)]
    ubufs = [smv(4608, 2048), smv(6656, 2048)]
    wbc = smv(8704, 1024)
    silus = [smv(9728, 512), smv(10240, 512)]
    dstg = smv(10752, 1024)

    def act_ap(e, fc, th):
        base = G[e // 2] + (e % 2) * 8192
        return AR[:, base + fc * 1024 + th * 512: base + fc * 1024 + (th + 1) * 512]

    gu_marks = {}

    def emit_gu_load(e_, fc_, slot):
        em.wait("gpsimd", sTE)
        em.dma("gpsimd",
               bass.AP(SM, 512 + slot * 2048, [[12288, 128], [128, NG], [1, 128]]),
               bass.AP(gate_in, (e_ * 8 + fc_) * D * 128,
                       [[128, 128], [128 * 128, NG], [1, 128]]), sWA)
        gm = em.val(sWA)
        em.dma("gpsimd",
               bass.AP(SM, 4608 + slot * 2048, [[12288, 128], [128, NG], [1, 128]]),
               bass.AP(up_in, (e_ * 8 + fc_) * D * 128,
                       [[128, 128], [128 * 128, NG], [1, 128]]), sWB)
        gu_marks[(e_, fc_)] = (gm, em.val(sWB))

    dcast_seq = [0]

    def emit_downcast_step():
        i = dcast_seq[0]
        if i >= NE * 8 * 2:
            return
        e_, r = divmod(i, 16)
        fc_, half = divmod(r, 2)
        em.wait("gpsimd", sS0)
        em.dma("gpsimd", dstg,
               bass.AP(down_in, e_ * FMOE * D + fc_ * 128 * D + half * 1024,
                       [[D, 128], [1, 1024]]), sDC)
        em.wait("gpsimd", sDC)
        em.dma("gpsimd",
               bass.AP(downb_d, e_ * FMOE * D + fc_ * 128 * D + half * 1024,
                       [[D, 128], [1, 1024]]), dstg, sS0)
        dcast_seq[0] += 1

    w8rows = [w8row, w8rowb]
    w8rm = {}
    w8ld = {}
    silmark = {}

    def w8_prefetch(e_):
        slot = e_ % 2
        if slot in w8rm:
            em.wait("sync", w8rm[slot][0], w8rm[slot][1])
        em.wait("sync", sXD)
        em.dma("sync", w8rows[slot][0:1, :], w8_d[e_:e_ + 1, :], sPF)
        w8ld[e_] = em.val(sPF)

    w8_prefetch(0)
    for e in range(NE):
        wslot = e % 2
        em.wait("tensor", sPF, w8ld[e])
        abank_wait(0)
        for th2 in range(2):
            em.op("tensor", lambda ee, th2=th2, wslot=wslot: ee.matmul(
                ps8[0][:, :], onessq[0:1, 0:128],
                w8rows[wslot][0:1, th2 * 512:(th2 + 1) * 512],
                start=True, stop=True), inc=sTE)
            em.wait("vector", sTE)
            em.op("vector", lambda ee, th2=th2: ee.tensor_copy(
                wbc[:, th2 * 512:(th2 + 1) * 512], ps8[0][:, :]), inc=sDV)
            if th2 == 0:
                em.wait("tensor", sDV)
        gumark[0] = (sDV, em.val(sDV))
        w8rm[wslot] = (sTE, em.val(sTE))
        if e + 1 < NE:
            w8_prefetch(e + 1)
        for fc in range(8):
            gslot = (e * 8 + fc) % 2
            par = fc % 2
            if e == 0 and fc == 0:
                emit_gu_load(0, 0, 0)
            if e * 8 + fc + 1 < 64:
                en, fn = divmod(e * 8 + fc + 1, 8)
                emit_gu_load(en, fn, 1 - gslot)
            gm, um = gu_marks[(e, fc)]
            em.wait("tensor", sWA, gm)
            em.wait("tensor", sWB, um)
            emit_downcast_step()
            emit_downcast_step()
            gbuf, ubuf = gbufs[gslot], ubufs[gslot]
            for th in range(2):
                abank_wait(par * 4 + th)
            for c in range(NG):
                for th in range(2):
                    em.op("tensor", lambda ee, c=c, th=th, gbuf=gbuf,
                          par=par: ee.matmul(
                        ps8[par * 4 + th][:, :], gbuf[:, ts(c)],
                        x3f(c)[:, ts(th, 512)],
                        start=(c == 0), stop=(c == NG - 1)),
                        inc=(sTE if th == 1 else None))
            for th in range(2):
                abank_wait(par * 4 + 2 + th)
            for c in range(NG):
                for th in range(2):
                    em.op("tensor", lambda ee, c=c, th=th, ubuf=ubuf,
                          par=par: ee.matmul(
                        ps8[par * 4 + 2 + th][:, :], ubuf[:, ts(c)],
                        x3f(c)[:, ts(th, 512)],
                        start=(c == 0), stop=(c == NG - 1)),
                        inc=(sTE if th == 1 else None))
            tev = em.val(sTE)
            em.wait("scalar", sTE, tev)
            for th in range(2):
                if th in silmark:
                    em.wait("scalar", sDV, silmark[th])
                em.op("scalar", lambda ee, th=th, par=par: ee.activation(
                    silus[th], ps8[par * 4 + th][:, :], ACT.Silu), inc=sSC)
                gumark[par * 4 + th] = (sSC, em.val(sSC))
                em.wait("vector", sSC)
                em.op("vector", lambda ee, th=th, par=par: ee.tensor_mul(
                    silus[th], silus[th], ps8[par * 4 + 2 + th][:, :]))
                em.op("vector", lambda ee, e=e, fc=fc, th=th: ee.tensor_mul(
                    act_ap(e, fc, th), silus[th],
                    wbc[:, th * 512:(th + 1) * 512]), inc=sDV)
                mv = em.val(sDV)
                gumark[par * 4 + 2 + th] = (sDV, mv)
                silmark[th] = mv
    em.wait("tensor", sDV)
    em.wait("tensor", sSC)
    em.flush()
    pctx.close()

    if debug:
        dbg["dbg_downb"] = nc.dram_tensor("dbg_downb", [NE, FMOE, D], BF16,
                                          kind="ExternalOutput")
        dbg["dbg_act"] = nc.dram_tensor("dbg_act", [128, 8192], BF16,
                                        kind="ExternalOutput")
        dbg["dbg_wbc"] = nc.dram_tensor("dbg_wbc", [128, 1024], BF16,
                                        kind="ExternalOutput")
        em.wait("gpsimd", sS0)
        em.dma("gpsimd", dbg["dbg_downb"][:, :, :], downb_d[:, :, :], sXD)
        em.dma("gpsimd", dbg["dbg_act"][:, :], AR[:, G[0]:G[0] + 8192], sXD)
        em.dma("gpsimd", dbg["dbg_wbc"][:, :], wbc, sXD)
        em.wait("gpsimd", sXD)
        em.flush()

    pctx = contextlib.ExitStack()
    psD = [pctx.enter_context(nc.psum_tensor(f"p4_d{i}", [128, 512], F32))
           for i in range(8)]
    dbuf = [smv(512, 4096), smv(4608, 4096)]
    x3r = smv(8704, 2048)
    while dcast_seq[0] < NE * 8 * 2:
        emit_downcast_step()
    em.wait("gpsimd", sS0)
    em.wait("sync", sS0)
    R4 = OX
    dn_marks = {}

    def load_down_e(ttg, dsg, e_, ds2, slot):
        em.wait("sync", sTE)
        em.dma("sync", bass.AP(SM, 512 + slot * 4096,
                               [[12288, 128], [512, 8], [1, 512]]),
               bass.AP(downb_d, e_ * FMOE * D + dsg * 1024 + ds2 * 512,
                       [[D, 128], [128 * D, 8], [1, 512]]), wslot_sem[slot])
        dn_marks[(ttg, dsg, e_, ds2)] = (slot, em.val(wslot_sem[slot]))

    def dn_key(i):
        ttg, r = divmod(i, 2 * NE * 2)
        dsg, r2 = divmod(r, NE * 2)
        e_, ds2 = divmod(r2, 2)
        return (ttg, dsg, e_, ds2)

    NDN = 2 * 2 * NE * 2
    load_down_e(*dn_key(0), 0)
    dseq = [0]
    for ttg in range(2):
        for dsg in range(2):
            for e in range(NE):
                for ds2 in range(2):
                    i = dseq[0]
                    if i + 1 < NDN:
                        load_down_e(*dn_key(i + 1), (i + 1) % 2)
                    sl, mv = dn_marks[dn_key(i)]
                    em.wait("tensor", wslot_sem[sl], mv)
                    for fc in range(8):
                        first = (e == 0 and fc == 0)
                        last = (e == NE - 1 and fc == 7)
                        for t4 in range(4):
                            tt = ttg * 4 + t4
                            em.op("tensor", lambda ee, e=e, fc=fc, tt=tt, t4=t4,
                                  ds2=ds2, sl=sl, first=first, last=last: ee.matmul(
                                      psD[t4 * 2 + ds2][:, :],
                                      AR[:, G[e // 2] + (e % 2) * 8192 + fc * 1024
                                         + tt * 128: G[e // 2] + (e % 2) * 8192
                                         + fc * 1024 + (tt + 1) * 128],
                                      dbuf[sl][:, fc * 512:(fc + 1) * 512],
                                      start=first, stop=last),
                                  inc=(sTE if t4 == 3 else None))
                    dseq[0] += 1
            tev = em.val(sTE)
            em.wait("vector", sTE, tev)
            for t4 in range(4):
                tt = ttg * 4 + t4
                em.wait("sync", sDV)
                em.dma("sync", x3r, x3_spill_d[ts(tt), :], sPF)
                xm = em.val(sPF)
                em.wait("vector", sPF, xm)
                for ds2 in range(2):
                    em.op("vector", lambda ee, t4=t4, ds2=ds2, tt=tt, dsg=dsg:
                          ee.tensor_add(
                              AR[:, R4 + tt * 2048 + dsg * 1024 + ds2 * 512:
                                 R4 + tt * 2048 + dsg * 1024 + (ds2 + 1) * 512],
                              psD[t4 * 2 + ds2][:, :],
                              x3r[:, dsg * 1024 + ds2 * 512:
                                  dsg * 1024 + (ds2 + 1) * 512]), inc=sDV)
            em.wait("tensor", sDV)
        for t4 in range(4):
            tt = ttg * 4 + t4
            r4t = AR[:, R4 + tt * 2048: R4 + (tt + 1) * 2048]
            em.wait("vector", sS1)
            em.wait("scalar", sS1)
            emit_ln(r4t, outstg, 1e-5, D, sq_ap=smv(10752, 1024), sq_w=1024)
            em.wait("gpsimd", sDV)
            em.dma("gpsimd", out_d[ts(tt), :], outstg, sS1)
            if debug:
                em.dma("gpsimd", dbg["dbg_moe"][ts(tt), :], r4t, sXD)
    em.flush()
    pctx.close()

    em.wait("gpsimd", sS1)
    em.wait("gpsimd", sS0)
    em.wait("gpsimd", sXD)
    em.wait("gpsimd", sCC)
    em.flush()
    em.ctx.close()
    return nc, dbg


def _stage(inputs):
    f = np.ascontiguousarray
    x = np.asarray(inputs["L0_attn_x"], dtype=np.float32)
    wkT = np.asarray(inputs["L0_attn_Wk"], np.float32).T
    wvT = np.asarray(inputs["L0_attn_Wv"], np.float32).T
    wrT = np.asarray(inputs["L0_attn_Wr"], np.float32).T
    woT = np.asarray(inputs["L0_attn_Wo"], np.float32).T
    blk = lambda w: f(w.reshape(D, NG, 128).transpose(1, 0, 2))
    wkB, wvB, wrB, woB = blk(wkT), blk(wvT), blk(wrT), blk(woT)
    col = lambda v: f(np.asarray(v, np.float32).reshape(D, 1))
    wfcT = f(np.asarray(inputs["L0_ffn_w_fc"], np.float32).T)
    wprojT = f(np.asarray(inputs["L0_ffn_w_proj"], np.float32).T)
    wqT = f(np.asarray(inputs["L1_attn_wq"], np.float32).T)
    wk1T = f(np.asarray(inputs["L1_attn_wk"], np.float32).T)
    wv1T = f(np.asarray(inputs["L1_attn_wv"], np.float32).T)
    wo1T = f(np.asarray(inputs["L1_attn_wo"], np.float32).T)
    router = f(np.asarray(inputs["L1_ffn_router_w"], np.float32))
    gate = np.asarray(inputs["L1_ffn_gate_proj"], np.float32)
    up = np.asarray(inputs["L1_ffn_up_proj"], np.float32)
    down = f(np.asarray(inputs["L1_ffn_down_proj"], np.float32))
    gateB = f(gate.reshape(NE, D, 8, 128).transpose(0, 2, 1, 3))
    upB = f(up.reshape(NE, D, 8, 128).transpose(0, 2, 1, 3))
    identb = np.eye(128, dtype=ml_dtypes.bfloat16)
    maskst = np.triu(np.ones((128, 128), np.float32)).astype(ml_dtypes.bfloat16)
    ones_col = np.ones((128, 1), np.float32)
    ones_row = np.ones((1, 128), np.float32)
    ones_sq = np.ones((128, 128), dtype=ml_dtypes.bfloat16)
    attn_scale = np.full((128, 1), 1.0 / np.sqrt(128.0), np.float32)

    in_maps = []
    for c in range(8):
        b, q = c // 4, c % 4
        t0 = q * TL
        xloc = np.zeros((D, TL + 1), np.float32)
        xs = x[b, max(t0 - 1, 0):t0 + TL, :]
        xloc[:, (1 if t0 == 0 else 0):] = xs.T
        cm = np.full((1, 4), 1e6, np.float32)
        for j in range(q):
            cm[0, j] = 1024.0 * (q - 1 - j)
        hsl = slice(c * 1024, (c + 1) * 1024)
        heads = slice((c % 4) * 512, (c % 4) * 512 + 512)
        im = {
            "x_f": f(xloc),
            "l0_wk": wkB, "l0_wv": wvB, "l0_wr": wrB, "l0_wo": woB,
            "l0_decay": col(inputs["L0_attn_time_decay"]),
            "l0_first": col(inputs["L0_attn_time_first"]),
            "l0_mixk": col(inputs["L0_attn_time_mix_k"]),
            "l0_mixv": col(inputs["L0_attn_time_mix_v"]),
            "l0_mixr": col(inputs["L0_attn_time_mix_r"]),
            "carry_mult": cm,
            "wfc_slice": f(wfcT[:, hsl]),
            "wproj_slice": f(wprojT[hsl, :]),
            "l1_wq": f(wqT[:, heads]), "l1_wk": f(wk1T[:, heads]),
            "l1_wv": f(wv1T[:, heads]), "l1_wo": f(wo1T[heads, :]),
            "router_w": router,
            "moe_gate": gateB, "moe_up": upB, "moe_down": down,
            "ident_bf": identb, "mask_st": maskst, "ones_col": ones_col,
            "ones_row": ones_row, "ones_sq": ones_sq, "attn_scale": attn_scale,
        }
        in_maps.append(im)
    return in_maps


def kernel(**inputs):
    nc, _ = build_program(upto=99, debug=False)
    in_maps = _stage(inputs)
    res = run_bass_kernel_spmd(nc, in_maps, core_ids=list(range(8)))
    shards = [res.results[c]["out"] for c in range(8)]
    out = np.stack(shards).reshape(BATCH, SEQ, D).astype(np.float32)
    return out
